# version 06
# baseline (speedup 1.0000x reference)
import numpy as np
from contextlib import ExitStack

import concourse.bass as bass
import concourse.mybir as mybir
from concourse.bass_utils import run_bass_kernel_spmd

F32 = mybir.dt.float32
BF16 = mybir.dt.bfloat16
FP8 = mybir.dt.float8e4
AF = mybir.ActivationFunctionType
ALU = mybir.AluOpType

NCORES = 8
DM = 2048
NCH = 16
NH = 16
HD = 128
BLK = 256
TPC = 1024
NBLK = 4
FI = 6144
EPS_LN = 1e-5
EPS_RMS = 1e-6
IGRP = 512
NIG = FI // IGRP
BIG = 1.0e9
HT = 512

ENGS = ("sync", "gpsimd", "tensor", "scalar", "vector")


class Sem:
    def __init__(self, handle, owner=None):
        self.h = handle
        self.n = 0
        self.owner = owner

    def inc(self, d=1):
        self.n += d


class Prog:

    def __init__(self, nc, block):
        self.nc = nc
        self.block = block
        self.q = {e: [] for e in ENGS}
        self.lastw = {e: {} for e in ENGS}

    def emit(self, eng, fn, waits=(), inc=None, inc_by=1):
        ws = []
        for w in waits:
            if isinstance(w, Sem):
                s, v = w, w.n
            else:
                s, v = w
            if v <= 0:
                continue
            if s.owner == eng and s.owner in ("tensor", "scalar", "vector"):
                continue
            prev = self.lastw[eng].get(s, 0)
            if v > prev:
                self.lastw[eng][s] = v
                ws.append((s.h, v))
        incs = []
        if inc is not None:
            inc.inc(inc_by)
            incs.append((inc.h, inc_by))
        self.q[eng].append((ws, fn, incs))

    def flush(self):
        for e in ENGS:
            items = self.q[e]
            if not items:
                continue
            self.q[e] = []

            def body(engine, items=items):
                for ws, fn, incs in items:
                    for (h, v) in ws:
                        engine.wait_ge(h, v)
                    inst = fn(engine)
                    for (h, d) in incs:
                        inst.then_inc(h, d)
            getattr(self.block, e)(body)


def build(cfg):
    stage = cfg.get("stage", "full")
    upto = cfg.get("upto", None)
    nheads = cfg.get("nheads", NH)
    dbg = cfg.get("debug", False)
    nc = bass.Bass("TRN2", target_bir_lowering=False, debug=False,
                   detect_race_conditions=False)

    def din(name, shape):
        return nc.dram_tensor(name, shape, F32, kind="ExternalInput").ap()

    XT = din("xt", [DM, TPC])
    QTW = din("qtw", [NCH * HD, DM])
    KTW = din("ktw", [NCH * HD, DM])
    VTW = din("vtw", [NCH * HD, DM])
    WGT = din("wgt", [NCH * HD, DM])
    WOT = din("wot", [NCH * HD, DM])
    GW = din("gw", [NIG * HD, NCH * IGRP])
    UW = din("uw", [NIG * HD, NCH * IGRP])
    DW = din("dw", [NIG * HD, (IGRP // HD) * DM])
    SLOPES = din("slopes", [1, NH])
    EXPO = din("expo", [1, NCORES])
    NWC = din("nwc", [HD, NCH])
    GSC = din("gsc", [HD, FI // HD])
    USC = din("usc", [HD, FI // HD])
    DSC = din("dsc", [HD, NCH])
    IDENT = din("ident", [HD, HD])
    IDXT = din("idxt", [BLK, BLK])
    ARRQ = din("arrq", [1, BLK])
    ARRK = din("arrk", [1, BLK])
    ONESROW = din("onesrow", [1, HD])
    ONESCOL = din("onescol", [HD, 1])

    outs = {}

    def dout(name, shape):
        outs[name] = nc.dram_tensor(name, shape, F32, kind="ExternalOutput").ap()
        return outs[name]

    def scr(name, shape, debug_out=False):
        if debug_out and dbg:
            return dout(name, shape)
        return nc.dram_tensor(name, shape, F32, kind="Internal").ap()

    kfm_scr = scr("kfm_scr", [DM, TPC], debug_out=True)
    vtm_scr = scr("vtm_scr", [TPC, DM], debug_out=True)
    x1_scr = (dout("x1_scr", [DM, TPC]) if stage != "full"
              else nc.dram_tensor("x1_scr", [DM, TPC], F32, kind="Internal").ap())
    ag_in = nc.dram_tensor("ag_in", [NH * HD, HD], F32, kind="Internal").ap()
    ag_out = nc.dram_tensor("ag_out", [NCORES * NH * HD, HD], F32, kind="Internal").ap()
    if dbg:
        KV0 = dout("kv0_dbg", [HD, NH * HD])
        ODBG = dout("o_dbg", [HD, NH * TPC])
        GODBG = dout("go_dbg", [HD, NH * TPC])
        YDBG = dout("y_dbg", [HD, NCH * TPC])
    if stage == "full":
        OUTT = dout("outt", [DM, TPC])

    with ExitStack() as ctx:
        def sb(name, shape, dtype=F32):
            return ctx.enter_context(nc.sbuf_tensor("s_" + name, shape, dtype))

        def mksem(name, owner=None):
            return Sem(ctx.enter_context(nc.semaphore(name)), owner)

        ld = mksem("ld", "sync")
        gp = mksem("gp", "gpsimd")
        ts = mksem("ts", "tensor")
        ss = mksem("ss", "scalar")
        vs = mksem("vs", "vector")
        cc = mksem("cc", "gpsimd")
        st = mksem("st", "sync")

        ident = sb("ident", [HD, HD])
        idxt = sb("idxt", [HD, 2 * BLK])
        arrq = sb("arrq", [1, BLK])
        arrk = sb("arrk", [1, BLK])
        onesrow = sb("onesrow", [1, HD])
        onescol = sb("onescol", [HD, 1])
        slopes = sb("slopes", [1, NH])
        expo = sb("expo", [1, NCORES])
        nwc = sb("nwc", [HD, NCH])
        gsc = sb("gsc", [HD, FI // HD])
        usc = sb("usc", [HD, FI // HD])
        dsc = sb("dsc", [HD, NCH])
        sb16 = sb("sb16", [HD, NH])
        sbneg = sb("sbneg", [HD, NH])
        bdt = sb("bdt", [HD, NH])
        kdcol = sb("kdcol", [HD, NH * 2])
        wcomb = sb("wcomb", [HD, NCORES * NH])
        kvst = sb("kvst", [HD, NH * HD])
        rows = sb("rows", [1, 4 * TPC])
        bc1 = sb("bc1", [HD, TPC])
        bc2 = sb("bc2", [HD, TPC])

        block = ctx.enter_context(nc.Block())
        p = Prog(nc, block)

        def dma(eng, out, in_, sem, waits=()):
            p.emit(eng, lambda E, o=out, i=in_: E.dma_start(out=o, in_=i),
                   waits=waits, inc=sem, inc_by=16)
            return (sem, sem.n)

        def row(k, n=TPC):
            return rows[0:1, k * TPC:k * TPC + n]

        def bar():
            return [(ld, ld.n), (gp, gp.n), (ts, ts.n), (ss, ss.n), (vs, vs.n), (cc, cc.n)]

        def mm3(pb, wh, wl, xh_, xl_, tt, T, first_waits):
            for pi, (wp_, xp_) in enumerate(((wh, xh_), (wh, xl_), (wl, xh_))):
                for j in range(NCH):
                    p.emit("tensor",
                           lambda E, pi=pi, wp_=wp_, xp_=xp_, j=j, tt=tt, pb=pb:
                           E.matmul(
                               pb[:], wp_[:, j * HD:(j + 1) * HD],
                               xp_[:, j * T + tt * 512:j * T + (tt + 1) * 512],
                               start=(pi == 0 and j == 0),
                               stop=(pi == 2 and j == NCH - 1)),
                           waits=first_waits if (pi == 0 and j == 0) else [],
                           inc=(ts if (pi == 2 and j == NCH - 1) else None))

        def split_tile(fp, bh, bl, waits):
            p.emit("vector", lambda E: E.tensor_copy(bh, fp), waits=waits, inc=vs)
            p.emit("vector", lambda E: E.tensor_tensor(bl, fp, bh, ALU.subtract),
                   waits=[], inc=vs)
            return (vs, vs.n)

        _lrc = [0]

        def ln_rows_and_bcast(T, eps, rms=False, upto2=None):
            upto2 = upto2 or cfg.get("upto2")
            inv_dm = 1.0 / DM
            if not rms:
                p.emit("vector", lambda E: E.tensor_scalar(row(0, T), row(0, T), inv_dm,
                                                           None, ALU.mult), waits=[ss], inc=vs)
            p.emit("vector", lambda E: E.tensor_scalar(row(1, T), row(1, T), inv_dm,
                                                       None, ALU.mult), waits=[ss], inc=vs)
            if not rms:
                p.emit("vector", lambda E: E.tensor_tensor(row(2, T), row(0, T), row(0, T),
                                                           ALU.mult), waits=[], inc=vs)
                p.emit("vector", lambda E: E.tensor_tensor(row(1, T), row(1, T), row(2, T),
                                                           ALU.subtract), waits=[], inc=vs)
            p.emit("vector", lambda E: E.tensor_scalar(row(1, T), row(1, T), float(eps),
                                                       None, ALU.add), waits=[], inc=vs)
            if upto2 == "lr1":
                return
            p.emit("scalar", lambda E: E.activation(row(1, T), row(1, T), AF.Sqrt),
                   waits=[vs], inc=ss)
            p.emit("vector", lambda E: E.reciprocal(row(2, T), row(1, T)), waits=[ss], inc=vs)
            if upto2 == "lr2":
                return
            if not rms:
                p.emit("vector", lambda E: E.tensor_tensor(row(3, T), row(0, T), row(2, T),
                                                           ALU.mult), waits=[], inc=vs)
            if upto2 == "lr3":
                return
            _lrc[0] += 1
            with nc.psum_tensor(f"ps_bcr{_lrc[0]}", [HD, 512], F32) as ps_b:
                for u in range((T + 511) // 512):
                    w = min(512, T - u * 512)
                    p.emit("tensor", lambda E, u=u, w=w: E.matmul(
                        ps_b[:, 0:w], onesrow[:], row(2, T)[0:1, u * 512:u * 512 + w],
                        start=True, stop=True), waits=[vs, ss], inc=ts)
                    if upto2 == "lrb1":
                        continue
                    p.emit("scalar", lambda E, u=u, w=w: E.copy(
                        bc1[:, u * 512:u * 512 + w], ps_b[:, 0:w]), waits=[ts], inc=ss)
                    if upto2 == "lrb2" or rms:
                        continue
                    p.emit("tensor", lambda E, u=u, w=w: E.matmul(
                        ps_b[:, 0:w], onesrow[:], row(3, T)[0:1, u * 512:u * 512 + w],
                        start=True, stop=True), waits=[ss], inc=ts)
                    if upto2 == "lrb3":
                        continue
                    p.emit("scalar", lambda E, u=u, w=w: E.copy(
                        bc2[:, u * 512:u * 512 + w], ps_b[:, 0:w]), waits=[ts], inc=ss)
                p.flush()

        for (t, d) in [(ident, IDENT), (arrq, ARRQ), (arrk, ARRK),
                       (onesrow, ONESROW), (onescol, ONESCOL), (slopes, SLOPES),
                       (expo, EXPO), (nwc, NWC), (gsc, GSC), (usc, USC), (dsc, DSC)]:
            dma("sync", t[:], d, ld)
        for j in range(2):
            dma("sync", idxt[:, j * BLK:(j + 1) * BLK], IDXT[j * HD:(j + 1) * HD, :], ld)
        p0_ld = (ld, ld.n)

        with nc.psum_tensor("ps_bc0", [HD, 512], F32) as ps_bc:
            p.emit("tensor", lambda E: E.matmul(ps_bc[:, 0:NH], onesrow[:], slopes[:],
                                                start=True, stop=True),
                   waits=[p0_ld], inc=ts)
            p.emit("scalar", lambda E: E.copy(sb16[:], ps_bc[:, 0:NH]), waits=[ts], inc=ss)
            p.emit("vector", lambda E: E.tensor_scalar(sbneg[:], sb16[:], -1.0, None, ALU.mult),
                   waits=[ss], inc=vs)
            p.emit("scalar", lambda E: E.activation(bdt[:], sb16[:], AF.Exp, scale=-float(BLK)),
                   waits=[vs], inc=ss)
            p.emit("tensor", lambda E: E.matmul(ps_bc[:, 32:32 + NCORES], onesrow[:], expo[:],
                                                start=True, stop=True),
                   waits=[(ss, ss.n)], inc=ts)
            p.emit("scalar", lambda E: E.copy(bc1[:, 0:NCORES], ps_bc[:, 32:32 + NCORES]),
                   waits=[ts], inc=ss)
            for c in range(NCORES):
                p.emit("vector",
                       lambda E, c=c: E.tensor_scalar(bc2[:, 0:NH], sb16[:], -1024.0,
                                                      bc1[:, c:c + 1], ALU.mult, ALU.mult),
                       waits=[ss], inc=vs)
                p.emit("scalar",
                       lambda E, c=c: E.activation(wcomb[:, c * NH:(c + 1) * NH],
                                                   bc2[:, 0:NH], AF.Exp),
                       waits=[vs], inc=ss)
            with nc.psum_tensor("ps_tr0", [HD, HD], F32) as ps_tr:
                for h in range(NH):
                    p.emit("scalar",
                           lambda E, h=h: E.activation(row(0, BLK), arrk[:], AF.Exp,
                                                       scale=sbneg[0:1, h:h + 1]),
                           waits=[ts, vs], inc=ss)
                    p.emit("tensor",
                           lambda E: E.matmul(ps_bc[:, 0:BLK], onesrow[:], row(0, BLK),
                                              start=True, stop=True),
                           waits=[ss], inc=ts)
                    p.emit("scalar",
                           lambda E: E.copy(bc2[:, 0:BLK], ps_bc[:, 0:BLK]),
                           waits=[ts], inc=ss)
                    for j in range(2):
                        p.emit("tensor",
                               lambda E, j=j: E.transpose(ps_tr[:], bc2[:, j * HD:(j + 1) * HD],
                                                          ident[:]),
                               waits=[ss, p0_ld], inc=ts)
                        p.emit("scalar",
                               lambda E, h=h, j=j: E.copy(kdcol[:, h * 2 + j:h * 2 + j + 1],
                                                          ps_tr[:, 0:1]),
                               waits=[ts], inc=ss)
                p.flush()

        XT3 = XT.rearrange("(j p) t -> p j t", p=HD)
        kfm_scr3 = kfm_scr.rearrange("(h p) t -> p h t", p=HD)
        x1_scr3 = x1_scr.rearrange("(j p) t -> p j t", p=HD)

        def early_out():
            p.emit("sync", lambda E: E.nop(), waits=bar(), inc=None)
            p.flush()
            return nc, outs

        if upto == "p0":
            return early_out()

        with ExitStack() as p1ctx:
            p1sb = lambda n, s, dt=F32: p1ctx.enter_context(nc.sbuf_tensor("p1_" + n, s, dt))
            xlnh = p1sb("xlnh", [HD, NCH * TPC], BF16)
            xlnl = p1sb("xlnl", [HD, NCH * TPC], BF16)
            lntmp = p1sb("lntmp", [HD, TPC])
            xa = [p1sb(f"xa{i}", [HD, TPC]) for i in range(2)]
            sq = [p1sb(f"sq{i}", [HD, TPC]) for i in range(2)]
            kstage = p1sb("kstage", [HD, TPC])
            vstage = p1sb("vstage", [HD, TPC])
            vtstage = p1sb("vtstage", [HD, 8 * HD])
            ktm = p1sb("ktm", [HD, BLK])
            kpan = [p1sb(f"kpan{i}", [HD, NCH * HD]) for i in range(2)]
            vpan = [p1sb(f"vpan{i}", [HD, NCH * HD]) for i in range(2)]
            kbh = [p1sb(f"kbh{i}", [HD, NCH * HD], BF16) for i in range(2)]
            kbl = [p1sb(f"kbl{i}", [HD, NCH * HD], BF16) for i in range(2)]
            vbh = [p1sb(f"vbh{i}", [HD, NCH * HD], BF16) for i in range(2)]
            vbl = [p1sb(f"vbl{i}", [HD, NCH * HD], BF16) for i in range(2)]

            xa_free = [bar(), bar()]
            with ExitStack() as sctx:
                ps_s1 = [sctx.enter_context(nc.psum_tensor(f"ps_s1{u}", [1, 512], F32))
                         for u in range(2)]
                ps_s2 = [sctx.enter_context(nc.psum_tensor(f"ps_s2{u}", [1, 512], F32))
                         for u in range(2)]
                for j in range(NCH):
                    lv = dma("sync", xa[j % 2][:], XT3[:, j, :], ld, waits=xa_free[j % 2])
                    sqb = sq[j % 2]
                    p.emit("scalar", lambda E, j=j, sqb=sqb: E.square(sqb[:], xa[j % 2][:]),
                           waits=[lv, (ts, ts.n)], inc=ss)
                    for u in range(2):
                        p.emit("tensor", lambda E, j=j, u=u: E.matmul(
                            ps_s1[u][:], onescol[:], xa[j % 2][:, u * 512:(u + 1) * 512],
                            start=(j == 0), stop=(j == NCH - 1)),
                            waits=[lv, ss], inc=None)
                        p.emit("tensor", lambda E, j=j, u=u, sqb=sqb: E.matmul(
                            ps_s2[u][:], onescol[:], sqb[:, u * 512:(u + 1) * 512],
                            start=(j == 0), stop=(j == NCH - 1)),
                            waits=[ss], inc=(ts if u == 1 else None))
                    xa_free[j % 2] = [(ts, ts.n)]
                for u in range(2):
                    p.emit("scalar", lambda E, u=u: E.copy(row(0)[0:1, u * 512:(u + 1) * 512],
                                                           ps_s1[u][:]), waits=[ts], inc=ss)
                    p.emit("scalar", lambda E, u=u: E.copy(row(1)[0:1, u * 512:(u + 1) * 512],
                                                           ps_s2[u][:]), waits=[ts], inc=ss)
                p.flush()
            if upto == "stats0":
                return early_out()
            ln_rows_and_bcast(TPC, EPS_LN)
            if upto == "stats":
                return early_out()
            xa_free = [[(ts, ts.n)], [(ts, ts.n)]]
            for j in range(NCH):
                lv = dma("sync", xa[j % 2][:], XT3[:, j, :], ld, waits=xa_free[j % 2])
                p.emit("vector", lambda E, j=j: E.tensor_tensor(
                    lntmp[:], xa[j % 2][:], bc1[:], ALU.mult),
                    waits=[lv, ss], inc=vs)
                p.emit("vector", lambda E, j=j: E.tensor_tensor(
                    lntmp[:], lntmp[:], bc2[:], ALU.subtract), waits=[], inc=vs)
                p.emit("vector", lambda E, j=j: E.tensor_copy(
                    xlnh[:, j * TPC:(j + 1) * TPC], lntmp[:]), waits=[], inc=vs)
                p.emit("vector", lambda E, j=j: E.tensor_tensor(
                    xlnl[:, j * TPC:(j + 1) * TPC], lntmp[:],
                    xlnh[:, j * TPC:(j + 1) * TPC], ALU.subtract), waits=[], inc=vs)
                xa_free[j % 2] = [(vs, vs.n)]
            if upto == "xln":
                return early_out()


            p.emit("vector", lambda E: E.memset(kvst[:], 0.0), waits=[], inc=vs)
            with ExitStack() as dctx:
                ps_k = [dctx.enter_context(nc.psum_tensor(f"ps_k{i}", [HD, 512], F32))
                        for i in range(2)]
                ps_tr = [dctx.enter_context(nc.psum_tensor(f"ps_ktr{i}", [HD, HD], F32))
                         for i in range(2)]
                ps_w = dctx.enter_context(nc.psum_tensor("ps_w", [HD, HD], F32))
                kpan_free = [[(ts, ts.n)], [(ts, ts.n)]]
                vpan_free = [[(ts, ts.n)], [(ts, ts.n)]]
                kst_free = [(gp, gp.n), (ts, ts.n)]
                vst_free = [(ts, ts.n)]
                vts_free = [(gp, gp.n), (ts, ts.n)]
                xlnw = (vs, vs.n)
                kb_free = [bar(), bar()]
                vb_free = [bar(), bar()]
                for h in range(nheads):
                    kp, vp = kpan[h % 2], vpan[h % 2]
                    lvk = dma("sync", kp[:], KTW[h * HD:(h + 1) * HD, :], ld,
                              waits=kpan_free[h % 2])
                    lvv = dma("sync", vp[:], VTW[h * HD:(h + 1) * HD, :], ld,
                              waits=vpan_free[h % 2])
                    kspl = split_tile(kpan[h % 2][:], kbh[h % 2][:], kbl[h % 2][:],
                                      [lvk, *kb_free[h % 2]])
                    vspl = split_tile(vpan[h % 2][:], vbh[h % 2][:], vbl[h % 2][:],
                                      [lvv, *vb_free[h % 2]])
                    kpan_free[h % 2] = [(vs, kspl[1])]
                    vpan_free[h % 2] = [(vs, vspl[1])]
                    for tt in range(2):
                        pb = ps_k[tt]
                        mm3(pb, kbh[h % 2], kbl[h % 2], xlnh, xlnl, tt, TPC,
                            [kspl, xlnw, (ss, ss.n)])
                        p.emit("scalar", lambda E, tt=tt, pb=pb: E.activation(
                            kstage[:, tt * 512:(tt + 1) * 512], pb[:], AF.Silu),
                            waits=[ts, *kst_free], inc=ss)
                    kb_free[h % 2] = [(ts, ts.n)]
                    ksp = dma("gpsimd", kfm_scr3[:, h, :], kstage[:], gp, waits=[(ss, ss.n)])
                    for tt in range(2):
                        pb = ps_k[tt]
                        mm3(pb, vbh[h % 2], vbl[h % 2], xlnh, xlnl, tt, TPC,
                            [vspl, (ss, ss.n)])
                        p.emit("scalar", lambda E, tt=tt, pb=pb: E.activation(
                            vstage[:, tt * 512:(tt + 1) * 512], pb[:], AF.Silu),
                            waits=[ts, *vst_free], inc=ss)
                    vb_free[h % 2] = [(ts, ts.n)]
                    vst_free = [(ts, ts.n)]
                    for tc in range(8):
                        pt = ps_tr[tc % 2]
                        p.emit("tensor", lambda E, tc=tc, pt=pt: E.transpose(
                            pt[:], vstage[:, tc * HD:(tc + 1) * HD], ident[:]),
                            waits=[(ss, ss.n), *vts_free], inc=ts)
                        p.emit("scalar", lambda E, tc=tc, pt=pt: E.copy(
                            vtstage[:, tc * HD:(tc + 1) * HD], pt[:]),
                            waits=[ts] + (list(vts_free) if tc == 0 else []), inc=ss)
                    vst_free = [(ts, ts.n)]
                    vsp = dma("gpsimd",
                              vtm_scr[:, h * HD:(h + 1) * HD].rearrange(
                                  "(tc p) e -> p tc e", p=HD),
                              vtstage[:].rearrange("p (tc e) -> p tc e", tc=8),
                              gp, waits=[(ss, ss.n)])
                    for b in range(NBLK):
                        for j in range(2):
                            p.emit("tensor", lambda E, b=b, j=j: E.transpose(
                                ps_tr[j][:], kstage[:, b * BLK + j * HD:b * BLK + (j + 1) * HD],
                                ident[:]), waits=[(ss, ss.n), (vs, vs.n)], inc=ts)
                            p.emit("scalar", lambda E, j=j: E.copy(
                                ktm[:, j * HD:(j + 1) * HD], ps_tr[j][:]), waits=[ts], inc=ss)
                            p.emit("vector", lambda E, h=h, j=j: E.tensor_scalar(
                                ktm[:, j * HD:(j + 1) * HD], ktm[:, j * HD:(j + 1) * HD],
                                kdcol[:, h * 2 + j:h * 2 + j + 1], None, ALU.mult),
                                waits=[ss], inc=vs)
                        for j in range(2):
                            p.emit("tensor", lambda E, h=h, b=b, j=j: E.matmul(
                                ps_w[:], ktm[:, j * HD:(j + 1) * HD],
                                vtstage[:, (2 * b + j) * HD:(2 * b + j + 1) * HD],
                                start=(j == 0), stop=(j == 1)),
                                waits=[vs], inc=(ts if j == 1 else None))
                        p.emit("vector", lambda E, h=h: E.tensor_scalar(
                            kvst[:, h * HD:(h + 1) * HD], kvst[:, h * HD:(h + 1) * HD],
                            bdt[:, h:h + 1], None, ALU.mult), waits=[], inc=vs)
                        p.emit("vector", lambda E, h=h: E.tensor_tensor(
                            kvst[:, h * HD:(h + 1) * HD], kvst[:, h * HD:(h + 1) * HD],
                            ps_w[:], ALU.add), waits=[ts], inc=vs)
                    kst_free = [(gp, ksp[1]), (ts, ts.n)]
                    vts_free = [(gp, vsp[1]), (vs, vs.n), (ts, ts.n)]
                p.flush()

            if upto == "noag":
                return early_out()
            ag_in3 = ag_in.rearrange("(h d) e -> d h e", d=HD)
            agv = dma("sync", ag_in3, kvst[:].rearrange("d (h e) -> d h e", h=NH), ld,
                      waits=[(vs, vs.n)])
            p.emit("gpsimd", lambda E: E.collective_compute(
                "AllGather", ALU.bypass, replica_groups=[list(range(NCORES))],
                ins=[ag_in.opt()], outs=[ag_out.opt()]),
                waits=[agv], inc=cc)
            p.flush()

        def emit_kv_init():
            with ExitStack() as ectx:
                sslab = [ectx.enter_context(nc.sbuf_tensor(f"sslab{i}", [HD, NH * HD], F32))
                         for i in range(2)]
                p.emit("vector", lambda E: E.memset(kvst[:], 0.0), waits=[(cc, cc.n)], inc=vs)
                slab_free = [bar(), bar()]
                for c in range(NCORES):
                    srcap = ag_out[c * NH * HD:(c + 1) * NH * HD, :].rearrange(
                        "(h d) e -> d h e", d=HD)
                    lv = dma("sync", sslab[c % 2][:].rearrange("d (h e) -> d h e", h=NH),
                             srcap, ld, waits=[(cc, cc.n), *slab_free[c % 2]])
                    for h in range(NH):
                        p.emit("vector", lambda E, c=c, h=h: E.scalar_tensor_tensor(
                            kvst[:, h * HD:(h + 1) * HD], sslab[c % 2][:, h * HD:(h + 1) * HD],
                            wcomb[:, c * NH + h:c * NH + h + 1], kvst[:, h * HD:(h + 1) * HD],
                            ALU.mult, ALU.add), waits=[lv], inc=vs)
                    slab_free[c % 2] = [(vs, vs.n)]
                if dbg:
                    dma("gpsimd", KV0, kvst[:], gp, waits=[(vs, vs.n)])
                p.flush()

        if stage == "p1":
            emit_kv_init()
            p.emit("sync", lambda E: E.nop(), waits=bar(), inc=None)
            p.flush()
            return nc, outs

        for hh in range(2):
            with ExitStack() as hctx:
                hsb = lambda name, shape: hctx.enter_context(
                    nc.sbuf_tensor(f"h{hh}_{name}", shape, F32))
                xha = [hsb(f"xha{i}", [HD, HT]) for i in range(2)]
                hbsb = lambda name, shape: hctx.enter_context(
                    nc.sbuf_tensor(f"h{hh}_{name}", shape, BF16))
                xyh = hbsb("xyh", [HD, NCH * HT])
                xyl = hbsb("xyl", [HD, NCH * HT])
                lnt2 = hsb("lnt2", [HD, HT])
                qbuf = hsb("qbuf", [HD, NH * HT])
                obuf = hsb("obuf", [HD, NH * HT])
                sqh = [hsb(f"sqh{i}", [HD, HT]) for i in range(2)]
                pan = [hsb(f"pan{i}", [HD, NCH * HD]) for i in range(2)]
                pbh = [hbsb(f"pbh{i}", [HD, NCH * HD]) for i in range(2)]
                pbl = [hbsb(f"pbl{i}", [HD, NCH * HD]) for i in range(2)]
                kre = hsb("kre", [HD, BLK])
                vre = hsb("vre", [HD, BLK])
                ktmh = hsb("ktmh", [HD, BLK])
                qq = hsb("qq", [HD, BLK])
                sdt = hsb("sdt", [HD, 2 * BLK])
                dtt = hsb("dtt", [HD, 2 * BLK])
                qdb = hsb("qdb", [HD, BLK])
                gtile = hsb("gtile", [HD, HT])

                xa_free2 = [bar(), bar()]
                with ExitStack() as sctx:
                    ps_s1 = sctx.enter_context(nc.psum_tensor(f"ps2s1{hh}", [1, 512], F32))
                    ps_s2 = sctx.enter_context(nc.psum_tensor(f"ps2s2{hh}", [1, 512], F32))
                    for j in range(NCH):
                        lv = dma("sync", xha[j % 2][:], XT3[:, j, hh * HT:(hh + 1) * HT],
                                 ld, waits=xa_free2[j % 2])
                        sqb = sqh[j % 2]
                        p.emit("scalar", lambda E, j=j, sqb=sqb: E.square(
                            sqb[:], xha[j % 2][:]),
                            waits=[lv, (ts, ts.n)], inc=ss)
                        p.emit("tensor", lambda E, j=j: E.matmul(
                            ps_s1[:], onescol[:], xha[j % 2][:],
                            start=(j == 0), stop=(j == NCH - 1)),
                            waits=[lv, ss], inc=None)
                        p.emit("tensor", lambda E, j=j, sqb=sqb: E.matmul(
                            ps_s2[:], onescol[:], sqb[:],
                            start=(j == 0), stop=(j == NCH - 1)),
                            waits=[ss], inc=ts)
                        xa_free2[j % 2] = [(ts, ts.n)]
                    p.emit("scalar", lambda E: E.copy(row(0, HT), ps_s1[:]), waits=[ts], inc=ss)
                    p.emit("scalar", lambda E: E.copy(row(1, HT), ps_s2[:]), waits=[ts], inc=ss)
                    p.flush()
                ln_rows_and_bcast(HT, EPS_LN)
                for j in range(NCH):
                    lv = dma("sync", xha[j % 2][:], XT3[:, j, hh * HT:(hh + 1) * HT],
                             ld, waits=xa_free2[j % 2])
                    p.emit("vector", lambda E, j=j: E.tensor_tensor(
                        lnt2[:], xha[j % 2][:],
                        bc1[:, 0:HT], ALU.mult), waits=[lv, ss], inc=vs)
                    p.emit("vector", lambda E, j=j: E.tensor_tensor(
                        lnt2[:], lnt2[:], bc2[:, 0:HT], ALU.subtract), waits=[], inc=vs)
                    split_tile(lnt2[:], xyh[:, j * HT:(j + 1) * HT],
                               xyl[:, j * HT:(j + 1) * HT], [])
                    xa_free2[j % 2] = [(vs, vs.n)]

                with ExitStack() as qctx:
                    ps_q = [qctx.enter_context(nc.psum_tensor(f"ps_q{hh}{i}", [HD, 512], F32))
                            for i in range(2)]
                    pan_free = [[(ts, ts.n)], [(ts, ts.n)]]
                    pb_free = [bar(), bar()]
                    ps_free = [(ss, ss.n), (ss, ss.n)]
                    for h in range(NH):
                        wp = pan[h % 2]
                        lv = dma("sync", wp[:], QTW[h * HD:(h + 1) * HD, :], ld,
                                 waits=pan_free[h % 2])
                        spl = split_tile(wp[:], pbh[h % 2][:], pbl[h % 2][:],
                                         [lv, *pb_free[h % 2]])
                        pan_free[h % 2] = [spl]
                        pb = ps_q[h % 2]
                        mm3(pb, pbh[h % 2], pbl[h % 2], xyh, xyl, 0, HT,
                            [spl, ps_free[h % 2]])
                        p.emit("scalar", lambda E, h=h, pb=pb: E.activation(
                            qbuf[:, h * HT:(h + 1) * HT], pb[:], AF.Silu),
                            waits=[ts, (gp, gp.n)], inc=ss)
                        ps_free[h % 2] = (ss, ss.n)
                        pb_free[h % 2] = [(ts, ts.n)]
                    p.flush()

                if hh == 0:
                    emit_kv_init()

                with ExitStack() as kctx:
                    ps_st = [kctx.enter_context(nc.psum_tensor(f"ps_st{hh}{i}", [HD, BLK], F32))
                             for i in range(2)]
                    ps_o = kctx.enter_context(nc.psum_tensor(f"ps_o{hh}", [HD, BLK], F32))
                    ps_tr = [kctx.enter_context(nc.psum_tensor(f"ps_ptr{hh}{i}", [HD, HD], F32))
                             for i in range(2)]
                    ps_w = kctx.enter_context(nc.psum_tensor(f"ps_pw{hh}", [HD, HD], F32))
                    ps_bq = kctx.enter_context(nc.psum_tensor(f"ps_bq{hh}", [HD, BLK], F32))
                    for bb in range(2):
                        b = 2 * hh + bb
                        for h in range(NH):
                            kv_h = kvst[:, h * HD:(h + 1) * HD]
                            qcols = qbuf[:, h * HT + bb * BLK:h * HT + (bb + 1) * BLK]
                            lvk = dma("sync", kre[:], kfm_scr3[:, h, b * BLK:(b + 1) * BLK],
                                      ld, waits=[(ts, ts.n)])
                            lvv = dma("sync", vre[:].rearrange("p (j e) -> p j e", j=2),
                                      vtm_scr[b * BLK:(b + 1) * BLK,
                                              h * HD:(h + 1) * HD].rearrange(
                                          "(j p) e -> p j e", p=HD),
                                      ld, waits=[(ts, ts.n)])
                            p.emit("scalar", lambda E, h=h: E.activation(
                                row(3, BLK), arrq[:], AF.Exp, scale=sbneg[0:1, h:h + 1]),
                                waits=[(ts, ts.n), (vs, vs.n)], inc=ss)
                            p.emit("tensor", lambda E: E.matmul(
                                ps_bq[:], onesrow[:], row(3, BLK), start=True, stop=True),
                                waits=[ss], inc=ts)
                            p.emit("scalar", lambda E: E.copy(qdb[:], ps_bq[:]),
                                   waits=[ts], inc=ss)
                            for j in range(2):
                                p.emit("scalar", lambda E, h=h, j=j: E.activation(
                                    dtt[:, j * BLK:(j + 1) * BLK], idxt[:, j * BLK:(j + 1) * BLK],
                                    AF.Exp, scale=sbneg[:, h:h + 1]),
                                    waits=[(vs, vs.n)], inc=ss)
                            p.emit("vector", lambda E, qcols=qcols: E.tensor_tensor(
                                qq[:], qcols, qdb[:], ALU.mult),
                                waits=[(ss, ss.n)], inc=vs)
                            for j in range(2):
                                p.emit("tensor", lambda E, j=j, qcols=qcols: E.matmul(
                                    ps_st[j][:], kre[:, j * HD:(j + 1) * HD], qcols,
                                    start=True, stop=True),
                                    waits=[lvk, (ss, ss.n), (vs, vs.n)], inc=ts)
                            for j in range(2):
                                p.emit("vector", lambda E, j=j: E.tensor_tensor(
                                    sdt[:, j * BLK:(j + 1) * BLK], ps_st[j][:],
                                    dtt[:, j * BLK:(j + 1) * BLK], ALU.mult),
                                    waits=[ts, ss], inc=vs)
                            p.emit("tensor", lambda E, kv_h=kv_h: E.matmul(
                                ps_o[:], kv_h, qq[:], start=True, stop=False),
                                waits=[vs], inc=None)
                            for j in range(2):
                                p.emit("tensor", lambda E, j=j: E.matmul(
                                    ps_o[:], vre[:, j * HD:(j + 1) * HD],
                                    sdt[:, j * BLK:(j + 1) * BLK],
                                    start=False, stop=(j == 1)),
                                    waits=[lvv, vs], inc=(ts if j == 1 else None))
                            o_done = (ts, ts.n)
                            p.emit("scalar", lambda E, h=h, bb=bb: E.copy(
                                obuf[:, h * HT + bb * BLK:h * HT + (bb + 1) * BLK], ps_o[:]),
                                waits=[ts], inc=ss)
                            for j in range(2):
                                p.emit("tensor", lambda E, j=j: E.transpose(
                                    ps_tr[j][:], kre[:, j * HD:(j + 1) * HD], ident[:]),
                                    waits=[lvk, (vs, vs.n)], inc=ts)
                                p.emit("scalar", lambda E, j=j: E.copy(
                                    ktmh[:, j * HD:(j + 1) * HD], ps_tr[j][:]),
                                    waits=[ts], inc=ss)
                                p.emit("vector", lambda E, h=h, j=j: E.tensor_scalar(
                                    ktmh[:, j * HD:(j + 1) * HD], ktmh[:, j * HD:(j + 1) * HD],
                                    kdcol[:, h * 2 + j:h * 2 + j + 1], None, ALU.mult),
                                    waits=[ss], inc=vs)
                            for j in range(2):
                                p.emit("tensor", lambda E, j=j: E.matmul(
                                    ps_w[:], ktmh[:, j * HD:(j + 1) * HD],
                                    vre[:, j * HD:(j + 1) * HD],
                                    start=(j == 0), stop=(j == 1)),
                                    waits=[vs], inc=(ts if j == 1 else None))
                            p.emit("vector", lambda E, h=h, kv_h=kv_h: E.tensor_scalar(
                                kv_h, kv_h, bdt[:, h:h + 1], None, ALU.mult),
                                waits=[o_done], inc=vs)
                            p.emit("vector", lambda E, kv_h=kv_h: E.tensor_tensor(
                                kv_h, kv_h, ps_w[:], ALU.add), waits=[ts], inc=vs)
                    p.flush()

                if dbg:
                    for h in range(NH):
                        dma("gpsimd", ODBG[:, (h * 2 + hh) * HT:(h * 2 + hh + 1) * HT],
                            obuf[:, h * HT:(h + 1) * HT], gp, waits=[(ss, ss.n), (vs, vs.n)])
                with ExitStack() as sctx:
                    ps_s2 = sctx.enter_context(nc.psum_tensor(f"ps_rms{hh}", [1, 512], F32))
                    for h in range(NH):
                        sqb = sqh[h % 2]
                        p.emit("scalar", lambda E, h=h, sqb=sqb: E.square(
                            sqb[:], obuf[:, h * HT:(h + 1) * HT]),
                            waits=[(vs, vs.n), (ts, ts.n)], inc=ss)
                        p.emit("tensor", lambda E, h=h, sqb=sqb: E.matmul(
                            ps_s2[:], onescol[:], sqb[:],
                            start=(h == 0), stop=(h == NH - 1)),
                            waits=[ss], inc=ts)
                    p.emit("scalar", lambda E: E.copy(row(1, HT), ps_s2[:]), waits=[ts], inc=ss)
                    p.flush()
                ln_rows_and_bcast(HT, EPS_RMS, rms=True)
                for h in range(NH):
                    p.emit("vector", lambda E, h=h: E.tensor_tensor(
                        obuf[:, h * HT:(h + 1) * HT], obuf[:, h * HT:(h + 1) * HT],
                        bc1[:, 0:HT], ALU.mult), waits=[ss, (gp, gp.n)], inc=vs)
                    p.emit("vector", lambda E, h=h: E.tensor_scalar(
                        obuf[:, h * HT:(h + 1) * HT], obuf[:, h * HT:(h + 1) * HT],
                        nwc[:, h:h + 1], None, ALU.mult), waits=[], inc=vs)

                with ExitStack() as gctx:
                    ps_g = [gctx.enter_context(nc.psum_tensor(f"ps_g{hh}{i}", [HD, 512], F32))
                            for i in range(2)]
                    pan_free = [[(ts, ts.n)], [(ts, ts.n)]]
                    pb_free = [bar(), bar()]
                    ps_free = [(ss, ss.n), (ss, ss.n)]
                    for oc in range(NCH):
                        wp = pan[oc % 2]
                        lv = dma("sync", wp[:], WGT[oc * HD:(oc + 1) * HD, :], ld,
                                 waits=pan_free[oc % 2])
                        spl = split_tile(wp[:], pbh[oc % 2][:], pbl[oc % 2][:],
                                         [lv, *pb_free[oc % 2]])
                        pan_free[oc % 2] = [spl]
                        pb = ps_g[oc % 2]
                        mm3(pb, pbh[oc % 2], pbl[oc % 2], xyh, xyl, 0, HT,
                            [spl, ps_free[oc % 2]])
                        p.emit("scalar", lambda E, pb=pb: E.activation(
                            gtile[:], pb[:], AF.Sigmoid), waits=[ts, (vs, vs.n)], inc=ss)
                        p.emit("vector", lambda E, oc=oc: E.tensor_tensor(
                            obuf[:, oc * HT:(oc + 1) * HT], obuf[:, oc * HT:(oc + 1) * HT],
                            gtile[:], ALU.mult), waits=[ss], inc=vs)
                        ps_free[oc % 2] = (ss, ss.n)
                        pb_free[oc % 2] = [(ts, ts.n)]
                    p.flush()

                if dbg:
                    for h in range(NH):
                        dma("gpsimd", GODBG[:, (h * 2 + hh) * HT:(h * 2 + hh + 1) * HT],
                            obuf[:, h * HT:(h + 1) * HT], gp, waits=[(ss, ss.n), (vs, vs.n)])
                with ExitStack() as octx:
                    ps_y = [octx.enter_context(nc.psum_tensor(f"ps_y{hh}{i}", [HD, 512], F32))
                            for i in range(2)]
                    ps_s1 = octx.enter_context(nc.psum_tensor(f"ps_ys1{hh}", [1, 512], F32))
                    ps_s2 = octx.enter_context(nc.psum_tensor(f"ps_ys2{hh}", [1, 512], F32))
                    pan_free = [[(ts, ts.n)], [(ts, ts.n)]]
                    ps_free = [(ss, ss.n), (ss, ss.n)]
                    for fc in range(NCH):
                        wp = pan[fc % 2]
                        lv = dma("sync", wp[:], WOT[fc * HD:(fc + 1) * HD, :], ld,
                                 waits=pan_free[fc % 2])
                        pb = ps_y[fc % 2]
                        for oc in range(NCH):
                            p.emit("tensor", lambda E, fc=fc, oc=oc, pb=pb, wp=wp: E.matmul(
                                pb[:], wp[:, oc * HD:(oc + 1) * HD],
                                obuf[:, oc * HT:(oc + 1) * HT],
                                start=(oc == 0), stop=(oc == NCH - 1)),
                                waits=[lv, ps_free[fc % 2], (vs, vs.n)],
                                inc=(ts if oc == NCH - 1 else None))
                        p.emit("scalar", lambda E, fc=fc, pb=pb: E.copy(
                            qbuf[:, fc * HT:(fc + 1) * HT], pb[:]), waits=[ts], inc=ss)
                        sqb = sqh[fc % 2]
                        p.emit("scalar", lambda E, fc=fc, sqb=sqb: E.square(
                            sqb[:], qbuf[:, fc * HT:(fc + 1) * HT]),
                            waits=[(ts, ts.n)], inc=ss)
                        p.emit("tensor", lambda E, fc=fc: E.matmul(
                            ps_s1[:], onescol[:], qbuf[:, fc * HT:(fc + 1) * HT],
                            start=(fc == 0), stop=(fc == NCH - 1)), waits=[ss], inc=None)
                        p.emit("tensor", lambda E, fc=fc, sqb=sqb: E.matmul(
                            ps_s2[:], onescol[:], sqb[:],
                            start=(fc == 0), stop=(fc == NCH - 1)),
                            waits=[ss], inc=ts)
                        ps_free[fc % 2] = (ss, ss.n)
                        pan_free[fc % 2] = [(ts, ts.n)]
                    p.emit("scalar", lambda E: E.copy(row(0, HT), ps_s1[:]), waits=[ts], inc=ss)
                    p.emit("scalar", lambda E: E.copy(row(1, HT), ps_s2[:]), waits=[ts], inc=ss)
                    p.flush()
                if dbg:
                    for j in range(NCH):
                        dma("gpsimd", YDBG[:, (j * 2 + hh) * HT:(j * 2 + hh + 1) * HT],
                            qbuf[:, j * HT:(j + 1) * HT], gp, waits=[(ss, ss.n)])
                ln_rows_and_bcast(HT, EPS_LN)
                for j in range(NCH):
                    lv = dma("sync", xha[j % 2][:], XT3[:, j, hh * HT:(hh + 1) * HT],
                             ld, waits=xa_free2[j % 2])
                    p.emit("vector", lambda E, j=j: E.tensor_tensor(
                        qbuf[:, j * HT:(j + 1) * HT], qbuf[:, j * HT:(j + 1) * HT],
                        bc1[:, 0:HT], ALU.mult), waits=[ss, (gp, gp.n)], inc=vs)
                    p.emit("vector", lambda E, j=j: E.tensor_tensor(
                        qbuf[:, j * HT:(j + 1) * HT], qbuf[:, j * HT:(j + 1) * HT],
                        bc2[:, 0:HT], ALU.subtract), waits=[], inc=vs)
                    p.emit("vector", lambda E, j=j: E.tensor_tensor(
                        qbuf[:, j * HT:(j + 1) * HT], qbuf[:, j * HT:(j + 1) * HT],
                        xha[j % 2][:], ALU.add), waits=[lv], inc=vs)
                    dma("gpsimd", x1_scr3[:, j, hh * HT:(hh + 1) * HT],
                        qbuf[:, j * HT:(j + 1) * HT], gp, waits=[(vs, vs.n)])
                    xa_free2[j % 2] = [(vs, vs.n)]
                p.flush()

        if stage == "p2":
            p.emit("sync", lambda E: E.nop(), waits=bar(), inc=None)
            p.flush()
            return nc, outs

        OUTT3 = OUTT.rearrange("(j p) t -> p j t", p=HD)
        nic = IGRP // HD
        with ExitStack() as fctx:
            fsb = lambda name, shape, dt=F32: fctx.enter_context(
                nc.sbuf_tensor(f"f_{name}", shape, dt))
            ln2h = fsb("ln2h", [HD, NCH * TPC], BF16)
            ln2l = fsb("ln2l", [HD, NCH * TPC], BF16)
            y2 = fsb("y2", [HD, NCH * HT])
            x1t = [fsb(f"x1t{i}", [HD, TPC]) for i in range(2)]
            sqf = [fsb(f"sqf{i}", [HD, TPC]) for i in range(2)]
            w8 = [fsb(f"w8_{i}", [HD, NCH * IGRP], FP8) for i in range(2)]
            gall = fsb("gall", [HD, nic * HT])
            utmp = fsb("utmp", [HD, HT])
            fh = fsb("fh", [HD, nic * HT], BF16)
            fl = fsb("fl", [HD, nic * HT], BF16)
            lntmp = fsb("lntmp", [HD, TPC])
            x1_free = [bar(), bar()]
            with ExitStack() as sctx:
                ps_s1 = [sctx.enter_context(nc.psum_tensor(f"ps3s1{u}", [1, 512], F32))
                         for u in range(2)]
                ps_s2 = [sctx.enter_context(nc.psum_tensor(f"ps3s2{u}", [1, 512], F32))
                         for u in range(2)]
                for j in range(NCH):
                    lv = dma("sync", x1t[j % 2][:], x1_scr3[:, j, :],
                             ld, waits=[*x1_free[j % 2], (gp, gp.n)])
                    sqb = sqf[j % 2]
                    p.emit("scalar", lambda E, j=j, sqb=sqb: E.square(
                        sqb[:], x1t[j % 2][:]), waits=[lv, (ts, ts.n)], inc=ss)
                    for u in range(2):
                        p.emit("tensor", lambda E, j=j, u=u: E.matmul(
                            ps_s1[u][:], onescol[:], x1t[j % 2][:, u * 512:(u + 1) * 512],
                            start=(j == 0), stop=(j == NCH - 1)), waits=[lv, ss], inc=None)
                        p.emit("tensor", lambda E, j=j, u=u, sqb=sqb: E.matmul(
                            ps_s2[u][:], onescol[:], sqb[:, u * 512:(u + 1) * 512],
                            start=(j == 0), stop=(j == NCH - 1)),
                            waits=[ss], inc=(ts if u == 1 else None))
                    x1_free[j % 2] = [(ts, ts.n)]
                for u in range(2):
                    p.emit("scalar", lambda E, u=u: E.copy(row(0)[0:1, u * 512:(u + 1) * 512],
                                                           ps_s1[u][:]), waits=[ts], inc=ss)
                    p.emit("scalar", lambda E, u=u: E.copy(row(1)[0:1, u * 512:(u + 1) * 512],
                                                           ps_s2[u][:]), waits=[ts], inc=ss)
                p.flush()
            ln_rows_and_bcast(TPC, EPS_LN)
            for j in range(NCH):
                lv = dma("sync", x1t[j % 2][:], x1_scr3[:, j, :], ld, waits=x1_free[j % 2])
                p.emit("vector", lambda E, j=j: E.tensor_tensor(
                    lntmp[:], x1t[j % 2][:], bc1[:], ALU.mult),
                    waits=[lv, ss, (ts, ts.n)], inc=vs)
                p.emit("vector", lambda E, j=j: E.tensor_tensor(
                    lntmp[:], lntmp[:], bc2[:], ALU.subtract), waits=[], inc=vs)
                p.emit("vector", lambda E, j=j: E.tensor_copy(
                    ln2h[:, j * TPC:(j + 1) * TPC], lntmp[:]), waits=[], inc=vs)
                p.emit("vector", lambda E, j=j: E.tensor_tensor(
                    ln2l[:, j * TPC:(j + 1) * TPC], lntmp[:],
                    ln2h[:, j * TPC:(j + 1) * TPC], ALU.subtract), waits=[], inc=vs)
                x1_free[j % 2] = [(vs, vs.n)]

            with ExitStack() as wctx:
                ps_gu = [wctx.enter_context(
                    nc.psum_tensor(f"ps_gu{i}", [HD, 512], F32)) for i in range(2)]
                ps_d = [wctx.enter_context(
                    nc.psum_tensor(f"ps_d{i}", [HD, 512], F32)) for i in range(2)]
                slab_free = [[(ts, ts.n)], [(ts, ts.n)]]
                slot = 0
                lnw = (vs, vs.n)
                for hh in range(2):
                    for ig in range(NIG):
                        sg = slot; slot = (slot + 1) % 2
                        lvg = dma("gpsimd", w8[sg][:], GW[ig * HD:(ig + 1) * HD, :], gp,
                                  waits=slab_free[sg])
                        for ic in range(nic):
                            pb = ps_gu[ic % 2]
                            for pi, lnb in enumerate((ln2h, ln2l)):
                                for j in range(NCH):
                                    p.emit("tensor", lambda E, ig=ig, ic=ic, j=j, pb=pb,
                                           lnb=lnb, sg=sg, pi=pi, hh=hh:
                                           E.matmul(pb[:],
                                                    w8[sg][:, j * IGRP + ic * HD:j * IGRP + (ic + 1) * HD],
                                                    lnb[:, j * TPC + hh * HT:j * TPC + (hh + 1) * HT],
                                                    start=(pi == 0 and j == 0),
                                                    stop=(pi == 1 and j == NCH - 1)),
                                           waits=[lvg, lnw, (ss, ss.n)],
                                           inc=(ts if (pi == 1 and j == NCH - 1) else None))
                            p.emit("scalar", lambda E, ig=ig, ic=ic, pb=pb: E.activation(
                                gall[:, ic * HT:(ic + 1) * HT], pb[:],
                                AF.Silu, scale=gsc[:, ig * nic + ic:ig * nic + ic + 1]),
                                waits=[ts], inc=ss)
                        slab_free[sg] = [(ts, ts.n)]
                        su = slot; slot = (slot + 1) % 2
                        lvu = dma("gpsimd", w8[su][:], UW[ig * HD:(ig + 1) * HD, :], gp,
                                  waits=slab_free[su])
                        for ic in range(nic):
                            pb = ps_gu[ic % 2]
                            for pi, lnb in enumerate((ln2h, ln2l)):
                                for j in range(NCH):
                                    p.emit("tensor", lambda E, ig=ig, ic=ic, j=j, pb=pb,
                                           lnb=lnb, su=su, pi=pi, hh=hh:
                                           E.matmul(pb[:],
                                                    w8[su][:, j * IGRP + ic * HD:j * IGRP + (ic + 1) * HD],
                                                    lnb[:, j * TPC + hh * HT:j * TPC + (hh + 1) * HT],
                                                    start=(pi == 0 and j == 0),
                                                    stop=(pi == 1 and j == NCH - 1)),
                                           waits=[lvu, (vs, vs.n), (ss, ss.n)],
                                           inc=(ts if (pi == 1 and j == NCH - 1) else None))
                            p.emit("scalar", lambda E, ig=ig, ic=ic, pb=pb: E.activation(
                                utmp[:], pb[:], AF.Copy,
                                scale=usc[:, ig * nic + ic:ig * nic + ic + 1]),
                                waits=[ts, (vs, vs.n)], inc=ss)
                            p.emit("vector", lambda E, ic=ic: E.tensor_tensor(
                                gall[:, ic * HT:(ic + 1) * HT], gall[:, ic * HT:(ic + 1) * HT],
                                utmp[:], ALU.mult), waits=[ss], inc=vs)
                            p.emit("vector", lambda E, ic=ic: E.tensor_copy(
                                fh[:, ic * HT:(ic + 1) * HT], gall[:, ic * HT:(ic + 1) * HT]),
                                waits=[], inc=vs)
                            p.emit("vector", lambda E, ic=ic: E.tensor_tensor(
                                fl[:, ic * HT:(ic + 1) * HT], gall[:, ic * HT:(ic + 1) * HT],
                                fh[:, ic * HT:(ic + 1) * HT], ALU.subtract), waits=[], inc=vs)
                        slab_free[su] = [(ts, ts.n)]
                        sd = slot; slot = (slot + 1) % 2
                        lvd = dma("gpsimd", w8[sd][:], DW[ig * HD:(ig + 1) * HD, :], gp,
                                  waits=slab_free[sd])
                        for fc in range(NCH):
                            pb = ps_d[fc % 2]
                            for pi, fb in enumerate((fh, fl)):
                                for jc in range(nic):
                                    p.emit("tensor", lambda E, ig=ig, fc=fc, jc=jc, pb=pb,
                                           fb=fb, sd=sd, pi=pi:
                                           E.matmul(pb[:],
                                                    w8[sd][:, jc * DM + fc * HD:jc * DM + (fc + 1) * HD],
                                                    fb[:, jc * HT:(jc + 1) * HT],
                                                    start=(pi == 0 and jc == 0),
                                                    stop=(pi == 1 and jc == nic - 1)),
                                           waits=[lvd, (vs, vs.n), (ss, ss.n)],
                                           inc=(ts if (pi == 1 and jc == nic - 1) else None))
                            if ig == 0:
                                p.emit("scalar", lambda E, fc=fc, pb=pb: E.copy(
                                    y2[:, fc * HT:(fc + 1) * HT], pb[:]),
                                    waits=[ts, (st, st.n)], inc=ss)
                            else:
                                p.emit("vector", lambda E, fc=fc, pb=pb: E.tensor_tensor(
                                    y2[:, fc * HT:(fc + 1) * HT], y2[:, fc * HT:(fc + 1) * HT],
                                    pb[:], ALU.add), waits=[ts], inc=vs)
                        slab_free[sd] = [(ts, ts.n)]
                    for fc in range(NCH):
                        p.emit("vector", lambda E, fc=fc: E.tensor_scalar(
                            y2[:, fc * HT:(fc + 1) * HT], y2[:, fc * HT:(fc + 1) * HT],
                            dsc[:, fc:fc + 1], None, ALU.mult), waits=[(ss, ss.n)], inc=vs)
                    with ExitStack() as sctx:
                        ps_s1 = sctx.enter_context(nc.psum_tensor(f"ps4s1{hh}", [1, 512], F32))
                        ps_s2 = sctx.enter_context(nc.psum_tensor(f"ps4s2{hh}", [1, 512], F32))
                        for j in range(NCH):
                            sqb = sqf[j % 2]
                            p.emit("scalar", lambda E, j=j, sqb=sqb: E.square(
                                sqb[:, 0:HT], y2[:, j * HT:(j + 1) * HT]),
                                waits=[(vs, vs.n), (ts, ts.n)], inc=ss)
                            p.emit("tensor", lambda E, j=j: E.matmul(
                                ps_s1[:], onescol[:], y2[:, j * HT:(j + 1) * HT],
                                start=(j == 0), stop=(j == NCH - 1)),
                                waits=[(vs, vs.n), ss], inc=None)
                            p.emit("tensor", lambda E, j=j, sqb=sqb: E.matmul(
                                ps_s2[:], onescol[:], sqb[:, 0:HT],
                                start=(j == 0), stop=(j == NCH - 1)), waits=[ss], inc=ts)
                        p.emit("scalar", lambda E: E.copy(row(0, HT), ps_s1[:]),
                               waits=[ts], inc=ss)
                        p.emit("scalar", lambda E: E.copy(row(1, HT), ps_s2[:]),
                               waits=[ts], inc=ss)
                        p.flush()
                    ln_rows_and_bcast(HT, EPS_LN)
                    for j in range(NCH):
                        lv = dma("sync", x1t[j % 2][:, 0:HT],
                                 x1_scr3[:, j, hh * HT:(hh + 1) * HT], ld,
                                 waits=x1_free[j % 2])
                        p.emit("vector", lambda E, j=j: E.tensor_tensor(
                            y2[:, j * HT:(j + 1) * HT], y2[:, j * HT:(j + 1) * HT],
                            bc1[:, 0:HT], ALU.mult), waits=[ss], inc=vs)
                        p.emit("vector", lambda E, j=j: E.tensor_tensor(
                            y2[:, j * HT:(j + 1) * HT], y2[:, j * HT:(j + 1) * HT],
                            bc2[:, 0:HT], ALU.subtract), waits=[], inc=vs)
                        p.emit("vector", lambda E, j=j: E.tensor_tensor(
                            y2[:, j * HT:(j + 1) * HT], y2[:, j * HT:(j + 1) * HT],
                            x1t[j % 2][:, 0:HT], ALU.add), waits=[lv], inc=vs)
                        dma("sync", OUTT3[:, j, hh * HT:(hh + 1) * HT],
                            y2[:, j * HT:(j + 1) * HT], st, waits=[(vs, vs.n)])
                        x1_free[j % 2] = [(vs, vs.n)]
                    p.flush()

        p.emit("sync", lambda E: E.nop(), waits=[(st, st.n)], inc=None)
        p.flush()
        return nc, outs


def stage_inputs(inputs):
    x = np.ascontiguousarray(np.asarray(inputs["L0_attn_hidden_states"], np.float32))

    def pan(M, csz):
        K, N = M.shape
        t = M.reshape(K // HD, HD, N // csz, csz).transpose(2, 1, 0, 3)
        return np.ascontiguousarray(t.reshape(N // csz * HD, K // HD * csz))

    wqkv = np.asarray(inputs["L0_attn_w_qkv"], np.float32).reshape(NH, 3, HD, DM)
    QTW = pan(wqkv[:, 0].reshape(DM, DM).T, HD)
    KTW = pan(wqkv[:, 1].reshape(DM, DM).T, HD)
    VTW = pan(wqkv[:, 2].reshape(DM, DM).T, HD)
    WGT = pan(np.asarray(inputs["L0_attn_w_gate"], np.float32).T, HD)
    WOT = pan(np.asarray(inputs["L0_attn_w_out"], np.float32).T, HD)
    GW = pan(np.asarray(inputs["L0_ffn_gate_w"], np.float32), IGRP)
    UW = pan(np.asarray(inputs["L0_ffn_up_w"], np.float32), IGRP)
    dw_ = np.asarray(inputs["L0_ffn_down_w"], np.float32)
    DW = np.ascontiguousarray(dw_.reshape(NIG, IGRP // HD, HD, DM).transpose(0, 2, 1, 3)
                              .reshape(NIG * HD, (IGRP // HD) * DM))
    SLOPES = np.asarray(inputs["L0_attn_slope_rate"], np.float32).reshape(1, NH)
    NWC = np.ascontiguousarray(np.asarray(inputs["L0_attn_norm_w"], np.float32)
                               .reshape(NCH, HD).T)
    GSC = np.ascontiguousarray(np.asarray(inputs["L0_ffn_gate_scale"], np.float32)
                               .reshape(FI // HD, HD).T)
    USC = np.ascontiguousarray(np.asarray(inputs["L0_ffn_up_scale"], np.float32)
                               .reshape(FI // HD, HD).T)
    DSC = np.ascontiguousarray(np.asarray(inputs["L0_ffn_down_scale"], np.float32)
                               .reshape(NCH, HD).T)
    IDENT = np.eye(HD, dtype=np.float32)
    m = np.arange(BLK, dtype=np.float32)
    IDXT = np.where(m[None, :] >= m[:, None], m[None, :] - m[:, None],
                    np.float32(BIG)).astype(np.float32)
    ARRQ = np.arange(1, BLK + 1, dtype=np.float32).reshape(1, BLK)
    ARRK = (BLK - np.arange(1, BLK + 1, dtype=np.float32)).reshape(1, BLK)
    ONESROW = np.ones((1, HD), np.float32)
    ONESCOL = np.ones((HD, 1), np.float32)

    shared = dict(qtw=QTW, ktw=KTW, vtw=VTW, wgt=WGT, wot=WOT, gw=GW, uw=UW, dw=DW,
                  slopes=SLOPES, nwc=NWC, gsc=GSC, usc=USC, dsc=DSC, ident=IDENT,
                  idxt=IDXT, arrq=ARRQ, arrk=ARRK, onesrow=ONESROW, onescol=ONESCOL)
    in_maps = []
    for c in range(NCORES):
        b, pp = c // 4, c % 4
        xt = np.ascontiguousarray(x[b, pp * TPC:(pp + 1) * TPC, :].T)
        expo = np.full((1, NCORES), BIG, np.float32)
        for cp in range(NCORES):
            if cp // 4 == b and cp % 4 < pp:
                expo[0, cp] = np.float32(pp - 1 - (cp % 4))
        in_maps.append(dict(shared, xt=xt, expo=expo))
    return in_maps


def run(inputs, cfg):
    nc, outs = build(cfg)
    in_maps = stage_inputs(inputs)
    res = run_bass_kernel_spmd(nc, in_maps, core_ids=list(range(NCORES)))
    return res, outs


def kernel(**inputs):
    res, outs = run(inputs, {"stage": "full", "debug": False})
    out = np.empty((2, 4096, DM), np.float32)
    for c in range(NCORES):
        b, pp = c // 4, c % 4
        out[b, pp * TPC:(pp + 1) * TPC, :] = np.asarray(res.results[c]["outt"]).T
    return out


# version 09
# speedup vs baseline: 1.1674x; 1.1674x over previous
import numpy as np
from contextlib import ExitStack

import concourse.bass as bass
import concourse.mybir as mybir
from concourse.bass_utils import run_bass_kernel_spmd

F32 = mybir.dt.float32
BF16 = mybir.dt.bfloat16
FP8 = mybir.dt.float8e4
AF = mybir.ActivationFunctionType
ALU = mybir.AluOpType

NCORES = 8
DM = 2048
NCH = 16
NH = 16
HD = 128
BLK = 256
TPC = 1024
NBLK = 4
FI = 6144
EPS_LN = 1e-5
EPS_RMS = 1e-6
IGRP = 512
NIG = FI // IGRP
BIG = 1.0e9
HT = 512

ENGS = ("sync", "gpsimd", "tensor", "scalar", "vector")


class Sem:
    def __init__(self, handle, owner=None):
        self.h = handle
        self.n = 0
        self.owner = owner

    def inc(self, d=1):
        self.n += d


class Prog:

    def __init__(self, nc, block):
        self.nc = nc
        self.block = block
        self.q = {e: [] for e in ENGS}
        self.lastw = {e: {} for e in ENGS}

    def emit(self, eng, fn, waits=(), inc=None, inc_by=1):
        ws = []
        for w in waits:
            if isinstance(w, Sem):
                s, v = w, w.n
            else:
                s, v = w
            if v <= 0:
                continue
            if s.owner == eng and s.owner in ("tensor", "scalar", "vector"):
                continue
            prev = self.lastw[eng].get(s, 0)
            if v > prev:
                self.lastw[eng][s] = v
                ws.append((s.h, v))
        incs = []
        if inc is not None:
            inc.inc(inc_by)
            incs.append((inc.h, inc_by))
        self.q[eng].append((ws, fn, incs))

    def flush(self):
        for e in ENGS:
            items = self.q[e]
            if not items:
                continue
            self.q[e] = []

            def body(engine, items=items):
                for ws, fn, incs in items:
                    for (h, v) in ws:
                        engine.wait_ge(h, v)
                    inst = fn(engine)
                    for (h, d) in incs:
                        inst.then_inc(h, d)
            getattr(self.block, e)(body)


def build(cfg):
    stage = cfg.get("stage", "full")
    upto = cfg.get("upto", None)
    nheads = cfg.get("nheads", NH)
    dbg = cfg.get("debug", False)
    nc = bass.Bass("TRN2", target_bir_lowering=False, debug=False,
                   detect_race_conditions=False)

    def din(name, shape):
        return nc.dram_tensor(name, shape, F32, kind="ExternalInput").ap()

    XT = din("xt", [DM, TPC])
    QTW = din("qtw", [NCH * HD, DM])
    KTW = din("ktw", [NCH * HD, DM])
    VTW = din("vtw", [NCH * HD, DM])
    WGT = din("wgt", [NCH * HD, DM])
    WOT = din("wot", [NCH * HD, DM])
    GW = din("gw", [NIG * HD, NCH * IGRP])
    UW = din("uw", [NIG * HD, NCH * IGRP])
    DW = din("dw", [NIG * HD, (IGRP // HD) * DM])
    SLOPES = din("slopes", [1, NH])
    EXPO = din("expo", [1, NCORES])
    NWC = din("nwc", [HD, NCH])
    GSC = din("gsc", [HD, FI // HD])
    USC = din("usc", [HD, FI // HD])
    DSC = din("dsc", [HD, NCH])
    IDENT = din("ident", [HD, HD])
    IDXT = din("idxt", [BLK, BLK])
    ARRQ = din("arrq", [1, BLK])
    ARRK = din("arrk", [1, BLK])
    ONESROW = din("onesrow", [1, HD])
    ONESCOL = din("onescol", [HD, 1])

    outs = {}

    def dout(name, shape):
        outs[name] = nc.dram_tensor(name, shape, F32, kind="ExternalOutput").ap()
        return outs[name]

    def scr(name, shape, debug_out=False):
        if debug_out and dbg:
            return dout(name, shape)
        return nc.dram_tensor(name, shape, F32, kind="Internal").ap()

    kfm_scr = scr("kfm_scr", [DM, TPC], debug_out=True)
    vtm_scr = scr("vtm_scr", [TPC, DM], debug_out=True)
    x1_scr = (dout("x1_scr", [DM, TPC]) if stage != "full"
              else nc.dram_tensor("x1_scr", [DM, TPC], F32, kind="Internal").ap())
    ag_in = nc.dram_tensor("ag_in", [NH * HD, HD], F32, kind="Internal").ap()
    ag_out = nc.dram_tensor("ag_out", [NCORES * NH * HD, HD], F32, kind="Internal").ap()
    if dbg:
        KV0 = dout("kv0_dbg", [HD, NH * HD])
        ODBG = dout("o_dbg", [HD, NH * TPC])
        GODBG = dout("go_dbg", [HD, NH * TPC])
        YDBG = dout("y_dbg", [HD, NCH * TPC])
    if stage == "full":
        OUTT = dout("outt", [DM, TPC])

    with ExitStack() as ctx:
        def sb(name, shape, dtype=F32):
            return ctx.enter_context(nc.sbuf_tensor("s_" + name, shape, dtype))

        def mksem(name, owner=None):
            return Sem(ctx.enter_context(nc.semaphore(name)), owner)

        ld = mksem("ld", "sync")
        gp = mksem("gp", "gpsimd")
        ts = mksem("ts", "tensor")
        ss = mksem("ss", "scalar")
        vs = mksem("vs", "vector")
        cc = mksem("cc", "gpsimd")
        st = mksem("st", "sync")

        ident = sb("ident", [HD, HD])
        idxt = sb("idxt", [HD, 2 * BLK])
        arrq = sb("arrq", [1, BLK])
        arrk = sb("arrk", [1, BLK])
        onesrow = sb("onesrow", [1, HD])
        onescol = sb("onescol", [HD, 1])
        slopes = sb("slopes", [1, NH])
        expo = sb("expo", [1, NCORES])
        nwc = sb("nwc", [HD, NCH])
        gsc = sb("gsc", [HD, FI // HD])
        usc = sb("usc", [HD, FI // HD])
        dsc = sb("dsc", [HD, NCH])
        sb16 = sb("sb16", [HD, NH])
        sbneg = sb("sbneg", [HD, NH])
        bdt = sb("bdt", [HD, NH])
        kdcol = sb("kdcol", [HD, NH * 2])
        wcomb = sb("wcomb", [HD, NCORES * NH])
        kvst = sb("kvst", [HD, NH * HD])
        rows = sb("rows", [1, 4 * TPC])
        rows2 = sb("rows2", [1, 2 * TPC])
        bc1 = sb("bc1", [HD, TPC])
        bc2 = sb("bc2", [HD, TPC])

        block = ctx.enter_context(nc.Block())
        p = Prog(nc, block)

        def dma(eng, out, in_, sem, waits=()):
            p.emit(eng, lambda E, o=out, i=in_: E.dma_start(out=o, in_=i),
                   waits=waits, inc=sem, inc_by=16)
            return (sem, sem.n)

        def row(k, n=TPC):
            return rows[0:1, k * TPC:k * TPC + n]

        def bar():
            return [(ld, ld.n), (gp, gp.n), (ts, ts.n), (ss, ss.n), (vs, vs.n), (cc, cc.n)]

        def mm3(pb, wh, wl, xh_, xl_, tt, T, first_waits):
            for pi, (wp_, xp_) in enumerate(((wh, xh_), (wh, xl_), (wl, xh_))):
                for j in range(NCH):
                    p.emit("tensor",
                           lambda E, pi=pi, wp_=wp_, xp_=xp_, j=j, tt=tt, pb=pb:
                           E.matmul(
                               pb[:], wp_[:, j * HD:(j + 1) * HD],
                               xp_[:, j * T + tt * 512:j * T + (tt + 1) * 512],
                               start=(pi == 0 and j == 0),
                               stop=(pi == 2 and j == NCH - 1)),
                           waits=first_waits if (pi == 0 and j == 0) else [],
                           inc=(ts if (pi == 2 and j == NCH - 1) else None))

        def split_tile(fp, bh, bl, waits):
            p.emit("vector", lambda E: E.tensor_copy(bh, fp), waits=waits, inc=vs)
            p.emit("vector", lambda E: E.tensor_tensor(bl, fp, bh, ALU.subtract),
                   waits=[], inc=vs)
            return (vs, vs.n)

        _lrc = [0]

        def ln_rows_and_bcast(T, eps, rms=False, upto2=None):
            upto2 = upto2 or cfg.get("upto2")
            inv_dm = 1.0 / DM
            if not rms:
                p.emit("vector", lambda E: E.tensor_scalar(row(0, T), row(0, T), inv_dm,
                                                           None, ALU.mult), waits=[ss], inc=vs)
            p.emit("vector", lambda E: E.tensor_scalar(row(1, T), row(1, T), inv_dm,
                                                       None, ALU.mult), waits=[ss], inc=vs)
            if not rms:
                p.emit("vector", lambda E: E.tensor_tensor(row(2, T), row(0, T), row(0, T),
                                                           ALU.mult), waits=[], inc=vs)
                p.emit("vector", lambda E: E.tensor_tensor(row(1, T), row(1, T), row(2, T),
                                                           ALU.subtract), waits=[], inc=vs)
            p.emit("vector", lambda E: E.tensor_scalar(row(1, T), row(1, T), float(eps),
                                                       None, ALU.add), waits=[], inc=vs)
            if upto2 == "lr1":
                return
            p.emit("scalar", lambda E: E.activation(row(1, T), row(1, T), AF.Sqrt),
                   waits=[vs], inc=ss)
            p.emit("vector", lambda E: E.reciprocal(row(2, T), row(1, T)), waits=[ss], inc=vs)
            if upto2 == "lr2":
                return
            if not rms:
                p.emit("vector", lambda E: E.tensor_tensor(row(3, T), row(0, T), row(2, T),
                                                           ALU.mult), waits=[], inc=vs)
            if upto2 == "lr3":
                return
            _lrc[0] += 1
            with nc.psum_tensor(f"ps_bcr{_lrc[0]}", [HD, 512], F32) as ps_b:
                for u in range((T + 511) // 512):
                    w = min(512, T - u * 512)
                    p.emit("tensor", lambda E, u=u, w=w: E.matmul(
                        ps_b[:, 0:w], onesrow[:], row(2, T)[0:1, u * 512:u * 512 + w],
                        start=True, stop=True), waits=[vs, ss], inc=ts)
                    if upto2 == "lrb1":
                        continue
                    p.emit("scalar", lambda E, u=u, w=w: E.copy(
                        bc1[:, u * 512:u * 512 + w], ps_b[:, 0:w]), waits=[ts], inc=ss)
                    if upto2 == "lrb2" or rms:
                        continue
                    p.emit("tensor", lambda E, u=u, w=w: E.matmul(
                        ps_b[:, 0:w], onesrow[:], row(3, T)[0:1, u * 512:u * 512 + w],
                        start=True, stop=True), waits=[ss], inc=ts)
                    if upto2 == "lrb3":
                        continue
                    p.emit("scalar", lambda E, u=u, w=w: E.copy(
                        bc2[:, u * 512:u * 512 + w], ps_b[:, 0:w]), waits=[ts], inc=ss)
                p.flush()

        for (t, d) in [(ident, IDENT), (arrq, ARRQ), (arrk, ARRK),
                       (onesrow, ONESROW), (onescol, ONESCOL), (slopes, SLOPES),
                       (expo, EXPO), (nwc, NWC), (gsc, GSC), (usc, USC), (dsc, DSC)]:
            dma("sync", t[:], d, ld)
        for j in range(2):
            dma("sync", idxt[:, j * BLK:(j + 1) * BLK], IDXT[j * HD:(j + 1) * HD, :], ld)
        p0_ld = (ld, ld.n)

        with nc.psum_tensor("ps_bc0", [HD, 512], F32) as ps_bc:
            p.emit("tensor", lambda E: E.matmul(ps_bc[:, 0:NH], onesrow[:], slopes[:],
                                                start=True, stop=True),
                   waits=[p0_ld], inc=ts)
            p.emit("scalar", lambda E: E.copy(sb16[:], ps_bc[:, 0:NH]), waits=[ts], inc=ss)
            p.emit("vector", lambda E: E.tensor_scalar(sbneg[:], sb16[:], -1.0, None, ALU.mult),
                   waits=[ss], inc=vs)
            p.emit("scalar", lambda E: E.activation(bdt[:], sb16[:], AF.Exp, scale=-float(BLK)),
                   waits=[vs], inc=ss)
            p.emit("tensor", lambda E: E.matmul(ps_bc[:, 32:32 + NCORES], onesrow[:], expo[:],
                                                start=True, stop=True),
                   waits=[(ss, ss.n)], inc=ts)
            p.emit("scalar", lambda E: E.copy(bc1[:, 0:NCORES], ps_bc[:, 32:32 + NCORES]),
                   waits=[ts], inc=ss)
            for c in range(NCORES):
                p.emit("vector",
                       lambda E, c=c: E.tensor_scalar(bc2[:, 0:NH], sb16[:], -1024.0,
                                                      bc1[:, c:c + 1], ALU.mult, ALU.mult),
                       waits=[ss], inc=vs)
                p.emit("scalar",
                       lambda E, c=c: E.activation(wcomb[:, c * NH:(c + 1) * NH],
                                                   bc2[:, 0:NH], AF.Exp),
                       waits=[vs], inc=ss)
            with nc.psum_tensor("ps_tr0", [HD, HD], F32) as ps_tr:
                for h in range(NH):
                    p.emit("scalar",
                           lambda E, h=h: E.activation(row(0, BLK), arrk[:], AF.Exp,
                                                       scale=sbneg[0:1, h:h + 1]),
                           waits=[ts, vs], inc=ss)
                    p.emit("tensor",
                           lambda E: E.matmul(ps_bc[:, 0:BLK], onesrow[:], row(0, BLK),
                                              start=True, stop=True),
                           waits=[ss], inc=ts)
                    p.emit("scalar",
                           lambda E: E.copy(bc2[:, 0:BLK], ps_bc[:, 0:BLK]),
                           waits=[ts], inc=ss)
                    for j in range(2):
                        p.emit("tensor",
                               lambda E, j=j: E.transpose(ps_tr[:], bc2[:, j * HD:(j + 1) * HD],
                                                          ident[:]),
                               waits=[ss, p0_ld], inc=ts)
                        p.emit("scalar",
                               lambda E, h=h, j=j: E.copy(kdcol[:, h * 2 + j:h * 2 + j + 1],
                                                          ps_tr[:, 0:1]),
                               waits=[ts], inc=ss)
                p.flush()

        XT3 = XT.rearrange("(j p) t -> p j t", p=HD)
        kfm_scr3 = kfm_scr.rearrange("(h p) t -> p h t", p=HD)
        x1_scr3 = x1_scr.rearrange("(j p) t -> p j t", p=HD)

        def early_out():
            p.emit("sync", lambda E: E.nop(), waits=bar(), inc=None)
            p.flush()
            return nc, outs

        if upto == "p0":
            return early_out()

        with ExitStack() as p1ctx:
            p1sb = lambda n, s, dt=F32: p1ctx.enter_context(nc.sbuf_tensor("p1_" + n, s, dt))
            xlnh = p1sb("xlnh", [HD, NCH * TPC], BF16)
            xlnl = p1sb("xlnl", [HD, NCH * TPC], BF16)
            lntmp = p1sb("lntmp", [HD, TPC])
            xa = [p1sb(f"xa{i}", [HD, TPC]) for i in range(2)]
            sq = [p1sb(f"sq{i}", [HD, TPC]) for i in range(2)]
            kstage = p1sb("kstage", [HD, TPC])
            vstage = p1sb("vstage", [HD, TPC])
            vtstage = p1sb("vtstage", [HD, 8 * HD])
            ktm = p1sb("ktm", [HD, BLK])
            kpan = [p1sb(f"kpan{i}", [HD, NCH * HD]) for i in range(2)]
            vpan = [p1sb(f"vpan{i}", [HD, NCH * HD]) for i in range(2)]
            kbh = [p1sb(f"kbh{i}", [HD, NCH * HD], BF16) for i in range(2)]
            kbl = [p1sb(f"kbl{i}", [HD, NCH * HD], BF16) for i in range(2)]
            vbh = [p1sb(f"vbh{i}", [HD, NCH * HD], BF16) for i in range(2)]
            vbl = [p1sb(f"vbl{i}", [HD, NCH * HD], BF16) for i in range(2)]

            xa_free = [bar(), bar()]
            with ExitStack() as sctx:
                ps_s1 = [sctx.enter_context(nc.psum_tensor(f"ps_s1{u}", [1, 512], F32))
                         for u in range(2)]
                ps_s2 = [sctx.enter_context(nc.psum_tensor(f"ps_s2{u}", [1, 512], F32))
                         for u in range(2)]
                for j in range(NCH):
                    lv = dma("sync", xa[j % 2][:], XT3[:, j, :], ld, waits=xa_free[j % 2])
                    sqb = sq[j % 2]
                    p.emit("scalar", lambda E, j=j, sqb=sqb: E.square(sqb[:], xa[j % 2][:]),
                           waits=[lv, (ts, ts.n)], inc=ss)
                    for u in range(2):
                        p.emit("tensor", lambda E, j=j, u=u: E.matmul(
                            ps_s1[u][:], onescol[:], xa[j % 2][:, u * 512:(u + 1) * 512],
                            start=(j == 0), stop=(j == NCH - 1)),
                            waits=[lv, ss], inc=None)
                        p.emit("tensor", lambda E, j=j, u=u, sqb=sqb: E.matmul(
                            ps_s2[u][:], onescol[:], sqb[:, u * 512:(u + 1) * 512],
                            start=(j == 0), stop=(j == NCH - 1)),
                            waits=[ss], inc=(ts if u == 1 else None))
                    xa_free[j % 2] = [(ts, ts.n)]
                for u in range(2):
                    p.emit("scalar", lambda E, u=u: E.copy(row(0)[0:1, u * 512:(u + 1) * 512],
                                                           ps_s1[u][:]), waits=[ts], inc=ss)
                    p.emit("scalar", lambda E, u=u: E.copy(row(1)[0:1, u * 512:(u + 1) * 512],
                                                           ps_s2[u][:]), waits=[ts], inc=ss)
                p.flush()
            if upto == "stats0":
                return early_out()
            ln_rows_and_bcast(TPC, EPS_LN)
            p.emit("vector", lambda E: E.tensor_copy(rows2[0:1, 0:TPC], row(0)),
                   waits=[ss], inc=vs)
            p.emit("vector", lambda E: E.tensor_copy(rows2[0:1, TPC:2 * TPC], row(2)),
                   waits=[], inc=vs)
            if upto == "stats":
                return early_out()
            xa_free = [[(ts, ts.n)], [(ts, ts.n)]]
            for j in range(NCH):
                lv = dma("sync", xa[j % 2][:], XT3[:, j, :], ld, waits=xa_free[j % 2])
                p.emit("vector", lambda E, j=j: E.tensor_tensor(
                    lntmp[:], xa[j % 2][:], bc1[:], ALU.mult),
                    waits=[lv, ss], inc=vs)
                p.emit("vector", lambda E, j=j: E.tensor_tensor(
                    lntmp[:], lntmp[:], bc2[:], ALU.subtract), waits=[], inc=vs)
                p.emit("vector", lambda E, j=j: E.tensor_copy(
                    xlnh[:, j * TPC:(j + 1) * TPC], lntmp[:]), waits=[], inc=vs)
                p.emit("vector", lambda E, j=j: E.tensor_tensor(
                    xlnl[:, j * TPC:(j + 1) * TPC], lntmp[:],
                    xlnh[:, j * TPC:(j + 1) * TPC], ALU.subtract), waits=[], inc=vs)
                xa_free[j % 2] = [(vs, vs.n)]
            if upto == "xln":
                return early_out()


            p.emit("vector", lambda E: E.memset(kvst[:], 0.0), waits=[], inc=vs)
            with ExitStack() as dctx:
                ps_k = [dctx.enter_context(nc.psum_tensor(f"ps_k{i}", [HD, 512], F32))
                        for i in range(2)]
                ps_tr = [dctx.enter_context(nc.psum_tensor(f"ps_ktr{i}", [HD, HD], F32))
                         for i in range(2)]
                ps_w = dctx.enter_context(nc.psum_tensor("ps_w", [HD, HD], F32))
                kpan_free = [[(ts, ts.n)], [(ts, ts.n)]]
                vpan_free = [[(ts, ts.n)], [(ts, ts.n)]]
                kst_free = [(gp, gp.n), (ts, ts.n)]
                vst_free = [(ts, ts.n)]
                vts_free = [(gp, gp.n), (ts, ts.n)]
                xlnw = (vs, vs.n)
                kb_free = [bar(), bar()]
                vb_free = [bar(), bar()]
                psk_free = [(ss, ss.n), (ss, ss.n)]
                for h in range(nheads):
                    kp, vp = kpan[h % 2], vpan[h % 2]
                    lvk = dma("sync", kp[:], KTW[h * HD:(h + 1) * HD, :], ld,
                              waits=kpan_free[h % 2])
                    lvv = dma("sync", vp[:], VTW[h * HD:(h + 1) * HD, :], ld,
                              waits=vpan_free[h % 2])
                    kspl = split_tile(kpan[h % 2][:], kbh[h % 2][:], kbl[h % 2][:],
                                      [lvk, *kb_free[h % 2]])
                    vspl = split_tile(vpan[h % 2][:], vbh[h % 2][:], vbl[h % 2][:],
                                      [lvv, *vb_free[h % 2]])
                    kpan_free[h % 2] = [(vs, kspl[1])]
                    vpan_free[h % 2] = [(vs, vspl[1])]
                    for tt in range(2):
                        pb = ps_k[tt]
                        mm3(pb, kbh[h % 2], kbl[h % 2], xlnh, xlnl, tt, TPC,
                            [kspl, xlnw, psk_free[tt]])
                        p.emit("scalar", lambda E, tt=tt, pb=pb: E.activation(
                            kstage[:, tt * 512:(tt + 1) * 512], pb[:], AF.Silu),
                            waits=[ts, *kst_free], inc=ss)
                        psk_free[tt] = (ss, ss.n)
                    kb_free[h % 2] = [(ts, ts.n)]
                    ksp = dma("gpsimd", kfm_scr3[:, h, :], kstage[:], gp, waits=[(ss, ss.n)])
                    for tt in range(2):
                        pb = ps_k[tt]
                        mm3(pb, vbh[h % 2], vbl[h % 2], xlnh, xlnl, tt, TPC,
                            [vspl, psk_free[tt]])
                        p.emit("scalar", lambda E, tt=tt, pb=pb: E.activation(
                            vstage[:, tt * 512:(tt + 1) * 512], pb[:], AF.Silu),
                            waits=[ts, *vst_free], inc=ss)
                        psk_free[tt] = (ss, ss.n)
                    vb_free[h % 2] = [(ts, ts.n)]
                    vst_free = [(ts, ts.n)]
                    for tc in range(8):
                        pt = ps_tr[tc % 2]
                        p.emit("tensor", lambda E, tc=tc, pt=pt: E.transpose(
                            pt[:], vstage[:, tc * HD:(tc + 1) * HD], ident[:]),
                            waits=[(ss, ss.n), *vts_free], inc=ts)
                        p.emit("scalar", lambda E, tc=tc, pt=pt: E.copy(
                            vtstage[:, tc * HD:(tc + 1) * HD], pt[:]),
                            waits=[ts] + (list(vts_free) if tc == 0 else []), inc=ss)
                    vst_free = [(ts, ts.n)]
                    vsp = dma("gpsimd",
                              vtm_scr[:, h * HD:(h + 1) * HD].rearrange(
                                  "(tc p) e -> p tc e", p=HD),
                              vtstage[:].rearrange("p (tc e) -> p tc e", tc=8),
                              gp, waits=[(ss, ss.n)])
                    for b in range(NBLK):
                        for j in range(2):
                            p.emit("tensor", lambda E, b=b, j=j: E.transpose(
                                ps_tr[j][:], kstage[:, b * BLK + j * HD:b * BLK + (j + 1) * HD],
                                ident[:]), waits=[(ss, ss.n), (vs, vs.n)], inc=ts)
                            p.emit("scalar", lambda E, j=j: E.copy(
                                ktm[:, j * HD:(j + 1) * HD], ps_tr[j][:]), waits=[ts], inc=ss)
                            p.emit("vector", lambda E, h=h, j=j: E.tensor_scalar(
                                ktm[:, j * HD:(j + 1) * HD], ktm[:, j * HD:(j + 1) * HD],
                                kdcol[:, h * 2 + j:h * 2 + j + 1], None, ALU.mult),
                                waits=[ss], inc=vs)
                        for j in range(2):
                            p.emit("tensor", lambda E, h=h, b=b, j=j: E.matmul(
                                ps_w[:], ktm[:, j * HD:(j + 1) * HD],
                                vtstage[:, (2 * b + j) * HD:(2 * b + j + 1) * HD],
                                start=(j == 0), stop=(j == 1)),
                                waits=[vs], inc=(ts if j == 1 else None))
                        p.emit("vector", lambda E, h=h: E.tensor_scalar(
                            kvst[:, h * HD:(h + 1) * HD], kvst[:, h * HD:(h + 1) * HD],
                            bdt[:, h:h + 1], None, ALU.mult), waits=[], inc=vs)
                        p.emit("vector", lambda E, h=h: E.tensor_tensor(
                            kvst[:, h * HD:(h + 1) * HD], kvst[:, h * HD:(h + 1) * HD],
                            ps_w[:], ALU.add), waits=[ts], inc=vs)
                    kst_free = [(gp, ksp[1]), (ts, ts.n)]
                    vts_free = [(gp, vsp[1]), (vs, vs.n), (ts, ts.n)]
                p.flush()

            if upto == "noag":
                return early_out()
            ag_in3 = ag_in.rearrange("(h d) e -> d h e", d=HD)
            agv = dma("sync", ag_in3, kvst[:].rearrange("d (h e) -> d h e", h=NH), ld,
                      waits=[(vs, vs.n)])
            p.emit("gpsimd", lambda E: E.collective_compute(
                "AllGather", ALU.bypass, replica_groups=[list(range(NCORES))],
                ins=[ag_in.opt()], outs=[ag_out.opt()]),
                waits=[agv], inc=cc)
            p.flush()

        def emit_kv_init(slab_buf=None):
            with ExitStack() as ectx:
                s0 = (slab_buf if slab_buf is not None else
                      ectx.enter_context(nc.sbuf_tensor("sslab0", [HD, NH * HD], F32)))
                sslab = [s0, s0]
                p.emit("vector", lambda E: E.memset(kvst[:], 0.0), waits=[(cc, cc.n)], inc=vs)
                slab_free = bar()
                for c in range(NCORES):
                    srcap = ag_out[c * NH * HD:(c + 1) * NH * HD, :].rearrange(
                        "(h d) e -> d h e", d=HD)
                    lv = dma("sync", sslab[0][:].rearrange("d (h e) -> d h e", h=NH),
                             srcap, ld, waits=[(cc, cc.n), *slab_free])
                    for h in range(NH):
                        p.emit("vector", lambda E, c=c, h=h: E.scalar_tensor_tensor(
                            kvst[:, h * HD:(h + 1) * HD], sslab[0][:, h * HD:(h + 1) * HD],
                            wcomb[:, c * NH + h:c * NH + h + 1], kvst[:, h * HD:(h + 1) * HD],
                            ALU.mult, ALU.add), waits=[lv], inc=vs)
                    slab_free = [(vs, vs.n)]
                if dbg:
                    dma("gpsimd", KV0, kvst[:], gp, waits=[(vs, vs.n)])
                p.flush()

        if stage == "p1":
            emit_kv_init()
            p.emit("sync", lambda E: E.nop(), waits=bar(), inc=None)
            p.flush()
            return nc, outs

        state_done = {}
        for hh in range(2):
            with ExitStack() as hctx:
                hsb = lambda name, shape: hctx.enter_context(
                    nc.sbuf_tensor(f"h{hh}_{name}", shape, F32))
                xha = [hsb(f"xha{i}", [HD, HT]) for i in range(2)]
                hbsb = lambda name, shape: hctx.enter_context(
                    nc.sbuf_tensor(f"h{hh}_{name}", shape, BF16))
                xyh = hbsb("xyh", [HD, NCH * HT])
                xyl = hbsb("xyl", [HD, NCH * HT])
                lnt2 = hsb("lnt2", [HD, HT])
                qbuf = hsb("qbuf", [HD, NH * HT])
                obuf = hsb("obuf", [HD, NH * HT])
                _sqh0 = hsb("sqh0", [HD, HT])
                sqh = [_sqh0, _sqh0]
                pan = [hsb(f"pan{i}", [HD, NCH * HD]) for i in range(2)]
                pbh = [hbsb(f"pbh{i}", [HD, NCH * HD]) for i in range(2)]
                pbl = [hbsb(f"pbl{i}", [HD, NCH * HD]) for i in range(2)]
                kre2 = [hsb(f"kre2_{i}", [HD, 2 * BLK]) for i in range(2)]
                vre2 = [hsb(f"vre2_{i}", [HD, 2 * BLK]) for i in range(2)]
                ktm2 = [hsb(f"ktm2_{i}", [HD, BLK]) for i in range(2)]
                qq2 = [hsb(f"qq2_{i}", [HD, BLK]) for i in range(2)]
                sdt2 = [hsb(f"sdt2_{i}", [HD, 2 * BLK]) for i in range(2)]
                dtt2 = [hsb(f"dtt2_{i}", [HD, 2 * BLK]) for i in range(2)]
                qdb2 = [hsb(f"qdb2_{i}", [HD, BLK]) for i in range(2)]
                gtile = hsb("gtile", [HD, HT])

                xa_free2 = [bar(), bar()]
                p.emit("vector", lambda E: E.tensor_tensor(
                    row(3, HT), rows2[0:1, hh * HT:(hh + 1) * HT],
                    rows2[0:1, TPC + hh * HT:TPC + (hh + 1) * HT], ALU.mult),
                    waits=bar(), inc=vs)
                with nc.psum_tensor(f"ps_bx{hh}", [HD, 512], F32) as ps_b:
                    p.emit("tensor", lambda E: E.matmul(
                        ps_b[:], onesrow[:], rows2[0:1, TPC + hh * HT:TPC + (hh + 1) * HT],
                        start=True, stop=True), waits=[vs, ss], inc=ts)
                    p.emit("scalar", lambda E: E.copy(bc1[:, 0:HT], ps_b[:]), waits=[ts], inc=ss)
                    p.emit("tensor", lambda E: E.matmul(
                        ps_b[:], onesrow[:], row(3, HT), start=True, stop=True),
                        waits=[vs, ss], inc=ts)
                    p.emit("scalar", lambda E: E.copy(bc2[:, 0:HT], ps_b[:]), waits=[ts], inc=ss)
                    p.flush()
                for j in range(NCH):
                    lv = dma("sync", xha[j % 2][:], XT3[:, j, hh * HT:(hh + 1) * HT],
                             ld, waits=xa_free2[j % 2])
                    p.emit("vector", lambda E, j=j: E.tensor_tensor(
                        lnt2[:], xha[j % 2][:],
                        bc1[:, 0:HT], ALU.mult), waits=[lv, ss], inc=vs)
                    p.emit("vector", lambda E, j=j: E.tensor_tensor(
                        lnt2[:], lnt2[:], bc2[:, 0:HT], ALU.subtract), waits=[], inc=vs)
                    split_tile(lnt2[:], xyh[:, j * HT:(j + 1) * HT],
                               xyl[:, j * HT:(j + 1) * HT], [])
                    xa_free2[j % 2] = [(vs, vs.n)]

                with ExitStack() as qctx:
                    ps_q = [qctx.enter_context(nc.psum_tensor(f"ps_q{hh}{i}", [HD, 512], F32))
                            for i in range(2)]
                    pan_free = [[(ts, ts.n)], [(ts, ts.n)]]
                    pb_free = [bar(), bar()]
                    ps_free = [(ss, ss.n), (ss, ss.n)]
                    for h in range(NH):
                        wp = pan[h % 2]
                        lv = dma("sync", wp[:], QTW[h * HD:(h + 1) * HD, :], ld,
                                 waits=pan_free[h % 2])
                        spl = split_tile(wp[:], pbh[h % 2][:], pbl[h % 2][:],
                                         [lv, *pb_free[h % 2]])
                        pan_free[h % 2] = [spl]
                        pb = ps_q[h % 2]
                        mm3(pb, pbh[h % 2], pbl[h % 2], xyh, xyl, 0, HT,
                            [spl, ps_free[h % 2]])
                        p.emit("scalar", lambda E, h=h, pb=pb: E.activation(
                            qbuf[:, h * HT:(h + 1) * HT], pb[:], AF.Silu),
                            waits=[ts, (gp, gp.n)], inc=ss)
                        ps_free[h % 2] = (ss, ss.n)
                        pb_free[h % 2] = [(ts, ts.n)]
                    p.flush()

                if hh == 0:
                    emit_kv_init(obuf[:, 0:NH * HD])
                    for _h in range(NH):
                        state_done[_h] = (vs, vs.n)

                with ExitStack() as kctx:
                    ps_st = [kctx.enter_context(nc.psum_tensor(f"ps_st{hh}{i}", [HD, 2 * BLK], F32))
                             for i in range(2)]
                    ps_o = [kctx.enter_context(nc.psum_tensor(f"ps_o{hh}{i}", [HD, BLK], F32))
                            for i in range(2)]
                    ps_tr = [kctx.enter_context(nc.psum_tensor(f"ps_tr{hh}{i}", [HD, BLK], F32))
                             for i in range(2)]
                    ps_w = [kctx.enter_context(nc.psum_tensor(f"ps_w{hh}{i}", [HD, HD], F32))
                            for i in range(2)]
                    cur = lambda sem: (sem, sem.n)
                    tok = {k: [bar(), bar()] for k in
                           ("kre", "vre", "psst", "pso", "pstr", "psw",
                            "qdb", "dtt", "qq", "sdt", "ktm", "rowq")}
                    head_info = {}

                    def stage1(h, bb, su):
                        sh = h % 2
                        if bb == 0:
                            lvk = dma("sync", kre2[sh][:],
                                      kfm_scr3[:, h, hh * HT:(hh + 1) * HT], ld,
                                      waits=tok["kre"][sh])
                            lvv = dma("sync",
                                      vre2[sh][:].rearrange("p (tc e) -> p tc e", tc=4),
                                      vtm_scr[hh * HT:(hh + 1) * HT,
                                              h * HD:(h + 1) * HD].rearrange(
                                          "(tc p) e -> p tc e", p=HD),
                                      ld, waits=tok["vre"][sh])
                            p.emit("scalar", lambda E, h=h, sh=sh: E.activation(
                                rows[0:1, (2 + sh) * TPC:(2 + sh) * TPC + BLK], arrq[:],
                                AF.Exp, scale=sbneg[0:1, h:h + 1]),
                                waits=tok["rowq"][sh], inc=ss)
                            qdrow = cur(ss)
                            p.emit("tensor", lambda E, su=su, sh=sh: E.matmul(
                                ps_o[su][:], onesrow[:],
                                rows[0:1, (2 + sh) * TPC:(2 + sh) * TPC + BLK],
                                start=True, stop=True),
                                waits=[qdrow, *tok["pso"][su]], inc=ts)
                            bqmm = cur(ts)
                            tok["rowq"][sh] = [bqmm]
                            p.emit("scalar", lambda E, sh=sh, su=su: E.copy(
                                qdb2[sh][:], ps_o[su][:]),
                                waits=[bqmm, *tok["qdb"][sh]], inc=ss)
                            qdbev = cur(ss)
                            for j in range(2):
                                p.emit("scalar", lambda E, h=h, sh=sh, j=j: E.activation(
                                    dtt2[sh][:, j * BLK:(j + 1) * BLK],
                                    idxt[:, j * BLK:(j + 1) * BLK],
                                    AF.Exp, scale=sbneg[:, h:h + 1]),
                                    waits=tok["dtt"][sh] if j == 0 else [], inc=ss)
                            dttev = cur(ss)
                            head_info[h] = (lvk, lvv, qdbev, dttev)
                        lvk, lvv, qdbev, dttev = head_info[h]
                        qcols = qbuf[:, h * HT + bb * BLK:h * HT + (bb + 1) * BLK]
                        p.emit("vector", lambda E, sh=sh, su=su, qcols=qcols: E.tensor_tensor(
                            qq2[su][:], qcols, qdb2[sh][:], ALU.mult),
                            waits=[qdbev, *tok["qq"][su]], inc=vs)
                        qqv = cur(vs)
                        for j in range(2):
                            p.emit("tensor", lambda E, sh=sh, su=su, j=j, bb=bb, qcols=qcols:
                                   E.matmul(
                                ps_st[su][:, j * BLK:(j + 1) * BLK],
                                kre2[sh][:, bb * BLK + j * HD:bb * BLK + (j + 1) * HD],
                                qcols, start=(j == 0), stop=(j == 1)),
                                waits=[lvk, *(tok["psst"][su] if j == 0 else [])],
                                inc=(ts if j == 1 else None))
                        stmm = cur(ts)
                        return qqv, stmm

                    def stage2(h, bb, su, qqv, stmm):
                        sh = h % 2
                        lvk, lvv, qdbev, dttev = head_info[h]
                        kv_h = kvst[:, h * HD:(h + 1) * HD]
                        for j in range(2):
                            p.emit("vector", lambda E, sh=sh, su=su, j=j: E.tensor_tensor(
                                sdt2[su][:, j * BLK:(j + 1) * BLK],
                                ps_st[su][:, j * BLK:(j + 1) * BLK],
                                dtt2[sh][:, j * BLK:(j + 1) * BLK], ALU.mult),
                                waits=[stmm, dttev, *(tok["sdt"][su] if j == 0 else [])],
                                inc=vs)
                        sdtv = cur(vs)
                        tok["psst"][su] = [sdtv]
                        p.emit("tensor", lambda E, su=su, kv_h=kv_h: E.matmul(
                            ps_o[su][:], kv_h, qq2[su][:], start=True, stop=False),
                            waits=[qqv, state_done[h], qdbev, *tok["pso"][su]], inc=None)
                        for j in range(2):
                            p.emit("tensor", lambda E, sh=sh, su=su, j=j, bb=bb: E.matmul(
                                ps_o[su][:],
                                vre2[sh][:, (bb * 2 + j) * HD:(bb * 2 + j + 1) * HD],
                                sdt2[su][:, j * BLK:(j + 1) * BLK],
                                start=False, stop=(j == 1)),
                                waits=[lvv, sdtv], inc=(ts if j == 1 else None))
                        o_done = cur(ts)
                        tok["qq"][su] = [o_done]
                        tok["sdt"][su] = [o_done]
                        p.emit("scalar", lambda E, su=su, h=h, bb=bb: E.copy(
                            obuf[:, h * HT + bb * BLK:h * HT + (bb + 1) * BLK],
                            ps_o[su][:]), waits=[o_done], inc=ss)
                        oev = cur(ss)
                        tok["pso"][su] = [oev]
                        for j in range(2):
                            p.emit("tensor", lambda E, sh=sh, su=su, j=j, bb=bb: E.transpose(
                                ps_tr[su][:, j * HD:(j + 1) * HD],
                                kre2[sh][:, bb * BLK + j * HD:bb * BLK + (j + 1) * HD],
                                ident[:]),
                                waits=[lvk, *(tok["pstr"][su] if j == 0 else [])], inc=ts)
                        trmm = cur(ts)
                        for j in range(2):
                            p.emit("scalar", lambda E, su=su, j=j: E.copy(
                                ktm2[su][:, j * HD:(j + 1) * HD],
                                ps_tr[su][:, j * HD:(j + 1) * HD]),
                                waits=[trmm, *(tok["ktm"][su] if j == 0 else [])], inc=ss)
                        ktmev = cur(ss)
                        tok["pstr"][su] = [ktmev]
                        for j in range(2):
                            p.emit("vector", lambda E, su=su, h=h, j=j: E.tensor_scalar(
                                ktm2[su][:, j * HD:(j + 1) * HD],
                                ktm2[su][:, j * HD:(j + 1) * HD],
                                kdcol[:, h * 2 + j:h * 2 + j + 1], None, ALU.mult),
                                waits=[ktmev] if j == 0 else [], inc=vs)
                        kkv = cur(vs)
                        for j in range(2):
                            p.emit("tensor", lambda E, sh=sh, su=su, j=j, bb=bb: E.matmul(
                                ps_w[su][:], ktm2[su][:, j * HD:(j + 1) * HD],
                                vre2[sh][:, (bb * 2 + j) * HD:(bb * 2 + j + 1) * HD],
                                start=(j == 0), stop=(j == 1)),
                                waits=[kkv, lvv, *(tok["psw"][su] if j == 0 else [])],
                                inc=(ts if j == 1 else None))
                        wmm = cur(ts)
                        tok["ktm"][su] = [wmm]
                        p.emit("vector", lambda E, h=h, kv_h=kv_h: E.tensor_scalar(
                            kv_h, kv_h, bdt[:, h:h + 1], None, ALU.mult),
                            waits=[o_done], inc=vs)
                        p.emit("vector", lambda E, su=su, kv_h=kv_h: E.tensor_tensor(
                            kv_h, kv_h, ps_w[su][:], ALU.add), waits=[wmm], inc=vs)
                        state_done[h] = (vs, vs.n)
                        tok["psw"][su] = [state_done[h]]
                        if bb == 1:
                            tok["kre"][sh] = [(ts, trmm[1] if trmm[1] > wmm[1] else wmm[1])]
                            tok["vre"][sh] = [wmm]
                            tok["qdb"][sh] = [qqv]
                            tok["dtt"][sh] = [sdtv]

                    units = [(h, bb) for h in range(NH) for bb in range(2)]
                    pend = None
                    for ui, (h, bb) in enumerate(units):
                        s1 = stage1(h, bb, ui % 2)
                        if pend is not None:
                            stage2(*pend)
                        pend = (h, bb, ui % 2, *s1)
                    stage2(*pend)
                    p.flush()

                if dbg:
                    for h in range(NH):
                        dma("gpsimd", ODBG[:, (h * 2 + hh) * HT:(h * 2 + hh + 1) * HT],
                            obuf[:, h * HT:(h + 1) * HT], gp, waits=[(ss, ss.n), (vs, vs.n)])
                with ExitStack() as sctx:
                    ps_s2 = sctx.enter_context(nc.psum_tensor(f"ps_rms{hh}", [1, 512], F32))
                    for h in range(NH):
                        sqb = sqh[h % 2]
                        p.emit("scalar", lambda E, h=h, sqb=sqb: E.square(
                            sqb[:], obuf[:, h * HT:(h + 1) * HT]),
                            waits=[(vs, vs.n), (ts, ts.n)], inc=ss)
                        p.emit("tensor", lambda E, h=h, sqb=sqb: E.matmul(
                            ps_s2[:], onescol[:], sqb[:],
                            start=(h == 0), stop=(h == NH - 1)),
                            waits=[ss], inc=ts)
                    p.emit("scalar", lambda E: E.copy(row(1, HT), ps_s2[:]), waits=[ts], inc=ss)
                    p.flush()
                ln_rows_and_bcast(HT, EPS_RMS, rms=True)
                for h in range(NH):
                    p.emit("vector", lambda E, h=h: E.tensor_tensor(
                        obuf[:, h * HT:(h + 1) * HT], obuf[:, h * HT:(h + 1) * HT],
                        bc1[:, 0:HT], ALU.mult), waits=[ss, (gp, gp.n)], inc=vs)
                    p.emit("vector", lambda E, h=h: E.tensor_scalar(
                        obuf[:, h * HT:(h + 1) * HT], obuf[:, h * HT:(h + 1) * HT],
                        nwc[:, h:h + 1], None, ALU.mult), waits=[], inc=vs)

                with ExitStack() as gctx:
                    ps_g = [gctx.enter_context(nc.psum_tensor(f"ps_g{hh}{i}", [HD, 512], F32))
                            for i in range(2)]
                    pan_free = [[(ts, ts.n)], [(ts, ts.n)]]
                    pb_free = [bar(), bar()]
                    ps_free = [(ss, ss.n), (ss, ss.n)]
                    for oc in range(NCH):
                        wp = pan[oc % 2]
                        lv = dma("sync", wp[:], WGT[oc * HD:(oc + 1) * HD, :], ld,
                                 waits=pan_free[oc % 2])
                        spl = split_tile(wp[:], pbh[oc % 2][:], pbl[oc % 2][:],
                                         [lv, *pb_free[oc % 2]])
                        pan_free[oc % 2] = [spl]
                        pb = ps_g[oc % 2]
                        mm3(pb, pbh[oc % 2], pbl[oc % 2], xyh, xyl, 0, HT,
                            [spl, ps_free[oc % 2]])
                        p.emit("scalar", lambda E, pb=pb: E.activation(
                            gtile[:], pb[:], AF.Sigmoid), waits=[ts, (vs, vs.n)], inc=ss)
                        p.emit("vector", lambda E, oc=oc: E.tensor_tensor(
                            obuf[:, oc * HT:(oc + 1) * HT], obuf[:, oc * HT:(oc + 1) * HT],
                            gtile[:], ALU.mult), waits=[ss], inc=vs)
                        ps_free[oc % 2] = (ss, ss.n)
                        pb_free[oc % 2] = [(ts, ts.n)]
                    p.flush()

                gospl = bar()
                for oc in range(NCH):
                    split_tile(obuf[:, oc * HT:(oc + 1) * HT],
                               xyh[:, oc * HT:(oc + 1) * HT],
                               xyl[:, oc * HT:(oc + 1) * HT],
                               gospl if oc == 0 else [])
                gosplv = (vs, vs.n)

                if dbg:
                    for h in range(NH):
                        dma("gpsimd", GODBG[:, (h * 2 + hh) * HT:(h * 2 + hh + 1) * HT],
                            obuf[:, h * HT:(h + 1) * HT], gp, waits=[(ss, ss.n), (vs, vs.n)])
                with ExitStack() as octx:
                    ps_y = [octx.enter_context(nc.psum_tensor(f"ps_y{hh}{i}", [HD, 512], F32))
                            for i in range(2)]
                    ps_s1 = octx.enter_context(nc.psum_tensor(f"ps_ys1{hh}", [1, 512], F32))
                    ps_s2 = octx.enter_context(nc.psum_tensor(f"ps_ys2{hh}", [1, 512], F32))
                    pan_free = [[(ts, ts.n)], [(ts, ts.n)]]
                    pb_free = [bar(), bar()]
                    ps_free = [(ss, ss.n), (ss, ss.n)]
                    for fc in range(NCH):
                        wp = pan[fc % 2]
                        lv = dma("sync", wp[:], WOT[fc * HD:(fc + 1) * HD, :], ld,
                                 waits=pan_free[fc % 2])
                        spl = split_tile(wp[:], pbh[fc % 2][:], pbl[fc % 2][:],
                                         [lv, gosplv, *pb_free[fc % 2]])
                        pan_free[fc % 2] = [spl]
                        pb = ps_y[fc % 2]
                        mm3(pb, pbh[fc % 2], pbl[fc % 2], xyh, xyl, 0, HT,
                            [spl, ps_free[fc % 2]])
                        pb_free[fc % 2] = [(ts, ts.n)]
                        p.emit("scalar", lambda E, fc=fc, pb=pb: E.copy(
                            qbuf[:, fc * HT:(fc + 1) * HT], pb[:]), waits=[ts], inc=ss)
                        sqb = sqh[fc % 2]
                        p.emit("scalar", lambda E, fc=fc, sqb=sqb: E.square(
                            sqb[:], qbuf[:, fc * HT:(fc + 1) * HT]),
                            waits=[(ts, ts.n)], inc=ss)
                        p.emit("tensor", lambda E, fc=fc: E.matmul(
                            ps_s1[:], onescol[:], qbuf[:, fc * HT:(fc + 1) * HT],
                            start=(fc == 0), stop=(fc == NCH - 1)), waits=[ss], inc=None)
                        p.emit("tensor", lambda E, fc=fc, sqb=sqb: E.matmul(
                            ps_s2[:], onescol[:], sqb[:],
                            start=(fc == 0), stop=(fc == NCH - 1)),
                            waits=[ss], inc=ts)
                        ps_free[fc % 2] = (ss, ss.n)
                        pan_free[fc % 2] = [(ts, ts.n)]
                    p.emit("scalar", lambda E: E.copy(row(0, HT), ps_s1[:]), waits=[ts], inc=ss)
                    p.emit("scalar", lambda E: E.copy(row(1, HT), ps_s2[:]), waits=[ts], inc=ss)
                    p.flush()
                if dbg:
                    for j in range(NCH):
                        dma("gpsimd", YDBG[:, (j * 2 + hh) * HT:(j * 2 + hh + 1) * HT],
                            qbuf[:, j * HT:(j + 1) * HT], gp, waits=[(ss, ss.n)])
                ln_rows_and_bcast(HT, EPS_LN)
                for j in range(NCH):
                    lv = dma("sync", xha[j % 2][:], XT3[:, j, hh * HT:(hh + 1) * HT],
                             ld, waits=xa_free2[j % 2])
                    p.emit("vector", lambda E, j=j: E.tensor_tensor(
                        qbuf[:, j * HT:(j + 1) * HT], qbuf[:, j * HT:(j + 1) * HT],
                        bc1[:, 0:HT], ALU.mult), waits=[ss, (gp, gp.n)], inc=vs)
                    p.emit("vector", lambda E, j=j: E.tensor_tensor(
                        qbuf[:, j * HT:(j + 1) * HT], qbuf[:, j * HT:(j + 1) * HT],
                        bc2[:, 0:HT], ALU.subtract), waits=[], inc=vs)
                    p.emit("vector", lambda E, j=j: E.tensor_tensor(
                        qbuf[:, j * HT:(j + 1) * HT], qbuf[:, j * HT:(j + 1) * HT],
                        xha[j % 2][:], ALU.add), waits=[lv], inc=vs)
                    dma("gpsimd", x1_scr3[:, j, hh * HT:(hh + 1) * HT],
                        qbuf[:, j * HT:(j + 1) * HT], gp, waits=[(vs, vs.n)])
                    xa_free2[j % 2] = [(vs, vs.n)]
                p.flush()

        if stage == "p2":
            p.emit("sync", lambda E: E.nop(), waits=bar(), inc=None)
            p.flush()
            return nc, outs

        OUTT3 = OUTT.rearrange("(j p) t -> p j t", p=HD)
        nic = IGRP // HD
        with ExitStack() as fctx:
            fsb = lambda name, shape, dt=F32: fctx.enter_context(
                nc.sbuf_tensor(f"f_{name}", shape, dt))
            ln2h = fsb("ln2h", [HD, NCH * TPC], BF16)
            ln2l = fsb("ln2l", [HD, NCH * TPC], BF16)
            y2 = fsb("y2", [HD, NCH * HT])
            x1t = [fsb(f"x1t{i}", [HD, TPC]) for i in range(2)]
            sqf = [fsb(f"sqf{i}", [HD, TPC]) for i in range(2)]
            w8 = [fsb(f"w8_{i}", [HD, NCH * IGRP], FP8) for i in range(2)]
            gall = fsb("gall", [HD, nic * HT])
            utmp = fsb("utmp", [HD, HT])
            fh = fsb("fh", [HD, nic * HT], BF16)
            fl = fsb("fl", [HD, nic * HT], BF16)
            lntmp = fsb("lntmp", [HD, TPC])
            x1_free = [bar(), bar()]
            with ExitStack() as sctx:
                ps_s1 = [sctx.enter_context(nc.psum_tensor(f"ps3s1{u}", [1, 512], F32))
                         for u in range(2)]
                ps_s2 = [sctx.enter_context(nc.psum_tensor(f"ps3s2{u}", [1, 512], F32))
                         for u in range(2)]
                for j in range(NCH):
                    lv = dma("sync", x1t[j % 2][:], x1_scr3[:, j, :],
                             ld, waits=[*x1_free[j % 2], (gp, gp.n)])
                    sqb = sqf[j % 2]
                    p.emit("scalar", lambda E, j=j, sqb=sqb: E.square(
                        sqb[:], x1t[j % 2][:]), waits=[lv, (ts, ts.n)], inc=ss)
                    for u in range(2):
                        p.emit("tensor", lambda E, j=j, u=u: E.matmul(
                            ps_s1[u][:], onescol[:], x1t[j % 2][:, u * 512:(u + 1) * 512],
                            start=(j == 0), stop=(j == NCH - 1)), waits=[lv, ss], inc=None)
                        p.emit("tensor", lambda E, j=j, u=u, sqb=sqb: E.matmul(
                            ps_s2[u][:], onescol[:], sqb[:, u * 512:(u + 1) * 512],
                            start=(j == 0), stop=(j == NCH - 1)),
                            waits=[ss], inc=(ts if u == 1 else None))
                    x1_free[j % 2] = [(ts, ts.n)]
                for u in range(2):
                    p.emit("scalar", lambda E, u=u: E.copy(row(0)[0:1, u * 512:(u + 1) * 512],
                                                           ps_s1[u][:]), waits=[ts], inc=ss)
                    p.emit("scalar", lambda E, u=u: E.copy(row(1)[0:1, u * 512:(u + 1) * 512],
                                                           ps_s2[u][:]), waits=[ts], inc=ss)
                p.flush()
            ln_rows_and_bcast(TPC, EPS_LN)
            for j in range(NCH):
                lv = dma("sync", x1t[j % 2][:], x1_scr3[:, j, :], ld, waits=x1_free[j % 2])
                p.emit("vector", lambda E, j=j: E.tensor_tensor(
                    lntmp[:], x1t[j % 2][:], bc1[:], ALU.mult),
                    waits=[lv, ss, (ts, ts.n)], inc=vs)
                p.emit("vector", lambda E, j=j: E.tensor_tensor(
                    lntmp[:], lntmp[:], bc2[:], ALU.subtract), waits=[], inc=vs)
                p.emit("vector", lambda E, j=j: E.tensor_copy(
                    ln2h[:, j * TPC:(j + 1) * TPC], lntmp[:]), waits=[], inc=vs)
                p.emit("vector", lambda E, j=j: E.tensor_tensor(
                    ln2l[:, j * TPC:(j + 1) * TPC], lntmp[:],
                    ln2h[:, j * TPC:(j + 1) * TPC], ALU.subtract), waits=[], inc=vs)
                x1_free[j % 2] = [(vs, vs.n)]

            with ExitStack() as wctx:
                ps_gu = [wctx.enter_context(
                    nc.psum_tensor(f"ps_gu{i}", [HD, 512], F32)) for i in range(2)]
                ps_d = [wctx.enter_context(
                    nc.psum_tensor(f"ps_d{i}", [HD, 512], F32)) for i in range(2)]
                slab_free = [[(ts, ts.n)], [(ts, ts.n)]]
                slot = 0
                lnw = (vs, vs.n)
                gu_free = [(ss, ss.n), (ss, ss.n)]
                d_free = [[(ss, ss.n)], [(ss, ss.n)]]
                for hh in range(2):
                    for ig in range(NIG):
                        sg = slot; slot = (slot + 1) % 2
                        lvg = dma("gpsimd", w8[sg][:], GW[ig * HD:(ig + 1) * HD, :], gp,
                                  waits=slab_free[sg])
                        for ic in range(nic):
                            pb = ps_gu[ic % 2]
                            for pi, lnb in enumerate((ln2h, ln2l)):
                                for j in range(NCH):
                                    p.emit("tensor", lambda E, ig=ig, ic=ic, j=j, pb=pb,
                                           lnb=lnb, sg=sg, pi=pi, hh=hh:
                                           E.matmul(pb[:],
                                                    w8[sg][:, j * IGRP + ic * HD:j * IGRP + (ic + 1) * HD],
                                                    lnb[:, j * TPC + hh * HT:j * TPC + (hh + 1) * HT],
                                                    start=(pi == 0 and j == 0),
                                                    stop=(pi == 1 and j == NCH - 1)),
                                           waits=[lvg, lnw, gu_free[ic % 2]],
                                           inc=(ts if (pi == 1 and j == NCH - 1) else None))
                            p.emit("scalar", lambda E, ig=ig, ic=ic, pb=pb: E.activation(
                                gall[:, ic * HT:(ic + 1) * HT], pb[:],
                                AF.Silu, scale=gsc[:, ig * nic + ic:ig * nic + ic + 1]),
                                waits=[ts], inc=ss)
                            gu_free[ic % 2] = (ss, ss.n)
                        slab_free[sg] = [(ts, ts.n)]
                        su = slot; slot = (slot + 1) % 2
                        lvu = dma("gpsimd", w8[su][:], UW[ig * HD:(ig + 1) * HD, :], gp,
                                  waits=slab_free[su])
                        for ic in range(nic):
                            pb = ps_gu[ic % 2]
                            for pi, lnb in enumerate((ln2h, ln2l)):
                                for j in range(NCH):
                                    p.emit("tensor", lambda E, ig=ig, ic=ic, j=j, pb=pb,
                                           lnb=lnb, su=su, pi=pi, hh=hh:
                                           E.matmul(pb[:],
                                                    w8[su][:, j * IGRP + ic * HD:j * IGRP + (ic + 1) * HD],
                                                    lnb[:, j * TPC + hh * HT:j * TPC + (hh + 1) * HT],
                                                    start=(pi == 0 and j == 0),
                                                    stop=(pi == 1 and j == NCH - 1)),
                                           waits=[lvu, gu_free[ic % 2]],
                                           inc=(ts if (pi == 1 and j == NCH - 1) else None))
                            p.emit("scalar", lambda E, ig=ig, ic=ic, pb=pb: E.activation(
                                utmp[:], pb[:], AF.Copy,
                                scale=usc[:, ig * nic + ic:ig * nic + ic + 1]),
                                waits=[ts, (vs, vs.n)], inc=ss)
                            gu_free[ic % 2] = (ss, ss.n)
                            p.emit("vector", lambda E, ic=ic: E.tensor_tensor(
                                gall[:, ic * HT:(ic + 1) * HT], gall[:, ic * HT:(ic + 1) * HT],
                                utmp[:], ALU.mult), waits=[ss], inc=vs)
                            p.emit("vector", lambda E, ic=ic: E.tensor_copy(
                                fh[:, ic * HT:(ic + 1) * HT], gall[:, ic * HT:(ic + 1) * HT]),
                                waits=[], inc=vs)
                            p.emit("vector", lambda E, ic=ic: E.tensor_tensor(
                                fl[:, ic * HT:(ic + 1) * HT], gall[:, ic * HT:(ic + 1) * HT],
                                fh[:, ic * HT:(ic + 1) * HT], ALU.subtract), waits=[], inc=vs)
                        slab_free[su] = [(ts, ts.n)]
                        sd = slot; slot = (slot + 1) % 2
                        lvd = dma("gpsimd", w8[sd][:], DW[ig * HD:(ig + 1) * HD, :], gp,
                                  waits=slab_free[sd])
                        for fc in range(NCH):
                            pb = ps_d[fc % 2]
                            for pi, fb in enumerate((fh, fl)):
                                for jc in range(nic):
                                    p.emit("tensor", lambda E, ig=ig, fc=fc, jc=jc, pb=pb,
                                           fb=fb, sd=sd, pi=pi:
                                           E.matmul(pb[:],
                                                    w8[sd][:, jc * DM + fc * HD:jc * DM + (fc + 1) * HD],
                                                    fb[:, jc * HT:(jc + 1) * HT],
                                                    start=(pi == 0 and jc == 0),
                                                    stop=(pi == 1 and jc == nic - 1)),
                                           waits=[lvd, (vs, vs.n), *d_free[fc % 2]],
                                           inc=(ts if (pi == 1 and jc == nic - 1) else None))
                            if ig == 0:
                                p.emit("scalar", lambda E, fc=fc, pb=pb: E.copy(
                                    y2[:, fc * HT:(fc + 1) * HT], pb[:]),
                                    waits=[ts, (st, st.n)], inc=ss)
                            else:
                                p.emit("vector", lambda E, fc=fc, pb=pb: E.tensor_tensor(
                                    y2[:, fc * HT:(fc + 1) * HT], y2[:, fc * HT:(fc + 1) * HT],
                                    pb[:], ALU.add), waits=[ts], inc=vs)
                            d_free[fc % 2] = [(ss, ss.n), (vs, vs.n)]
                        slab_free[sd] = [(ts, ts.n)]
                    for fc in range(NCH):
                        p.emit("vector", lambda E, fc=fc: E.tensor_scalar(
                            y2[:, fc * HT:(fc + 1) * HT], y2[:, fc * HT:(fc + 1) * HT],
                            dsc[:, fc:fc + 1], None, ALU.mult), waits=[(ss, ss.n)], inc=vs)
                    with ExitStack() as sctx:
                        ps_s1 = sctx.enter_context(nc.psum_tensor(f"ps4s1{hh}", [1, 512], F32))
                        ps_s2 = sctx.enter_context(nc.psum_tensor(f"ps4s2{hh}", [1, 512], F32))
                        for j in range(NCH):
                            sqb = sqf[j % 2]
                            p.emit("scalar", lambda E, j=j, sqb=sqb: E.square(
                                sqb[:, 0:HT], y2[:, j * HT:(j + 1) * HT]),
                                waits=[(vs, vs.n), (ts, ts.n)], inc=ss)
                            p.emit("tensor", lambda E, j=j: E.matmul(
                                ps_s1[:], onescol[:], y2[:, j * HT:(j + 1) * HT],
                                start=(j == 0), stop=(j == NCH - 1)),
                                waits=[(vs, vs.n), ss], inc=None)
                            p.emit("tensor", lambda E, j=j, sqb=sqb: E.matmul(
                                ps_s2[:], onescol[:], sqb[:, 0:HT],
                                start=(j == 0), stop=(j == NCH - 1)), waits=[ss], inc=ts)
                        p.emit("scalar", lambda E: E.copy(row(0, HT), ps_s1[:]),
                               waits=[ts], inc=ss)
                        p.emit("scalar", lambda E: E.copy(row(1, HT), ps_s2[:]),
                               waits=[ts], inc=ss)
                        p.flush()
                    ln_rows_and_bcast(HT, EPS_LN)
                    for j in range(NCH):
                        lv = dma("sync", x1t[j % 2][:, 0:HT],
                                 x1_scr3[:, j, hh * HT:(hh + 1) * HT], ld,
                                 waits=x1_free[j % 2])
                        p.emit("vector", lambda E, j=j: E.tensor_tensor(
                            y2[:, j * HT:(j + 1) * HT], y2[:, j * HT:(j + 1) * HT],
                            bc1[:, 0:HT], ALU.mult), waits=[ss], inc=vs)
                        p.emit("vector", lambda E, j=j: E.tensor_tensor(
                            y2[:, j * HT:(j + 1) * HT], y2[:, j * HT:(j + 1) * HT],
                            bc2[:, 0:HT], ALU.subtract), waits=[], inc=vs)
                        p.emit("vector", lambda E, j=j: E.tensor_tensor(
                            y2[:, j * HT:(j + 1) * HT], y2[:, j * HT:(j + 1) * HT],
                            x1t[j % 2][:, 0:HT], ALU.add), waits=[lv], inc=vs)
                        dma("sync", OUTT3[:, j, hh * HT:(hh + 1) * HT],
                            y2[:, j * HT:(j + 1) * HT], st, waits=[(vs, vs.n)])
                        x1_free[j % 2] = [(vs, vs.n)]
                    p.flush()

        p.emit("sync", lambda E: E.nop(), waits=[(st, st.n)], inc=None)
        p.flush()
        return nc, outs


def stage_inputs(inputs):
    x = np.ascontiguousarray(np.asarray(inputs["L0_attn_hidden_states"], np.float32))

    def pan(M, csz):
        K, N = M.shape
        t = M.reshape(K // HD, HD, N // csz, csz).transpose(2, 1, 0, 3)
        return np.ascontiguousarray(t.reshape(N // csz * HD, K // HD * csz))

    wqkv = np.asarray(inputs["L0_attn_w_qkv"], np.float32).reshape(NH, 3, HD, DM)
    QTW = pan(wqkv[:, 0].reshape(DM, DM).T, HD)
    KTW = pan(wqkv[:, 1].reshape(DM, DM).T, HD)
    VTW = pan(wqkv[:, 2].reshape(DM, DM).T, HD)
    WGT = pan(np.asarray(inputs["L0_attn_w_gate"], np.float32).T, HD)
    WOT = pan(np.asarray(inputs["L0_attn_w_out"], np.float32).T, HD)
    GW = pan(np.asarray(inputs["L0_ffn_gate_w"], np.float32), IGRP)
    UW = pan(np.asarray(inputs["L0_ffn_up_w"], np.float32), IGRP)
    dw_ = np.asarray(inputs["L0_ffn_down_w"], np.float32)
    DW = np.ascontiguousarray(dw_.reshape(NIG, IGRP // HD, HD, DM).transpose(0, 2, 1, 3)
                              .reshape(NIG * HD, (IGRP // HD) * DM))
    SLOPES = np.asarray(inputs["L0_attn_slope_rate"], np.float32).reshape(1, NH)
    NWC = np.ascontiguousarray(np.asarray(inputs["L0_attn_norm_w"], np.float32)
                               .reshape(NCH, HD).T)
    GSC = np.ascontiguousarray(np.asarray(inputs["L0_ffn_gate_scale"], np.float32)
                               .reshape(FI // HD, HD).T)
    USC = np.ascontiguousarray(np.asarray(inputs["L0_ffn_up_scale"], np.float32)
                               .reshape(FI // HD, HD).T)
    DSC = np.ascontiguousarray(np.asarray(inputs["L0_ffn_down_scale"], np.float32)
                               .reshape(NCH, HD).T)
    IDENT = np.eye(HD, dtype=np.float32)
    m = np.arange(BLK, dtype=np.float32)
    IDXT = np.where(m[None, :] >= m[:, None], m[None, :] - m[:, None],
                    np.float32(BIG)).astype(np.float32)
    ARRQ = np.arange(1, BLK + 1, dtype=np.float32).reshape(1, BLK)
    ARRK = (BLK - np.arange(1, BLK + 1, dtype=np.float32)).reshape(1, BLK)
    ONESROW = np.ones((1, HD), np.float32)
    ONESCOL = np.ones((HD, 1), np.float32)

    shared = dict(qtw=QTW, ktw=KTW, vtw=VTW, wgt=WGT, wot=WOT, gw=GW, uw=UW, dw=DW,
                  slopes=SLOPES, nwc=NWC, gsc=GSC, usc=USC, dsc=DSC, ident=IDENT,
                  idxt=IDXT, arrq=ARRQ, arrk=ARRK, onesrow=ONESROW, onescol=ONESCOL)
    in_maps = []
    for c in range(NCORES):
        b, pp = c // 4, c % 4
        xt = np.ascontiguousarray(x[b, pp * TPC:(pp + 1) * TPC, :].T)
        expo = np.full((1, NCORES), BIG, np.float32)
        for cp in range(NCORES):
            if cp // 4 == b and cp % 4 < pp:
                expo[0, cp] = np.float32(pp - 1 - (cp % 4))
        in_maps.append(dict(shared, xt=xt, expo=expo))
    return in_maps


def run(inputs, cfg):
    nc, outs = build(cfg)
    in_maps = stage_inputs(inputs)
    res = run_bass_kernel_spmd(nc, in_maps, core_ids=list(range(NCORES)))
    return res, outs


def kernel(**inputs):
    res, outs = run(inputs, {"stage": "full", "debug": False})
    out = np.empty((2, 4096, DM), np.float32)
    for c in range(NCORES):
        b, pp = c // 4, c % 4
        out[b, pp * TPC:(pp + 1) * TPC, :] = np.asarray(res.results[c]["outt"]).T
    return out


# version 10
# speedup vs baseline: 1.1825x; 1.0129x over previous
import numpy as np
from contextlib import ExitStack

import concourse.bass as bass
import concourse.mybir as mybir
from concourse.bass_utils import run_bass_kernel_spmd

F32 = mybir.dt.float32
BF16 = mybir.dt.bfloat16
FP8 = mybir.dt.float8e4
AF = mybir.ActivationFunctionType
ALU = mybir.AluOpType

NCORES = 8
DM = 2048
NCH = 16
NH = 16
HD = 128
BLK = 256
TPC = 1024
NBLK = 4
FI = 6144
EPS_LN = 1e-5
EPS_RMS = 1e-6
IGRP = 512
NIG = FI // IGRP
BIG = 1.0e9
HT = 512

ENGS = ("sync", "gpsimd", "tensor", "scalar", "vector")


class Sem:
    def __init__(self, handle, owner=None):
        self.h = handle
        self.n = 0
        self.owner = owner

    def inc(self, d=1):
        self.n += d


class Prog:

    def __init__(self, nc, block):
        self.nc = nc
        self.block = block
        self.q = {e: [] for e in ENGS}
        self.lastw = {e: {} for e in ENGS}

    def emit(self, eng, fn, waits=(), inc=None, inc_by=1):
        ws = []
        for w in waits:
            if isinstance(w, Sem):
                s, v = w, w.n
            else:
                s, v = w
            if v <= 0:
                continue
            if s.owner == eng and s.owner in ("tensor", "scalar", "vector"):
                continue
            prev = self.lastw[eng].get(s, 0)
            if v > prev:
                self.lastw[eng][s] = v
                ws.append((s.h, v))
        incs = []
        if inc is not None:
            inc.inc(inc_by)
            incs.append((inc.h, inc_by))
        self.q[eng].append((ws, fn, incs))

    def flush(self):
        for e in ENGS:
            items = self.q[e]
            if not items:
                continue
            self.q[e] = []

            def body(engine, items=items):
                for ws, fn, incs in items:
                    for (h, v) in ws:
                        engine.wait_ge(h, v)
                    inst = fn(engine)
                    for (h, d) in incs:
                        inst.then_inc(h, d)
            getattr(self.block, e)(body)


def build(cfg):
    stage = cfg.get("stage", "full")
    upto = cfg.get("upto", None)
    nheads = cfg.get("nheads", NH)
    dbg = cfg.get("debug", False)
    nc = bass.Bass("TRN2", target_bir_lowering=False, debug=False,
                   detect_race_conditions=False)

    def din(name, shape):
        return nc.dram_tensor(name, shape, F32, kind="ExternalInput").ap()

    XT = din("xt", [DM, TPC])
    QTW = din("qtw", [NCH * HD, DM])
    KTW = din("ktw", [NCH * HD, DM])
    VTW = din("vtw", [NCH * HD, DM])
    WGT = din("wgt", [NCH * HD, DM])
    WOT = din("wot", [NCH * HD, DM])
    GW = din("gw", [NIG * HD, NCH * IGRP])
    UW = din("uw", [NIG * HD, NCH * IGRP])
    DW = din("dw", [NIG * HD, (IGRP // HD) * DM])
    SLOPES = din("slopes", [1, NH])
    EXPO = din("expo", [1, NCORES])
    NWC = din("nwc", [HD, NCH])
    GSC = din("gsc", [HD, FI // HD])
    USC = din("usc", [HD, FI // HD])
    DSC = din("dsc", [HD, NCH])
    IDENT = din("ident", [HD, HD])
    IDXT = din("idxt", [BLK, BLK])
    ARRQ = din("arrq", [1, BLK])
    ARRK = din("arrk", [1, BLK])
    ONESROW = din("onesrow", [1, HD])
    ONESCOL = din("onescol", [HD, 1])

    outs = {}

    def dout(name, shape):
        outs[name] = nc.dram_tensor(name, shape, F32, kind="ExternalOutput").ap()
        return outs[name]

    def scr(name, shape, debug_out=False):
        if debug_out and dbg:
            return dout(name, shape)
        return nc.dram_tensor(name, shape, F32, kind="Internal").ap()

    kfm_scr = scr("kfm_scr", [DM, TPC], debug_out=True)
    vtm_scr = scr("vtm_scr", [TPC, DM], debug_out=True)
    x1_scr = (dout("x1_scr", [DM, TPC]) if stage != "full"
              else nc.dram_tensor("x1_scr", [DM, TPC], F32, kind="Internal").ap())
    ag_in = nc.dram_tensor("ag_in", [NH * HD, HD], F32, kind="Internal").ap()
    ag_out = nc.dram_tensor("ag_out", [NCORES * NH * HD, HD], F32, kind="Internal").ap()
    if dbg:
        KV0 = dout("kv0_dbg", [HD, NH * HD])
        ODBG = dout("o_dbg", [HD, NH * TPC])
        GODBG = dout("go_dbg", [HD, NH * TPC])
        YDBG = dout("y_dbg", [HD, NCH * TPC])
    if stage == "full":
        OUTT = dout("outt", [DM, TPC])

    with ExitStack() as ctx:
        def sb(name, shape, dtype=F32):
            return ctx.enter_context(nc.sbuf_tensor("s_" + name, shape, dtype))

        def mksem(name, owner=None):
            return Sem(ctx.enter_context(nc.semaphore(name)), owner)

        ld = mksem("ld", "sync")
        gp = mksem("gp", "gpsimd")
        ts = mksem("ts", "tensor")
        ss = mksem("ss", "scalar")
        vs = mksem("vs", "vector")
        cc = mksem("cc", "gpsimd")
        st = mksem("st", "sync")

        ident = sb("ident", [HD, HD])
        idxt = sb("idxt", [HD, 2 * BLK])
        arrq = sb("arrq", [1, BLK])
        arrk = sb("arrk", [1, BLK])
        onesrow = sb("onesrow", [1, HD])
        onescol = sb("onescol", [HD, 1])
        slopes = sb("slopes", [1, NH])
        expo = sb("expo", [1, NCORES])
        nwc = sb("nwc", [HD, NCH])
        gsc = sb("gsc", [HD, FI // HD])
        usc = sb("usc", [HD, FI // HD])
        dsc = sb("dsc", [HD, NCH])
        sb16 = sb("sb16", [HD, NH])
        sbneg = sb("sbneg", [HD, NH])
        bdt = sb("bdt", [HD, NH])
        kdcol = sb("kdcol", [HD, NH * 2])
        wcomb = sb("wcomb", [HD, NCORES * NH])
        kvst = sb("kvst", [HD, NH * HD])
        rows = sb("rows", [1, 4 * TPC])
        rows2 = sb("rows2", [1, 2 * TPC])
        bc1 = sb("bc1", [HD, TPC])
        bc2 = sb("bc2", [HD, TPC])

        block = ctx.enter_context(nc.Block())
        p = Prog(nc, block)

        def dma(eng, out, in_, sem, waits=()):
            p.emit(eng, lambda E, o=out, i=in_: E.dma_start(out=o, in_=i),
                   waits=waits, inc=sem, inc_by=16)
            return (sem, sem.n)

        def row(k, n=TPC):
            return rows[0:1, k * TPC:k * TPC + n]

        def bar():
            return [(ld, ld.n), (gp, gp.n), (ts, ts.n), (ss, ss.n), (vs, vs.n), (cc, cc.n)]

        def mm3(pb, wh, wl, xh_, xl_, tt, T, first_waits):
            for pi, (wp_, xp_) in enumerate(((wh, xh_), (wh, xl_), (wl, xh_))):
                for j in range(NCH):
                    p.emit("tensor",
                           lambda E, pi=pi, wp_=wp_, xp_=xp_, j=j, tt=tt, pb=pb:
                           E.matmul(
                               pb[:], wp_[:, j * HD:(j + 1) * HD],
                               xp_[:, j * T + tt * 512:j * T + (tt + 1) * 512],
                               start=(pi == 0 and j == 0),
                               stop=(pi == 2 and j == NCH - 1)),
                           waits=first_waits if (pi == 0 and j == 0) else [],
                           inc=(ts if (pi == 2 and j == NCH - 1) else None))

        def split_tile(fp, bh, bl, waits):
            p.emit("vector", lambda E: E.tensor_copy(bh, fp), waits=waits, inc=vs)
            p.emit("vector", lambda E: E.tensor_tensor(bl, fp, bh, ALU.subtract),
                   waits=[], inc=vs)
            return (vs, vs.n)

        _lrc = [0]

        def ln_rows_and_bcast(T, eps, rms=False, upto2=None):
            upto2 = upto2 or cfg.get("upto2")
            inv_dm = 1.0 / DM
            if not rms:
                p.emit("vector", lambda E: E.tensor_scalar(row(0, T), row(0, T), inv_dm,
                                                           None, ALU.mult), waits=[ss], inc=vs)
            p.emit("vector", lambda E: E.tensor_scalar(row(1, T), row(1, T), inv_dm,
                                                       None, ALU.mult), waits=[ss], inc=vs)
            if not rms:
                p.emit("vector", lambda E: E.tensor_tensor(row(2, T), row(0, T), row(0, T),
                                                           ALU.mult), waits=[], inc=vs)
                p.emit("vector", lambda E: E.tensor_tensor(row(1, T), row(1, T), row(2, T),
                                                           ALU.subtract), waits=[], inc=vs)
            p.emit("vector", lambda E: E.tensor_scalar(row(1, T), row(1, T), float(eps),
                                                       None, ALU.add), waits=[], inc=vs)
            if upto2 == "lr1":
                return
            p.emit("scalar", lambda E: E.activation(row(1, T), row(1, T), AF.Sqrt),
                   waits=[vs], inc=ss)
            p.emit("vector", lambda E: E.reciprocal(row(2, T), row(1, T)), waits=[ss], inc=vs)
            if upto2 == "lr2":
                return
            if not rms:
                p.emit("vector", lambda E: E.tensor_tensor(row(3, T), row(0, T), row(2, T),
                                                           ALU.mult), waits=[], inc=vs)
            if upto2 == "lr3":
                return
            _lrc[0] += 1
            with nc.psum_tensor(f"ps_bcr{_lrc[0]}", [HD, 512], F32) as ps_b:
                for u in range((T + 511) // 512):
                    w = min(512, T - u * 512)
                    p.emit("tensor", lambda E, u=u, w=w: E.matmul(
                        ps_b[:, 0:w], onesrow[:], row(2, T)[0:1, u * 512:u * 512 + w],
                        start=True, stop=True), waits=[vs, ss], inc=ts)
                    if upto2 == "lrb1":
                        continue
                    p.emit("scalar", lambda E, u=u, w=w: E.copy(
                        bc1[:, u * 512:u * 512 + w], ps_b[:, 0:w]), waits=[ts], inc=ss)
                    if upto2 == "lrb2" or rms:
                        continue
                    p.emit("tensor", lambda E, u=u, w=w: E.matmul(
                        ps_b[:, 0:w], onesrow[:], row(3, T)[0:1, u * 512:u * 512 + w],
                        start=True, stop=True), waits=[ss], inc=ts)
                    if upto2 == "lrb3":
                        continue
                    p.emit("scalar", lambda E, u=u, w=w: E.copy(
                        bc2[:, u * 512:u * 512 + w], ps_b[:, 0:w]), waits=[ts], inc=ss)
                p.flush()

        for (t, d) in [(ident, IDENT), (arrq, ARRQ), (arrk, ARRK),
                       (onesrow, ONESROW), (onescol, ONESCOL), (slopes, SLOPES),
                       (expo, EXPO), (nwc, NWC), (gsc, GSC), (usc, USC), (dsc, DSC)]:
            dma("sync", t[:], d, ld)
        for j in range(2):
            dma("sync", idxt[:, j * BLK:(j + 1) * BLK], IDXT[j * HD:(j + 1) * HD, :], ld)
        p0_ld = (ld, ld.n)

        with nc.psum_tensor("ps_bc0", [HD, 512], F32) as ps_bc:
            p.emit("tensor", lambda E: E.matmul(ps_bc[:, 0:NH], onesrow[:], slopes[:],
                                                start=True, stop=True),
                   waits=[p0_ld], inc=ts)
            p.emit("scalar", lambda E: E.copy(sb16[:], ps_bc[:, 0:NH]), waits=[ts], inc=ss)
            p.emit("vector", lambda E: E.tensor_scalar(sbneg[:], sb16[:], -1.0, None, ALU.mult),
                   waits=[ss], inc=vs)
            p.emit("scalar", lambda E: E.activation(bdt[:], sb16[:], AF.Exp, scale=-float(BLK)),
                   waits=[vs], inc=ss)
            p.emit("tensor", lambda E: E.matmul(ps_bc[:, 32:32 + NCORES], onesrow[:], expo[:],
                                                start=True, stop=True),
                   waits=[(ss, ss.n)], inc=ts)
            p.emit("scalar", lambda E: E.copy(bc1[:, 0:NCORES], ps_bc[:, 32:32 + NCORES]),
                   waits=[ts], inc=ss)
            for c in range(NCORES):
                p.emit("vector",
                       lambda E, c=c: E.tensor_scalar(bc2[:, 0:NH], sb16[:], -1024.0,
                                                      bc1[:, c:c + 1], ALU.mult, ALU.mult),
                       waits=[ss], inc=vs)
                p.emit("scalar",
                       lambda E, c=c: E.activation(wcomb[:, c * NH:(c + 1) * NH],
                                                   bc2[:, 0:NH], AF.Exp),
                       waits=[vs], inc=ss)
            with nc.psum_tensor("ps_tr0", [HD, HD], F32) as ps_tr:
                for h in range(NH):
                    p.emit("scalar",
                           lambda E, h=h: E.activation(row(0, BLK), arrk[:], AF.Exp,
                                                       scale=sbneg[0:1, h:h + 1]),
                           waits=[ts, vs], inc=ss)
                    p.emit("tensor",
                           lambda E: E.matmul(ps_bc[:, 0:BLK], onesrow[:], row(0, BLK),
                                              start=True, stop=True),
                           waits=[ss], inc=ts)
                    p.emit("scalar",
                           lambda E: E.copy(bc2[:, 0:BLK], ps_bc[:, 0:BLK]),
                           waits=[ts], inc=ss)
                    for j in range(2):
                        p.emit("tensor",
                               lambda E, j=j: E.transpose(ps_tr[:], bc2[:, j * HD:(j + 1) * HD],
                                                          ident[:]),
                               waits=[ss, p0_ld], inc=ts)
                        p.emit("scalar",
                               lambda E, h=h, j=j: E.copy(kdcol[:, h * 2 + j:h * 2 + j + 1],
                                                          ps_tr[:, 0:1]),
                               waits=[ts], inc=ss)
                p.flush()

        XT3 = XT.rearrange("(j p) t -> p j t", p=HD)
        kfm_scr3 = kfm_scr.rearrange("(h p) t -> p h t", p=HD)
        x1_scr3 = x1_scr.rearrange("(j p) t -> p j t", p=HD)

        def early_out():
            p.emit("sync", lambda E: E.nop(), waits=bar(), inc=None)
            p.flush()
            return nc, outs

        if upto == "p0":
            return early_out()

        with ExitStack() as p1ctx:
            p1sb = lambda n, s, dt=F32: p1ctx.enter_context(nc.sbuf_tensor("p1_" + n, s, dt))
            xlnh = p1sb("xlnh", [HD, NCH * TPC], BF16)
            xlnl = p1sb("xlnl", [HD, NCH * TPC], BF16)
            lntmp = p1sb("lntmp", [HD, TPC])
            xa = [p1sb(f"xa{i}", [HD, TPC]) for i in range(2)]
            _sq0 = p1sb("sq0", [HD, TPC])
            sq = [_sq0, _sq0]
            kstage2 = [p1sb(f"kstage{i}", [HD, TPC]) for i in range(2)]
            vstage = p1sb("vstage", [HD, TPC])
            vtstage2 = [p1sb(f"vtstage{i}", [HD, 8 * HD]) for i in range(2)]
            ktm = p1sb("ktm", [HD, BLK])
            _kp0 = p1sb("kpan0", [HD, NCH * HD])
            _vp0 = p1sb("vpan0", [HD, NCH * HD])
            kpan = [_kp0, _kp0]
            vpan = [_vp0, _vp0]
            kbh = [p1sb(f"kbh{i}", [HD, NCH * HD], BF16) for i in range(2)]
            kbl = [p1sb(f"kbl{i}", [HD, NCH * HD], BF16) for i in range(2)]
            vbh = [p1sb(f"vbh{i}", [HD, NCH * HD], BF16) for i in range(2)]
            vbl = [p1sb(f"vbl{i}", [HD, NCH * HD], BF16) for i in range(2)]

            xa_free = [bar(), bar()]
            with ExitStack() as sctx:
                ps_s1 = [sctx.enter_context(nc.psum_tensor(f"ps_s1{u}", [1, 512], F32))
                         for u in range(2)]
                ps_s2 = [sctx.enter_context(nc.psum_tensor(f"ps_s2{u}", [1, 512], F32))
                         for u in range(2)]
                for j in range(NCH):
                    lv = dma("sync", xa[j % 2][:], XT3[:, j, :], ld, waits=xa_free[j % 2])
                    sqb = sq[j % 2]
                    p.emit("scalar", lambda E, j=j, sqb=sqb: E.square(sqb[:], xa[j % 2][:]),
                           waits=[lv, (ts, ts.n)], inc=ss)
                    for u in range(2):
                        p.emit("tensor", lambda E, j=j, u=u: E.matmul(
                            ps_s1[u][:], onescol[:], xa[j % 2][:, u * 512:(u + 1) * 512],
                            start=(j == 0), stop=(j == NCH - 1)),
                            waits=[lv, ss], inc=None)
                        p.emit("tensor", lambda E, j=j, u=u, sqb=sqb: E.matmul(
                            ps_s2[u][:], onescol[:], sqb[:, u * 512:(u + 1) * 512],
                            start=(j == 0), stop=(j == NCH - 1)),
                            waits=[ss], inc=(ts if u == 1 else None))
                    xa_free[j % 2] = [(ts, ts.n)]
                for u in range(2):
                    p.emit("scalar", lambda E, u=u: E.copy(row(0)[0:1, u * 512:(u + 1) * 512],
                                                           ps_s1[u][:]), waits=[ts], inc=ss)
                    p.emit("scalar", lambda E, u=u: E.copy(row(1)[0:1, u * 512:(u + 1) * 512],
                                                           ps_s2[u][:]), waits=[ts], inc=ss)
                p.flush()
            if upto == "stats0":
                return early_out()
            ln_rows_and_bcast(TPC, EPS_LN)
            p.emit("vector", lambda E: E.tensor_copy(rows2[0:1, 0:TPC], row(0)),
                   waits=[ss], inc=vs)
            p.emit("vector", lambda E: E.tensor_copy(rows2[0:1, TPC:2 * TPC], row(2)),
                   waits=[], inc=vs)
            if upto == "stats":
                return early_out()
            xa_free = [[(ts, ts.n)], [(ts, ts.n)]]
            for j in range(NCH):
                lv = dma("sync", xa[j % 2][:], XT3[:, j, :], ld, waits=xa_free[j % 2])
                p.emit("vector", lambda E, j=j: E.tensor_tensor(
                    lntmp[:], xa[j % 2][:], bc1[:], ALU.mult),
                    waits=[lv, ss], inc=vs)
                p.emit("vector", lambda E, j=j: E.tensor_tensor(
                    lntmp[:], lntmp[:], bc2[:], ALU.subtract), waits=[], inc=vs)
                p.emit("vector", lambda E, j=j: E.tensor_copy(
                    xlnh[:, j * TPC:(j + 1) * TPC], lntmp[:]), waits=[], inc=vs)
                p.emit("vector", lambda E, j=j: E.tensor_tensor(
                    xlnl[:, j * TPC:(j + 1) * TPC], lntmp[:],
                    xlnh[:, j * TPC:(j + 1) * TPC], ALU.subtract), waits=[], inc=vs)
                xa_free[j % 2] = [(vs, vs.n)]
            if upto == "xln":
                return early_out()


            p.emit("vector", lambda E: E.memset(kvst[:], 0.0), waits=[], inc=vs)
            with ExitStack() as dctx:
                ps_k = [dctx.enter_context(nc.psum_tensor(f"ps_k{i}", [HD, 512], F32))
                        for i in range(2)]
                ps_tr = [dctx.enter_context(nc.psum_tensor(f"ps_ktr{i}", [HD, HD], F32))
                         for i in range(2)]
                ps_w = dctx.enter_context(nc.psum_tensor("ps_w", [HD, HD], F32))
                kpan_free = [[(ts, ts.n)], [(ts, ts.n)]]
                vpan_free = [[(ts, ts.n)], [(ts, ts.n)]]
                kst_free = [(gp, gp.n), (ts, ts.n)]
                vst_free = [(ts, ts.n)]
                vts_free = [(gp, gp.n), (ts, ts.n)]
                xlnw = (vs, vs.n)
                tk = {
                    "kb": [bar(), bar()], "vb": [bar(), bar()],
                    "psk": [(ss, ss.n), (ss, ss.n)],
                    "kst": [bar(), bar()], "vts": [bar(), bar()],
                    "vst": [(ts, ts.n)],
                    "pstr": [(ss, ss.n), (ss, ss.n)],
                    "psw": [(vs, vs.n)],
                    "ktm": [(ts, ts.n), (ts, ts.n)],
                    "state": (vs, vs.n),
                }
                head_tok = {}
                xlnw = (vs, vs.n)
                p.emit("vector", lambda E: E.memset(kvst[:], 0.0), waits=[], inc=vs)

                def proj_head(h):
                    sh = h % 2
                    kp, vp = kpan[0], vpan[0]
                    lvk = dma("sync", kp[:], KTW[h * HD:(h + 1) * HD, :], ld,
                              waits=kpan_free[0])
                    lvv = dma("sync", vp[:], VTW[h * HD:(h + 1) * HD, :], ld,
                              waits=vpan_free[0])
                    kspl = split_tile(kp[:], kbh[sh][:], kbl[sh][:], [lvk, *tk["kb"][sh]])
                    vspl = split_tile(vp[:], vbh[sh][:], vbl[sh][:], [lvv, *tk["vb"][sh]])
                    kpan_free[0] = [(vs, kspl[1])]
                    vpan_free[0] = [(vs, vspl[1])]
                    for tt in range(2):
                        pb = ps_k[tt]
                        mm3(pb, kbh[sh], kbl[sh], xlnh, xlnl, tt, TPC,
                            [kspl, xlnw, tk["psk"][tt]])
                        p.emit("scalar", lambda E, tt=tt, pb=pb, sh=sh: E.activation(
                            kstage2[sh][:, tt * 512:(tt + 1) * 512], pb[:], AF.Silu),
                            waits=[ts, *tk["kst"][sh]], inc=ss)
                        tk["psk"][tt] = (ss, ss.n)
                    kevac = (ss, ss.n)
                    tk["kb"][sh] = [(ts, ts.n)]
                    ksp = dma("gpsimd", kfm_scr3[:, h, :], kstage2[sh][:], gp,
                              waits=[kevac])
                    for tt in range(2):
                        pb = ps_k[tt]
                        mm3(pb, vbh[sh], vbl[sh], xlnh, xlnl, tt, TPC,
                            [vspl, tk["psk"][tt]])
                        p.emit("scalar", lambda E, tt=tt, pb=pb: E.activation(
                            vstage[:, tt * 512:(tt + 1) * 512], pb[:], AF.Silu),
                            waits=[ts, *tk["vst"]], inc=ss)
                        tk["psk"][tt] = (ss, ss.n)
                    tk["vb"][sh] = [(ts, ts.n)]
                    for tc in range(8):
                        pt = ps_tr[tc % 2]
                        p.emit("tensor", lambda E, tc=tc, pt=pt: E.transpose(
                            pt[:], vstage[:, tc * HD:(tc + 1) * HD], ident[:]),
                            waits=[(ss, ss.n), tk["pstr"][tc % 2]], inc=ts)
                        trv = (ts, ts.n)
                        p.emit("scalar", lambda E, tc=tc, pt=pt, sh=sh: E.copy(
                            vtstage2[sh][:, tc * HD:(tc + 1) * HD], pt[:]),
                            waits=[trv] + (list(tk["vts"][sh]) if tc == 0 else []), inc=ss)
                        tk["pstr"][tc % 2] = (ss, ss.n)
                    tk["vst"] = [(ts, ts.n)]
                    vtev = (ss, ss.n)
                    vsp = dma("gpsimd",
                              vtm_scr[:, h * HD:(h + 1) * HD].rearrange(
                                  "(tc p) e -> p tc e", p=HD),
                              vtstage2[sh][:].rearrange("p (tc e) -> p tc e", tc=8),
                              gp, waits=[vtev])
                    head_tok[h] = (kevac, vtev, ksp, vsp)

                def state_head(h):
                    sh = h % 2
                    kevac, vtev, ksp, vsp = head_tok[h]
                    for b in range(NBLK):
                        for j in range(2):
                            p.emit("tensor", lambda E, b=b, j=j, sh=sh: E.transpose(
                                ps_tr[j][:],
                                kstage2[sh][:, b * BLK + j * HD:b * BLK + (j + 1) * HD],
                                ident[:]), waits=[kevac, tk["pstr"][j]], inc=ts)
                            trv = (ts, ts.n)
                            p.emit("scalar", lambda E, j=j: E.copy(
                                ktm[:, j * HD:(j + 1) * HD], ps_tr[j][:]),
                                waits=[trv, tk["ktm"][j]], inc=ss)
                            tk["pstr"][j] = (ss, ss.n)
                            p.emit("vector", lambda E, h=h, j=j: E.tensor_scalar(
                                ktm[:, j * HD:(j + 1) * HD], ktm[:, j * HD:(j + 1) * HD],
                                kdcol[:, h * 2 + j:h * 2 + j + 1], None, ALU.mult),
                                waits=[(ss, ss.n)], inc=vs)
                        kkv = (vs, vs.n)
                        for j in range(2):
                            p.emit("tensor", lambda E, h=h, b=b, j=j, sh=sh: E.matmul(
                                ps_w[:], ktm[:, j * HD:(j + 1) * HD],
                                vtstage2[sh][:, (2 * b + j) * HD:(2 * b + j + 1) * HD],
                                start=(j == 0), stop=(j == 1)),
                                waits=[kkv, vtev, *(tk["psw"] if j == 0 else [])],
                                inc=(ts if j == 1 else None))
                        wmm = (ts, ts.n)
                        tk["ktm"][0] = wmm
                        tk["ktm"][1] = wmm
                        p.emit("vector", lambda E, h=h: E.tensor_scalar(
                            kvst[:, h * HD:(h + 1) * HD], kvst[:, h * HD:(h + 1) * HD],
                            bdt[:, h:h + 1], None, ALU.mult), waits=[tk["state"]], inc=vs)
                        p.emit("vector", lambda E, h=h: E.tensor_tensor(
                            kvst[:, h * HD:(h + 1) * HD], kvst[:, h * HD:(h + 1) * HD],
                            ps_w[:], ALU.add), waits=[wmm], inc=vs)
                        tk["state"] = (vs, vs.n)
                        tk["psw"] = [tk["state"]]
                    tk["kst"][sh] = [(gp, ksp[1]), (ts, ts.n)]
                    tk["vts"][sh] = [(gp, vsp[1]), (ts, ts.n)]

                for h in range(nheads):
                    proj_head(h)
                    if h > 0:
                        state_head(h - 1)
                state_head(nheads - 1)
                p.flush()
            if upto == "noag":
                return early_out()
            ag_in3 = ag_in.rearrange("(h d) e -> d h e", d=HD)
            agv = dma("sync", ag_in3, kvst[:].rearrange("d (h e) -> d h e", h=NH), ld,
                      waits=[(vs, vs.n)])
            p.emit("gpsimd", lambda E: E.collective_compute(
                "AllGather", ALU.bypass, replica_groups=[list(range(NCORES))],
                ins=[ag_in.opt()], outs=[ag_out.opt()]),
                waits=[agv], inc=cc)
            p.flush()

        def emit_kv_init(slab_buf=None):
            with ExitStack() as ectx:
                s0 = (slab_buf if slab_buf is not None else
                      ectx.enter_context(nc.sbuf_tensor("sslab0", [HD, NH * HD], F32)))
                sslab = [s0, s0]
                p.emit("vector", lambda E: E.memset(kvst[:], 0.0), waits=[(cc, cc.n)], inc=vs)
                slab_free = bar()
                for c in range(NCORES):
                    srcap = ag_out[c * NH * HD:(c + 1) * NH * HD, :].rearrange(
                        "(h d) e -> d h e", d=HD)
                    lv = dma("sync", sslab[0][:].rearrange("d (h e) -> d h e", h=NH),
                             srcap, ld, waits=[(cc, cc.n), *slab_free])
                    for h in range(NH):
                        p.emit("vector", lambda E, c=c, h=h: E.scalar_tensor_tensor(
                            kvst[:, h * HD:(h + 1) * HD], sslab[0][:, h * HD:(h + 1) * HD],
                            wcomb[:, c * NH + h:c * NH + h + 1], kvst[:, h * HD:(h + 1) * HD],
                            ALU.mult, ALU.add), waits=[lv], inc=vs)
                    slab_free = [(vs, vs.n)]
                if dbg:
                    dma("gpsimd", KV0, kvst[:], gp, waits=[(vs, vs.n)])
                p.flush()

        if stage == "p1":
            emit_kv_init()
            p.emit("sync", lambda E: E.nop(), waits=bar(), inc=None)
            p.flush()
            return nc, outs

        state_done = {}
        for hh in range(2):
            with ExitStack() as hctx:
                hsb = lambda name, shape: hctx.enter_context(
                    nc.sbuf_tensor(f"h{hh}_{name}", shape, F32))
                xha = [hsb(f"xha{i}", [HD, HT]) for i in range(2)]
                hbsb = lambda name, shape: hctx.enter_context(
                    nc.sbuf_tensor(f"h{hh}_{name}", shape, BF16))
                xyh = hbsb("xyh", [HD, NCH * HT])
                xyl = hbsb("xyl", [HD, NCH * HT])
                lnt2 = hsb("lnt2", [HD, HT])
                qbuf = hsb("qbuf", [HD, NH * HT])
                obuf = hsb("obuf", [HD, NH * HT])
                _sqh0 = hsb("sqh0", [HD, HT])
                sqh = [_sqh0, _sqh0]
                pan = [hsb(f"pan{i}", [HD, NCH * HD]) for i in range(2)]
                pbh = [hbsb(f"pbh{i}", [HD, NCH * HD]) for i in range(2)]
                pbl = [hbsb(f"pbl{i}", [HD, NCH * HD]) for i in range(2)]
                kre2 = [hsb(f"kre2_{i}", [HD, 2 * BLK]) for i in range(2)]
                vre2 = [hsb(f"vre2_{i}", [HD, 2 * BLK]) for i in range(2)]
                ktm2 = [hsb(f"ktm2_{i}", [HD, BLK]) for i in range(2)]
                qq2 = [hsb(f"qq2_{i}", [HD, BLK]) for i in range(2)]
                sdt2 = [hsb(f"sdt2_{i}", [HD, 2 * BLK]) for i in range(2)]
                dtt2 = [hsb(f"dtt2_{i}", [HD, 2 * BLK]) for i in range(2)]
                qdb2 = [hsb(f"qdb2_{i}", [HD, BLK]) for i in range(2)]
                gtile = hsb("gtile", [HD, HT])

                xa_free2 = [bar(), bar()]
                p.emit("vector", lambda E: E.tensor_tensor(
                    row(3, HT), rows2[0:1, hh * HT:(hh + 1) * HT],
                    rows2[0:1, TPC + hh * HT:TPC + (hh + 1) * HT], ALU.mult),
                    waits=bar(), inc=vs)
                with nc.psum_tensor(f"ps_bx{hh}", [HD, 512], F32) as ps_b:
                    p.emit("tensor", lambda E: E.matmul(
                        ps_b[:], onesrow[:], rows2[0:1, TPC + hh * HT:TPC + (hh + 1) * HT],
                        start=True, stop=True), waits=[vs, ss], inc=ts)
                    p.emit("scalar", lambda E: E.copy(bc1[:, 0:HT], ps_b[:]), waits=[ts], inc=ss)
                    p.emit("tensor", lambda E: E.matmul(
                        ps_b[:], onesrow[:], row(3, HT), start=True, stop=True),
                        waits=[vs, ss], inc=ts)
                    p.emit("scalar", lambda E: E.copy(bc2[:, 0:HT], ps_b[:]), waits=[ts], inc=ss)
                    p.flush()
                for j in range(NCH):
                    lv = dma("sync", xha[j % 2][:], XT3[:, j, hh * HT:(hh + 1) * HT],
                             ld, waits=xa_free2[j % 2])
                    p.emit("vector", lambda E, j=j: E.tensor_tensor(
                        lnt2[:], xha[j % 2][:],
                        bc1[:, 0:HT], ALU.mult), waits=[lv, ss], inc=vs)
                    p.emit("vector", lambda E, j=j: E.tensor_tensor(
                        lnt2[:], lnt2[:], bc2[:, 0:HT], ALU.subtract), waits=[], inc=vs)
                    split_tile(lnt2[:], xyh[:, j * HT:(j + 1) * HT],
                               xyl[:, j * HT:(j + 1) * HT], [])
                    xa_free2[j % 2] = [(vs, vs.n)]

                with ExitStack() as qctx:
                    ps_q = [qctx.enter_context(nc.psum_tensor(f"ps_q{hh}{i}", [HD, 512], F32))
                            for i in range(2)]
                    pan_free = [[(ts, ts.n)], [(ts, ts.n)]]
                    pb_free = [bar(), bar()]
                    ps_free = [(ss, ss.n), (ss, ss.n)]
                    for h in range(NH):
                        wp = pan[h % 2]
                        lv = dma("sync", wp[:], QTW[h * HD:(h + 1) * HD, :], ld,
                                 waits=pan_free[h % 2])
                        spl = split_tile(wp[:], pbh[h % 2][:], pbl[h % 2][:],
                                         [lv, *pb_free[h % 2]])
                        pan_free[h % 2] = [spl]
                        pb = ps_q[h % 2]
                        mm3(pb, pbh[h % 2], pbl[h % 2], xyh, xyl, 0, HT,
                            [spl, ps_free[h % 2]])
                        p.emit("scalar", lambda E, h=h, pb=pb: E.activation(
                            qbuf[:, h * HT:(h + 1) * HT], pb[:], AF.Silu),
                            waits=[ts, (gp, gp.n)], inc=ss)
                        ps_free[h % 2] = (ss, ss.n)
                        pb_free[h % 2] = [(ts, ts.n)]
                    p.flush()

                if hh == 0:
                    emit_kv_init(obuf[:, 0:NH * HD])
                    for _h in range(NH):
                        state_done[_h] = (vs, vs.n)

                with ExitStack() as kctx:
                    ps_st = [kctx.enter_context(nc.psum_tensor(f"ps_st{hh}{i}", [HD, 2 * BLK], F32))
                             for i in range(2)]
                    ps_o = [kctx.enter_context(nc.psum_tensor(f"ps_o{hh}{i}", [HD, BLK], F32))
                            for i in range(2)]
                    ps_tr = [kctx.enter_context(nc.psum_tensor(f"ps_tr{hh}{i}", [HD, BLK], F32))
                             for i in range(2)]
                    ps_w = [kctx.enter_context(nc.psum_tensor(f"ps_w{hh}{i}", [HD, HD], F32))
                            for i in range(2)]
                    cur = lambda sem: (sem, sem.n)
                    tok = {k: [bar(), bar()] for k in
                           ("kre", "vre", "psst", "pso", "pstr", "psw",
                            "qdb", "dtt", "qq", "sdt", "ktm", "rowq")}
                    head_info = {}

                    def stage1(h, bb, su):
                        sh = h % 2
                        if bb == 0:
                            lvk = dma("sync", kre2[sh][:],
                                      kfm_scr3[:, h, hh * HT:(hh + 1) * HT], ld,
                                      waits=tok["kre"][sh])
                            lvv = dma("sync",
                                      vre2[sh][:].rearrange("p (tc e) -> p tc e", tc=4),
                                      vtm_scr[hh * HT:(hh + 1) * HT,
                                              h * HD:(h + 1) * HD].rearrange(
                                          "(tc p) e -> p tc e", p=HD),
                                      ld, waits=tok["vre"][sh])
                            p.emit("scalar", lambda E, h=h, sh=sh: E.activation(
                                rows[0:1, (2 + sh) * TPC:(2 + sh) * TPC + BLK], arrq[:],
                                AF.Exp, scale=sbneg[0:1, h:h + 1]),
                                waits=tok["rowq"][sh], inc=ss)
                            qdrow = cur(ss)
                            p.emit("tensor", lambda E, su=su, sh=sh: E.matmul(
                                ps_o[su][:], onesrow[:],
                                rows[0:1, (2 + sh) * TPC:(2 + sh) * TPC + BLK],
                                start=True, stop=True),
                                waits=[qdrow, *tok["pso"][su]], inc=ts)
                            bqmm = cur(ts)
                            tok["rowq"][sh] = [bqmm]
                            p.emit("scalar", lambda E, sh=sh, su=su: E.copy(
                                qdb2[sh][:], ps_o[su][:]),
                                waits=[bqmm, *tok["qdb"][sh]], inc=ss)
                            qdbev = cur(ss)
                            for j in range(2):
                                p.emit("scalar", lambda E, h=h, sh=sh, j=j: E.activation(
                                    dtt2[sh][:, j * BLK:(j + 1) * BLK],
                                    idxt[:, j * BLK:(j + 1) * BLK],
                                    AF.Exp, scale=sbneg[:, h:h + 1]),
                                    waits=tok["dtt"][sh] if j == 0 else [], inc=ss)
                            dttev = cur(ss)
                            head_info[h] = (lvk, lvv, qdbev, dttev)
                        lvk, lvv, qdbev, dttev = head_info[h]
                        qcols = qbuf[:, h * HT + bb * BLK:h * HT + (bb + 1) * BLK]
                        p.emit("vector", lambda E, sh=sh, su=su, qcols=qcols: E.tensor_tensor(
                            qq2[su][:], qcols, qdb2[sh][:], ALU.mult),
                            waits=[qdbev, *tok["qq"][su]], inc=vs)
                        qqv = cur(vs)
                        for j in range(2):
                            p.emit("tensor", lambda E, sh=sh, su=su, j=j, bb=bb, qcols=qcols:
                                   E.matmul(
                                ps_st[su][:, j * BLK:(j + 1) * BLK],
                                kre2[sh][:, bb * BLK + j * HD:bb * BLK + (j + 1) * HD],
                                qcols, start=(j == 0), stop=(j == 1)),
                                waits=[lvk, *(tok["psst"][su] if j == 0 else [])],
                                inc=(ts if j == 1 else None))
                        stmm = cur(ts)
                        return qqv, stmm

                    def stage2(h, bb, su, qqv, stmm):
                        sh = h % 2
                        lvk, lvv, qdbev, dttev = head_info[h]
                        kv_h = kvst[:, h * HD:(h + 1) * HD]
                        for j in range(2):
                            p.emit("vector", lambda E, sh=sh, su=su, j=j: E.tensor_tensor(
                                sdt2[su][:, j * BLK:(j + 1) * BLK],
                                ps_st[su][:, j * BLK:(j + 1) * BLK],
                                dtt2[sh][:, j * BLK:(j + 1) * BLK], ALU.mult),
                                waits=[stmm, dttev, *(tok["sdt"][su] if j == 0 else [])],
                                inc=vs)
                        sdtv = cur(vs)
                        tok["psst"][su] = [sdtv]
                        p.emit("tensor", lambda E, su=su, kv_h=kv_h: E.matmul(
                            ps_o[su][:], kv_h, qq2[su][:], start=True, stop=False),
                            waits=[qqv, state_done[h], qdbev, *tok["pso"][su]], inc=None)
                        for j in range(2):
                            p.emit("tensor", lambda E, sh=sh, su=su, j=j, bb=bb: E.matmul(
                                ps_o[su][:],
                                vre2[sh][:, (bb * 2 + j) * HD:(bb * 2 + j + 1) * HD],
                                sdt2[su][:, j * BLK:(j + 1) * BLK],
                                start=False, stop=(j == 1)),
                                waits=[lvv, sdtv], inc=(ts if j == 1 else None))
                        o_done = cur(ts)
                        tok["qq"][su] = [o_done]
                        tok["sdt"][su] = [o_done]
                        p.emit("scalar", lambda E, su=su, h=h, bb=bb: E.copy(
                            obuf[:, h * HT + bb * BLK:h * HT + (bb + 1) * BLK],
                            ps_o[su][:]), waits=[o_done], inc=ss)
                        oev = cur(ss)
                        tok["pso"][su] = [oev]
                        for j in range(2):
                            p.emit("tensor", lambda E, sh=sh, su=su, j=j, bb=bb: E.transpose(
                                ps_tr[su][:, j * HD:(j + 1) * HD],
                                kre2[sh][:, bb * BLK + j * HD:bb * BLK + (j + 1) * HD],
                                ident[:]),
                                waits=[lvk, *(tok["pstr"][su] if j == 0 else [])], inc=ts)
                        trmm = cur(ts)
                        for j in range(2):
                            p.emit("scalar", lambda E, su=su, j=j: E.copy(
                                ktm2[su][:, j * HD:(j + 1) * HD],
                                ps_tr[su][:, j * HD:(j + 1) * HD]),
                                waits=[trmm, *(tok["ktm"][su] if j == 0 else [])], inc=ss)
                        ktmev = cur(ss)
                        tok["pstr"][su] = [ktmev]
                        for j in range(2):
                            p.emit("vector", lambda E, su=su, h=h, j=j: E.tensor_scalar(
                                ktm2[su][:, j * HD:(j + 1) * HD],
                                ktm2[su][:, j * HD:(j + 1) * HD],
                                kdcol[:, h * 2 + j:h * 2 + j + 1], None, ALU.mult),
                                waits=[ktmev] if j == 0 else [], inc=vs)
                        kkv = cur(vs)
                        for j in range(2):
                            p.emit("tensor", lambda E, sh=sh, su=su, j=j, bb=bb: E.matmul(
                                ps_w[su][:], ktm2[su][:, j * HD:(j + 1) * HD],
                                vre2[sh][:, (bb * 2 + j) * HD:(bb * 2 + j + 1) * HD],
                                start=(j == 0), stop=(j == 1)),
                                waits=[kkv, lvv, *(tok["psw"][su] if j == 0 else [])],
                                inc=(ts if j == 1 else None))
                        wmm = cur(ts)
                        tok["ktm"][su] = [wmm]
                        p.emit("vector", lambda E, h=h, kv_h=kv_h: E.tensor_scalar(
                            kv_h, kv_h, bdt[:, h:h + 1], None, ALU.mult),
                            waits=[o_done], inc=vs)
                        p.emit("vector", lambda E, su=su, kv_h=kv_h: E.tensor_tensor(
                            kv_h, kv_h, ps_w[su][:], ALU.add), waits=[wmm], inc=vs)
                        state_done[h] = (vs, vs.n)
                        tok["psw"][su] = [state_done[h]]
                        if bb == 1:
                            tok["kre"][sh] = [(ts, trmm[1] if trmm[1] > wmm[1] else wmm[1])]
                            tok["vre"][sh] = [wmm]
                            tok["qdb"][sh] = [qqv]
                            tok["dtt"][sh] = [sdtv]

                    units = [(h, bb) for h in range(NH) for bb in range(2)]
                    pend = None
                    for ui, (h, bb) in enumerate(units):
                        s1 = stage1(h, bb, ui % 2)
                        if pend is not None:
                            stage2(*pend)
                        pend = (h, bb, ui % 2, *s1)
                    stage2(*pend)
                    p.flush()

                if dbg:
                    for h in range(NH):
                        dma("gpsimd", ODBG[:, (h * 2 + hh) * HT:(h * 2 + hh + 1) * HT],
                            obuf[:, h * HT:(h + 1) * HT], gp, waits=[(ss, ss.n), (vs, vs.n)])
                with ExitStack() as sctx:
                    ps_s2 = sctx.enter_context(nc.psum_tensor(f"ps_rms{hh}", [1, 512], F32))
                    for h in range(NH):
                        sqb = sqh[h % 2]
                        p.emit("scalar", lambda E, h=h, sqb=sqb: E.square(
                            sqb[:], obuf[:, h * HT:(h + 1) * HT]),
                            waits=[(vs, vs.n), (ts, ts.n)], inc=ss)
                        p.emit("tensor", lambda E, h=h, sqb=sqb: E.matmul(
                            ps_s2[:], onescol[:], sqb[:],
                            start=(h == 0), stop=(h == NH - 1)),
                            waits=[ss], inc=ts)
                    p.emit("scalar", lambda E: E.copy(row(1, HT), ps_s2[:]), waits=[ts], inc=ss)
                    p.flush()
                ln_rows_and_bcast(HT, EPS_RMS, rms=True)
                for h in range(NH):
                    p.emit("vector", lambda E, h=h: E.tensor_tensor(
                        obuf[:, h * HT:(h + 1) * HT], obuf[:, h * HT:(h + 1) * HT],
                        bc1[:, 0:HT], ALU.mult), waits=[ss, (gp, gp.n)], inc=vs)
                    p.emit("vector", lambda E, h=h: E.tensor_scalar(
                        obuf[:, h * HT:(h + 1) * HT], obuf[:, h * HT:(h + 1) * HT],
                        nwc[:, h:h + 1], None, ALU.mult), waits=[], inc=vs)

                with ExitStack() as gctx:
                    ps_g = [gctx.enter_context(nc.psum_tensor(f"ps_g{hh}{i}", [HD, 512], F32))
                            for i in range(2)]
                    pan_free = [[(ts, ts.n)], [(ts, ts.n)]]
                    pb_free = [bar(), bar()]
                    ps_free = [(ss, ss.n), (ss, ss.n)]
                    for oc in range(NCH):
                        wp = pan[oc % 2]
                        lv = dma("sync", wp[:], WGT[oc * HD:(oc + 1) * HD, :], ld,
                                 waits=pan_free[oc % 2])
                        spl = split_tile(wp[:], pbh[oc % 2][:], pbl[oc % 2][:],
                                         [lv, *pb_free[oc % 2]])
                        pan_free[oc % 2] = [spl]
                        pb = ps_g[oc % 2]
                        mm3(pb, pbh[oc % 2], pbl[oc % 2], xyh, xyl, 0, HT,
                            [spl, ps_free[oc % 2]])
                        p.emit("scalar", lambda E, pb=pb: E.activation(
                            gtile[:], pb[:], AF.Sigmoid), waits=[ts, (vs, vs.n)], inc=ss)
                        p.emit("vector", lambda E, oc=oc: E.tensor_tensor(
                            obuf[:, oc * HT:(oc + 1) * HT], obuf[:, oc * HT:(oc + 1) * HT],
                            gtile[:], ALU.mult), waits=[ss], inc=vs)
                        ps_free[oc % 2] = (ss, ss.n)
                        pb_free[oc % 2] = [(ts, ts.n)]
                    p.flush()

                gospl = bar()
                for oc in range(NCH):
                    split_tile(obuf[:, oc * HT:(oc + 1) * HT],
                               xyh[:, oc * HT:(oc + 1) * HT],
                               xyl[:, oc * HT:(oc + 1) * HT],
                               gospl if oc == 0 else [])
                gosplv = (vs, vs.n)

                if dbg:
                    for h in range(NH):
                        dma("gpsimd", GODBG[:, (h * 2 + hh) * HT:(h * 2 + hh + 1) * HT],
                            obuf[:, h * HT:(h + 1) * HT], gp, waits=[(ss, ss.n), (vs, vs.n)])
                with ExitStack() as octx:
                    ps_y = [octx.enter_context(nc.psum_tensor(f"ps_y{hh}{i}", [HD, 512], F32))
                            for i in range(2)]
                    ps_s1 = octx.enter_context(nc.psum_tensor(f"ps_ys1{hh}", [1, 512], F32))
                    ps_s2 = octx.enter_context(nc.psum_tensor(f"ps_ys2{hh}", [1, 512], F32))
                    pan_free = [[(ts, ts.n)], [(ts, ts.n)]]
                    pb_free = [bar(), bar()]
                    ps_free = [(ss, ss.n), (ss, ss.n)]
                    for fc in range(NCH):
                        wp = pan[fc % 2]
                        lv = dma("sync", wp[:], WOT[fc * HD:(fc + 1) * HD, :], ld,
                                 waits=pan_free[fc % 2])
                        spl = split_tile(wp[:], pbh[fc % 2][:], pbl[fc % 2][:],
                                         [lv, gosplv, *pb_free[fc % 2]])
                        pan_free[fc % 2] = [spl]
                        pb = ps_y[fc % 2]
                        mm3(pb, pbh[fc % 2], pbl[fc % 2], xyh, xyl, 0, HT,
                            [spl, ps_free[fc % 2]])
                        pb_free[fc % 2] = [(ts, ts.n)]
                        p.emit("scalar", lambda E, fc=fc, pb=pb: E.copy(
                            qbuf[:, fc * HT:(fc + 1) * HT], pb[:]), waits=[ts], inc=ss)
                        sqb = sqh[fc % 2]
                        p.emit("scalar", lambda E, fc=fc, sqb=sqb: E.square(
                            sqb[:], qbuf[:, fc * HT:(fc + 1) * HT]),
                            waits=[(ts, ts.n)], inc=ss)
                        p.emit("tensor", lambda E, fc=fc: E.matmul(
                            ps_s1[:], onescol[:], qbuf[:, fc * HT:(fc + 1) * HT],
                            start=(fc == 0), stop=(fc == NCH - 1)), waits=[ss], inc=None)
                        p.emit("tensor", lambda E, fc=fc, sqb=sqb: E.matmul(
                            ps_s2[:], onescol[:], sqb[:],
                            start=(fc == 0), stop=(fc == NCH - 1)),
                            waits=[ss], inc=ts)
                        ps_free[fc % 2] = (ss, ss.n)
                        pan_free[fc % 2] = [(ts, ts.n)]
                    p.emit("scalar", lambda E: E.copy(row(0, HT), ps_s1[:]), waits=[ts], inc=ss)
                    p.emit("scalar", lambda E: E.copy(row(1, HT), ps_s2[:]), waits=[ts], inc=ss)
                    p.flush()
                if dbg:
                    for j in range(NCH):
                        dma("gpsimd", YDBG[:, (j * 2 + hh) * HT:(j * 2 + hh + 1) * HT],
                            qbuf[:, j * HT:(j + 1) * HT], gp, waits=[(ss, ss.n)])
                ln_rows_and_bcast(HT, EPS_LN)
                for j in range(NCH):
                    lv = dma("sync", xha[j % 2][:], XT3[:, j, hh * HT:(hh + 1) * HT],
                             ld, waits=xa_free2[j % 2])
                    p.emit("vector", lambda E, j=j: E.tensor_tensor(
                        qbuf[:, j * HT:(j + 1) * HT], qbuf[:, j * HT:(j + 1) * HT],
                        bc1[:, 0:HT], ALU.mult), waits=[ss, (gp, gp.n)], inc=vs)
                    p.emit("vector", lambda E, j=j: E.tensor_tensor(
                        qbuf[:, j * HT:(j + 1) * HT], qbuf[:, j * HT:(j + 1) * HT],
                        bc2[:, 0:HT], ALU.subtract), waits=[], inc=vs)
                    p.emit("vector", lambda E, j=j: E.tensor_tensor(
                        qbuf[:, j * HT:(j + 1) * HT], qbuf[:, j * HT:(j + 1) * HT],
                        xha[j % 2][:], ALU.add), waits=[lv], inc=vs)
                    dma("gpsimd", x1_scr3[:, j, hh * HT:(hh + 1) * HT],
                        qbuf[:, j * HT:(j + 1) * HT], gp, waits=[(vs, vs.n)])
                    xa_free2[j % 2] = [(vs, vs.n)]
                p.flush()

        if stage == "p2":
            p.emit("sync", lambda E: E.nop(), waits=bar(), inc=None)
            p.flush()
            return nc, outs

        OUTT3 = OUTT.rearrange("(j p) t -> p j t", p=HD)
        nic = IGRP // HD
        with ExitStack() as fctx:
            fsb = lambda name, shape, dt=F32: fctx.enter_context(
                nc.sbuf_tensor(f"f_{name}", shape, dt))
            ln2h = fsb("ln2h", [HD, NCH * TPC], BF16)
            ln2l = fsb("ln2l", [HD, NCH * TPC], BF16)
            y2 = fsb("y2", [HD, NCH * HT])
            x1t = [fsb(f"x1t{i}", [HD, TPC]) for i in range(2)]
            sqf = [fsb(f"sqf{i}", [HD, TPC]) for i in range(2)]
            w8 = [fsb(f"w8_{i}", [HD, NCH * IGRP], FP8) for i in range(2)]
            gall = fsb("gall", [HD, nic * HT])
            utmp = fsb("utmp", [HD, HT])
            fh = fsb("fh", [HD, nic * HT], BF16)
            fl = fsb("fl", [HD, nic * HT], BF16)
            lntmp = fsb("lntmp", [HD, TPC])
            x1_free = [bar(), bar()]
            with ExitStack() as sctx:
                ps_s1 = [sctx.enter_context(nc.psum_tensor(f"ps3s1{u}", [1, 512], F32))
                         for u in range(2)]
                ps_s2 = [sctx.enter_context(nc.psum_tensor(f"ps3s2{u}", [1, 512], F32))
                         for u in range(2)]
                for j in range(NCH):
                    lv = dma("sync", x1t[j % 2][:], x1_scr3[:, j, :],
                             ld, waits=[*x1_free[j % 2], (gp, gp.n)])
                    sqb = sqf[j % 2]
                    p.emit("scalar", lambda E, j=j, sqb=sqb: E.square(
                        sqb[:], x1t[j % 2][:]), waits=[lv, (ts, ts.n)], inc=ss)
                    for u in range(2):
                        p.emit("tensor", lambda E, j=j, u=u: E.matmul(
                            ps_s1[u][:], onescol[:], x1t[j % 2][:, u * 512:(u + 1) * 512],
                            start=(j == 0), stop=(j == NCH - 1)), waits=[lv, ss], inc=None)
                        p.emit("tensor", lambda E, j=j, u=u, sqb=sqb: E.matmul(
                            ps_s2[u][:], onescol[:], sqb[:, u * 512:(u + 1) * 512],
                            start=(j == 0), stop=(j == NCH - 1)),
                            waits=[ss], inc=(ts if u == 1 else None))
                    x1_free[j % 2] = [(ts, ts.n)]
                for u in range(2):
                    p.emit("scalar", lambda E, u=u: E.copy(row(0)[0:1, u * 512:(u + 1) * 512],
                                                           ps_s1[u][:]), waits=[ts], inc=ss)
                    p.emit("scalar", lambda E, u=u: E.copy(row(1)[0:1, u * 512:(u + 1) * 512],
                                                           ps_s2[u][:]), waits=[ts], inc=ss)
                p.flush()
            ln_rows_and_bcast(TPC, EPS_LN)
            for j in range(NCH):
                lv = dma("sync", x1t[j % 2][:], x1_scr3[:, j, :], ld, waits=x1_free[j % 2])
                p.emit("vector", lambda E, j=j: E.tensor_tensor(
                    lntmp[:], x1t[j % 2][:], bc1[:], ALU.mult),
                    waits=[lv, ss, (ts, ts.n)], inc=vs)
                p.emit("vector", lambda E, j=j: E.tensor_tensor(
                    lntmp[:], lntmp[:], bc2[:], ALU.subtract), waits=[], inc=vs)
                p.emit("vector", lambda E, j=j: E.tensor_copy(
                    ln2h[:, j * TPC:(j + 1) * TPC], lntmp[:]), waits=[], inc=vs)
                p.emit("vector", lambda E, j=j: E.tensor_tensor(
                    ln2l[:, j * TPC:(j + 1) * TPC], lntmp[:],
                    ln2h[:, j * TPC:(j + 1) * TPC], ALU.subtract), waits=[], inc=vs)
                x1_free[j % 2] = [(vs, vs.n)]

            with ExitStack() as wctx:
                ps_gu = [wctx.enter_context(
                    nc.psum_tensor(f"ps_gu{i}", [HD, 512], F32)) for i in range(2)]
                ps_d = [wctx.enter_context(
                    nc.psum_tensor(f"ps_d{i}", [HD, 512], F32)) for i in range(2)]
                slab_free = [[(ts, ts.n)], [(ts, ts.n)]]
                slot = 0
                lnw = (vs, vs.n)
                gu_free = [(ss, ss.n), (ss, ss.n)]
                d_free = [[(ss, ss.n)], [(ss, ss.n)]]
                for hh in range(2):
                    for ig in range(NIG):
                        sg = slot; slot = (slot + 1) % 2
                        lvg = dma("gpsimd", w8[sg][:], GW[ig * HD:(ig + 1) * HD, :], gp,
                                  waits=slab_free[sg])
                        for ic in range(nic):
                            pb = ps_gu[ic % 2]
                            for pi, lnb in enumerate((ln2h, ln2l)):
                                for j in range(NCH):
                                    p.emit("tensor", lambda E, ig=ig, ic=ic, j=j, pb=pb,
                                           lnb=lnb, sg=sg, pi=pi, hh=hh:
                                           E.matmul(pb[:],
                                                    w8[sg][:, j * IGRP + ic * HD:j * IGRP + (ic + 1) * HD],
                                                    lnb[:, j * TPC + hh * HT:j * TPC + (hh + 1) * HT],
                                                    start=(pi == 0 and j == 0),
                                                    stop=(pi == 1 and j == NCH - 1)),
                                           waits=[lvg, lnw, gu_free[ic % 2]],
                                           inc=(ts if (pi == 1 and j == NCH - 1) else None))
                            p.emit("scalar", lambda E, ig=ig, ic=ic, pb=pb: E.activation(
                                gall[:, ic * HT:(ic + 1) * HT], pb[:],
                                AF.Silu, scale=gsc[:, ig * nic + ic:ig * nic + ic + 1]),
                                waits=[ts], inc=ss)
                            gu_free[ic % 2] = (ss, ss.n)
                        slab_free[sg] = [(ts, ts.n)]
                        su = slot; slot = (slot + 1) % 2
                        lvu = dma("gpsimd", w8[su][:], UW[ig * HD:(ig + 1) * HD, :], gp,
                                  waits=slab_free[su])
                        for ic in range(nic):
                            pb = ps_gu[ic % 2]
                            for pi, lnb in enumerate((ln2h, ln2l)):
                                for j in range(NCH):
                                    p.emit("tensor", lambda E, ig=ig, ic=ic, j=j, pb=pb,
                                           lnb=lnb, su=su, pi=pi, hh=hh:
                                           E.matmul(pb[:],
                                                    w8[su][:, j * IGRP + ic * HD:j * IGRP + (ic + 1) * HD],
                                                    lnb[:, j * TPC + hh * HT:j * TPC + (hh + 1) * HT],
                                                    start=(pi == 0 and j == 0),
                                                    stop=(pi == 1 and j == NCH - 1)),
                                           waits=[lvu, gu_free[ic % 2]],
                                           inc=(ts if (pi == 1 and j == NCH - 1) else None))
                            p.emit("scalar", lambda E, ig=ig, ic=ic, pb=pb: E.activation(
                                utmp[:], pb[:], AF.Copy,
                                scale=usc[:, ig * nic + ic:ig * nic + ic + 1]),
                                waits=[ts, (vs, vs.n)], inc=ss)
                            gu_free[ic % 2] = (ss, ss.n)
                            p.emit("vector", lambda E, ic=ic: E.tensor_tensor(
                                gall[:, ic * HT:(ic + 1) * HT], gall[:, ic * HT:(ic + 1) * HT],
                                utmp[:], ALU.mult), waits=[ss], inc=vs)
                            p.emit("vector", lambda E, ic=ic: E.tensor_copy(
                                fh[:, ic * HT:(ic + 1) * HT], gall[:, ic * HT:(ic + 1) * HT]),
                                waits=[], inc=vs)
                            p.emit("vector", lambda E, ic=ic: E.tensor_tensor(
                                fl[:, ic * HT:(ic + 1) * HT], gall[:, ic * HT:(ic + 1) * HT],
                                fh[:, ic * HT:(ic + 1) * HT], ALU.subtract), waits=[], inc=vs)
                        slab_free[su] = [(ts, ts.n)]
                        sd = slot; slot = (slot + 1) % 2
                        lvd = dma("gpsimd", w8[sd][:], DW[ig * HD:(ig + 1) * HD, :], gp,
                                  waits=slab_free[sd])
                        for fc in range(NCH):
                            pb = ps_d[fc % 2]
                            for pi, fb in enumerate((fh, fl)):
                                for jc in range(nic):
                                    p.emit("tensor", lambda E, ig=ig, fc=fc, jc=jc, pb=pb,
                                           fb=fb, sd=sd, pi=pi:
                                           E.matmul(pb[:],
                                                    w8[sd][:, jc * DM + fc * HD:jc * DM + (fc + 1) * HD],
                                                    fb[:, jc * HT:(jc + 1) * HT],
                                                    start=(pi == 0 and jc == 0),
                                                    stop=(pi == 1 and jc == nic - 1)),
                                           waits=[lvd, (vs, vs.n), *d_free[fc % 2]],
                                           inc=(ts if (pi == 1 and jc == nic - 1) else None))
                            if ig == 0:
                                p.emit("scalar", lambda E, fc=fc, pb=pb: E.copy(
                                    y2[:, fc * HT:(fc + 1) * HT], pb[:]),
                                    waits=[ts, (st, st.n)], inc=ss)
                            else:
                                p.emit("vector", lambda E, fc=fc, pb=pb: E.tensor_tensor(
                                    y2[:, fc * HT:(fc + 1) * HT], y2[:, fc * HT:(fc + 1) * HT],
                                    pb[:], ALU.add), waits=[ts], inc=vs)
                            d_free[fc % 2] = [(ss, ss.n), (vs, vs.n)]
                        slab_free[sd] = [(ts, ts.n)]
                    for fc in range(NCH):
                        p.emit("vector", lambda E, fc=fc: E.tensor_scalar(
                            y2[:, fc * HT:(fc + 1) * HT], y2[:, fc * HT:(fc + 1) * HT],
                            dsc[:, fc:fc + 1], None, ALU.mult), waits=[(ss, ss.n)], inc=vs)
                    with ExitStack() as sctx:
                        ps_s1 = sctx.enter_context(nc.psum_tensor(f"ps4s1{hh}", [1, 512], F32))
                        ps_s2 = sctx.enter_context(nc.psum_tensor(f"ps4s2{hh}", [1, 512], F32))
                        for j in range(NCH):
                            sqb = sqf[j % 2]
                            p.emit("scalar", lambda E, j=j, sqb=sqb: E.square(
                                sqb[:, 0:HT], y2[:, j * HT:(j + 1) * HT]),
                                waits=[(vs, vs.n), (ts, ts.n)], inc=ss)
                            p.emit("tensor", lambda E, j=j: E.matmul(
                                ps_s1[:], onescol[:], y2[:, j * HT:(j + 1) * HT],
                                start=(j == 0), stop=(j == NCH - 1)),
                                waits=[(vs, vs.n), ss], inc=None)
                            p.emit("tensor", lambda E, j=j, sqb=sqb: E.matmul(
                                ps_s2[:], onescol[:], sqb[:, 0:HT],
                                start=(j == 0), stop=(j == NCH - 1)), waits=[ss], inc=ts)
                        p.emit("scalar", lambda E: E.copy(row(0, HT), ps_s1[:]),
                               waits=[ts], inc=ss)
                        p.emit("scalar", lambda E: E.copy(row(1, HT), ps_s2[:]),
                               waits=[ts], inc=ss)
                        p.flush()
                    ln_rows_and_bcast(HT, EPS_LN)
                    for j in range(NCH):
                        lv = dma("sync", x1t[j % 2][:, 0:HT],
                                 x1_scr3[:, j, hh * HT:(hh + 1) * HT], ld,
                                 waits=x1_free[j % 2])
                        p.emit("vector", lambda E, j=j: E.tensor_tensor(
                            y2[:, j * HT:(j + 1) * HT], y2[:, j * HT:(j + 1) * HT],
                            bc1[:, 0:HT], ALU.mult), waits=[ss], inc=vs)
                        p.emit("vector", lambda E, j=j: E.tensor_tensor(
                            y2[:, j * HT:(j + 1) * HT], y2[:, j * HT:(j + 1) * HT],
                            bc2[:, 0:HT], ALU.subtract), waits=[], inc=vs)
                        p.emit("vector", lambda E, j=j: E.tensor_tensor(
                            y2[:, j * HT:(j + 1) * HT], y2[:, j * HT:(j + 1) * HT],
                            x1t[j % 2][:, 0:HT], ALU.add), waits=[lv], inc=vs)
                        dma("sync", OUTT3[:, j, hh * HT:(hh + 1) * HT],
                            y2[:, j * HT:(j + 1) * HT], st, waits=[(vs, vs.n)])
                        x1_free[j % 2] = [(vs, vs.n)]
                    p.flush()

        p.emit("sync", lambda E: E.nop(), waits=[(st, st.n)], inc=None)
        p.flush()
        return nc, outs


def stage_inputs(inputs):
    x = np.ascontiguousarray(np.asarray(inputs["L0_attn_hidden_states"], np.float32))

    def pan(M, csz):
        K, N = M.shape
        t = M.reshape(K // HD, HD, N // csz, csz).transpose(2, 1, 0, 3)
        return np.ascontiguousarray(t.reshape(N // csz * HD, K // HD * csz))

    wqkv = np.asarray(inputs["L0_attn_w_qkv"], np.float32).reshape(NH, 3, HD, DM)
    QTW = pan(wqkv[:, 0].reshape(DM, DM).T, HD)
    KTW = pan(wqkv[:, 1].reshape(DM, DM).T, HD)
    VTW = pan(wqkv[:, 2].reshape(DM, DM).T, HD)
    WGT = pan(np.asarray(inputs["L0_attn_w_gate"], np.float32).T, HD)
    WOT = pan(np.asarray(inputs["L0_attn_w_out"], np.float32).T, HD)
    GW = pan(np.asarray(inputs["L0_ffn_gate_w"], np.float32), IGRP)
    UW = pan(np.asarray(inputs["L0_ffn_up_w"], np.float32), IGRP)
    dw_ = np.asarray(inputs["L0_ffn_down_w"], np.float32)
    DW = np.ascontiguousarray(dw_.reshape(NIG, IGRP // HD, HD, DM).transpose(0, 2, 1, 3)
                              .reshape(NIG * HD, (IGRP // HD) * DM))
    SLOPES = np.asarray(inputs["L0_attn_slope_rate"], np.float32).reshape(1, NH)
    NWC = np.ascontiguousarray(np.asarray(inputs["L0_attn_norm_w"], np.float32)
                               .reshape(NCH, HD).T)
    GSC = np.ascontiguousarray(np.asarray(inputs["L0_ffn_gate_scale"], np.float32)
                               .reshape(FI // HD, HD).T)
    USC = np.ascontiguousarray(np.asarray(inputs["L0_ffn_up_scale"], np.float32)
                               .reshape(FI // HD, HD).T)
    DSC = np.ascontiguousarray(np.asarray(inputs["L0_ffn_down_scale"], np.float32)
                               .reshape(NCH, HD).T)
    IDENT = np.eye(HD, dtype=np.float32)
    m = np.arange(BLK, dtype=np.float32)
    IDXT = np.where(m[None, :] >= m[:, None], m[None, :] - m[:, None],
                    np.float32(BIG)).astype(np.float32)
    ARRQ = np.arange(1, BLK + 1, dtype=np.float32).reshape(1, BLK)
    ARRK = (BLK - np.arange(1, BLK + 1, dtype=np.float32)).reshape(1, BLK)
    ONESROW = np.ones((1, HD), np.float32)
    ONESCOL = np.ones((HD, 1), np.float32)

    shared = dict(qtw=QTW, ktw=KTW, vtw=VTW, wgt=WGT, wot=WOT, gw=GW, uw=UW, dw=DW,
                  slopes=SLOPES, nwc=NWC, gsc=GSC, usc=USC, dsc=DSC, ident=IDENT,
                  idxt=IDXT, arrq=ARRQ, arrk=ARRK, onesrow=ONESROW, onescol=ONESCOL)
    in_maps = []
    for c in range(NCORES):
        b, pp = c // 4, c % 4
        xt = np.ascontiguousarray(x[b, pp * TPC:(pp + 1) * TPC, :].T)
        expo = np.full((1, NCORES), BIG, np.float32)
        for cp in range(NCORES):
            if cp // 4 == b and cp % 4 < pp:
                expo[0, cp] = np.float32(pp - 1 - (cp % 4))
        in_maps.append(dict(shared, xt=xt, expo=expo))
    return in_maps


def run(inputs, cfg):
    nc, outs = build(cfg)
    in_maps = stage_inputs(inputs)
    res = run_bass_kernel_spmd(nc, in_maps, core_ids=list(range(NCORES)))
    return res, outs


def kernel(**inputs):
    res, outs = run(inputs, {"stage": "full", "debug": False})
    out = np.empty((2, 4096, DM), np.float32)
    for c in range(NCORES):
        b, pp = c // 4, c % 4
        out[b, pp * TPC:(pp + 1) * TPC, :] = np.asarray(res.results[c]["outt"]).T
    return out


# version 11
# speedup vs baseline: 1.1899x; 1.0062x over previous
import numpy as np
from contextlib import ExitStack

import concourse.bass as bass
import concourse.mybir as mybir
from concourse.bass_utils import run_bass_kernel_spmd

F32 = mybir.dt.float32
BF16 = mybir.dt.bfloat16
FP8 = mybir.dt.float8e4
AF = mybir.ActivationFunctionType
ALU = mybir.AluOpType

NCORES = 8
DM = 2048
NCH = 16
NH = 16
HD = 128
BLK = 256
TPC = 1024
NBLK = 4
FI = 6144
EPS_LN = 1e-5
EPS_RMS = 1e-6
IGRP = 512
NIG = FI // IGRP
BIG = 1.0e9
HT = 512

ENGS = ("sync", "gpsimd", "tensor", "scalar", "vector")


class Sem:
    def __init__(self, handle, owner=None):
        self.h = handle
        self.n = 0
        self.owner = owner

    def inc(self, d=1):
        self.n += d


class Prog:

    def __init__(self, nc, block):
        self.nc = nc
        self.block = block
        self.q = {e: [] for e in ENGS}
        self.lastw = {e: {} for e in ENGS}

    def emit(self, eng, fn, waits=(), inc=None, inc_by=1):
        ws = []
        for w in waits:
            if isinstance(w, Sem):
                s, v = w, w.n
            else:
                s, v = w
            if v <= 0:
                continue
            if s.owner == eng and s.owner in ("tensor", "scalar", "vector"):
                continue
            prev = self.lastw[eng].get(s, 0)
            if v > prev:
                self.lastw[eng][s] = v
                ws.append((s.h, v))
        incs = []
        if inc is not None:
            inc.inc(inc_by)
            incs.append((inc.h, inc_by))
        self.q[eng].append((ws, fn, incs))

    def flush(self):
        for e in ENGS:
            items = self.q[e]
            if not items:
                continue
            self.q[e] = []

            def body(engine, items=items):
                for ws, fn, incs in items:
                    for (h, v) in ws:
                        engine.wait_ge(h, v)
                    inst = fn(engine)
                    for (h, d) in incs:
                        inst.then_inc(h, d)
            getattr(self.block, e)(body)


def build(cfg):
    stage = cfg.get("stage", "full")
    upto = cfg.get("upto", None)
    nheads = cfg.get("nheads", NH)
    dbg = cfg.get("debug", False)
    nc = bass.Bass("TRN2", target_bir_lowering=False, debug=False,
                   detect_race_conditions=False)

    def din(name, shape):
        return nc.dram_tensor(name, shape, F32, kind="ExternalInput").ap()

    XT = din("xt", [DM, TPC])
    QTW = din("qtw", [NCH * HD, DM])
    KTW = din("ktw", [NCH * HD, DM])
    VTW = din("vtw", [NCH * HD, DM])
    WGT = din("wgt", [NCH * HD, DM])
    WOT = din("wot", [NCH * HD, DM])
    GW = din("gw", [NIG * HD, NCH * IGRP])
    UW = din("uw", [NIG * HD, NCH * IGRP])
    DW = din("dw", [NIG * HD, (IGRP // HD) * DM])
    SLOPES = din("slopes", [1, NH])
    EXPO = din("expo", [1, NCORES])
    NWC = din("nwc", [HD, NCH])
    GSC = din("gsc", [HD, FI // HD])
    USC = din("usc", [HD, FI // HD])
    DSC = din("dsc", [HD, NCH])
    IDENT = din("ident", [HD, HD])
    IDXT = din("idxt", [BLK, BLK])
    ARRQ = din("arrq", [1, BLK])
    ARRK = din("arrk", [1, BLK])
    ONESROW = din("onesrow", [1, HD])
    ONESCOL = din("onescol", [HD, 1])

    outs = {}

    def dout(name, shape):
        outs[name] = nc.dram_tensor(name, shape, F32, kind="ExternalOutput").ap()
        return outs[name]

    def scr(name, shape, debug_out=False):
        if debug_out and dbg:
            return dout(name, shape)
        return nc.dram_tensor(name, shape, F32, kind="Internal").ap()

    kfm_scr = scr("kfm_scr", [DM, TPC], debug_out=True)
    vtm_scr = scr("vtm_scr", [TPC, DM], debug_out=True)
    x1_scr = (dout("x1_scr", [DM, TPC]) if stage != "full"
              else nc.dram_tensor("x1_scr", [DM, TPC], F32, kind="Internal").ap())
    ag_in = nc.dram_tensor("ag_in", [NH * HD, HD], F32, kind="Internal").ap()
    ag_out = nc.dram_tensor("ag_out", [NCORES * NH * HD, HD], F32, kind="Internal").ap()
    if dbg:
        KV0 = dout("kv0_dbg", [HD, NH * HD])
        ODBG = dout("o_dbg", [HD, NH * TPC])
        GODBG = dout("go_dbg", [HD, NH * TPC])
        YDBG = dout("y_dbg", [HD, NCH * TPC])
    if stage == "full":
        OUTT = dout("outt", [DM, TPC])

    with ExitStack() as ctx:
        def sb(name, shape, dtype=F32):
            return ctx.enter_context(nc.sbuf_tensor("s_" + name, shape, dtype))

        def mksem(name, owner=None):
            return Sem(ctx.enter_context(nc.semaphore(name)), owner)

        ld = mksem("ld", "sync")
        gp = mksem("gp", "gpsimd")
        ts = mksem("ts", "tensor")
        ss = mksem("ss", "scalar")
        vs = mksem("vs", "vector")
        cc = mksem("cc", "gpsimd")
        st = mksem("st", "sync")

        ident = sb("ident", [HD, HD])
        idxt = sb("idxt", [HD, 2 * BLK])
        arrq = sb("arrq", [1, BLK])
        arrk = sb("arrk", [1, BLK])
        onesrow = sb("onesrow", [1, HD])
        onescol = sb("onescol", [HD, 1])
        slopes = sb("slopes", [1, NH])
        expo = sb("expo", [1, NCORES])
        nwc = sb("nwc", [HD, NCH])
        gsc = sb("gsc", [HD, FI // HD])
        usc = sb("usc", [HD, FI // HD])
        dsc = sb("dsc", [HD, NCH])
        sb16 = sb("sb16", [HD, NH])
        sbneg = sb("sbneg", [HD, NH])
        bdt = sb("bdt", [HD, NH])
        kdcol = sb("kdcol", [HD, NH * 2])
        wcomb = sb("wcomb", [HD, NCORES * NH])
        kvst = sb("kvst", [HD, NH * HD])
        rows = sb("rows", [1, 4 * TPC])
        rows2 = sb("rows2", [1, 2 * TPC])
        bc1 = sb("bc1", [HD, TPC])
        bc2 = sb("bc2", [HD, TPC])

        block = ctx.enter_context(nc.Block())
        p = Prog(nc, block)

        def dma(eng, out, in_, sem, waits=()):
            p.emit(eng, lambda E, o=out, i=in_: E.dma_start(out=o, in_=i),
                   waits=waits, inc=sem, inc_by=16)
            return (sem, sem.n)

        def row(k, n=TPC):
            return rows[0:1, k * TPC:k * TPC + n]

        def bar():
            return [(ld, ld.n), (gp, gp.n), (ts, ts.n), (ss, ss.n), (vs, vs.n), (cc, cc.n)]

        def mm3(pb, wh, wl, xh_, xl_, tt, T, first_waits):
            for pi, (wp_, xp_) in enumerate(((wh, xh_), (wh, xl_), (wl, xh_))):
                for j in range(NCH):
                    p.emit("tensor",
                           lambda E, pi=pi, wp_=wp_, xp_=xp_, j=j, tt=tt, pb=pb:
                           E.matmul(
                               pb[:], wp_[:, j * HD:(j + 1) * HD],
                               xp_[:, j * T + tt * 512:j * T + (tt + 1) * 512],
                               start=(pi == 0 and j == 0),
                               stop=(pi == 2 and j == NCH - 1)),
                           waits=first_waits if (pi == 0 and j == 0) else [],
                           inc=(ts if (pi == 2 and j == NCH - 1) else None))

        def split_tile(fp, bh, bl, waits):
            p.emit("vector", lambda E: E.tensor_copy(bh, fp), waits=waits, inc=vs)
            p.emit("vector", lambda E: E.tensor_tensor(bl, fp, bh, ALU.subtract),
                   waits=[], inc=vs)
            return (vs, vs.n)

        _lrc = [0]

        def ln_rows_and_bcast(T, eps, rms=False, upto2=None):
            upto2 = upto2 or cfg.get("upto2")
            inv_dm = 1.0 / DM
            if not rms:
                p.emit("vector", lambda E: E.tensor_scalar(row(0, T), row(0, T), inv_dm,
                                                           None, ALU.mult), waits=[ss], inc=vs)
            p.emit("vector", lambda E: E.tensor_scalar(row(1, T), row(1, T), inv_dm,
                                                       None, ALU.mult), waits=[ss], inc=vs)
            if not rms:
                p.emit("vector", lambda E: E.tensor_tensor(row(2, T), row(0, T), row(0, T),
                                                           ALU.mult), waits=[], inc=vs)
                p.emit("vector", lambda E: E.tensor_tensor(row(1, T), row(1, T), row(2, T),
                                                           ALU.subtract), waits=[], inc=vs)
            p.emit("vector", lambda E: E.tensor_scalar(row(1, T), row(1, T), float(eps),
                                                       None, ALU.add), waits=[], inc=vs)
            if upto2 == "lr1":
                return
            p.emit("scalar", lambda E: E.activation(row(1, T), row(1, T), AF.Sqrt),
                   waits=[vs], inc=ss)
            p.emit("vector", lambda E: E.reciprocal(row(2, T), row(1, T)), waits=[ss], inc=vs)
            if upto2 == "lr2":
                return
            if not rms:
                p.emit("vector", lambda E: E.tensor_tensor(row(3, T), row(0, T), row(2, T),
                                                           ALU.mult), waits=[], inc=vs)
            if upto2 == "lr3":
                return
            _lrc[0] += 1
            with nc.psum_tensor(f"ps_bcr{_lrc[0]}", [HD, 512], F32) as ps_b:
                for u in range((T + 511) // 512):
                    w = min(512, T - u * 512)
                    p.emit("tensor", lambda E, u=u, w=w: E.matmul(
                        ps_b[:, 0:w], onesrow[:], row(2, T)[0:1, u * 512:u * 512 + w],
                        start=True, stop=True), waits=[vs, ss], inc=ts)
                    if upto2 == "lrb1":
                        continue
                    p.emit("scalar", lambda E, u=u, w=w: E.copy(
                        bc1[:, u * 512:u * 512 + w], ps_b[:, 0:w]), waits=[ts], inc=ss)
                    if upto2 == "lrb2" or rms:
                        continue
                    p.emit("tensor", lambda E, u=u, w=w: E.matmul(
                        ps_b[:, 0:w], onesrow[:], row(3, T)[0:1, u * 512:u * 512 + w],
                        start=True, stop=True), waits=[ss], inc=ts)
                    if upto2 == "lrb3":
                        continue
                    p.emit("scalar", lambda E, u=u, w=w: E.copy(
                        bc2[:, u * 512:u * 512 + w], ps_b[:, 0:w]), waits=[ts], inc=ss)
                p.flush()

        for (t, d) in [(ident, IDENT), (arrq, ARRQ), (arrk, ARRK),
                       (onesrow, ONESROW), (onescol, ONESCOL), (slopes, SLOPES),
                       (expo, EXPO), (nwc, NWC), (gsc, GSC), (usc, USC), (dsc, DSC)]:
            dma("sync", t[:], d, ld)
        for j in range(2):
            dma("sync", idxt[:, j * BLK:(j + 1) * BLK], IDXT[j * HD:(j + 1) * HD, :], ld)
        p0_ld = (ld, ld.n)

        XT3 = XT.rearrange("(j p) t -> p j t", p=HD)
        kfm_scr3 = kfm_scr.rearrange("(h p) t -> p h t", p=HD)
        x1_scr3 = x1_scr.rearrange("(j p) t -> p j t", p=HD)

        def early_out():
            p.emit("sync", lambda E: E.nop(), waits=bar(), inc=None)
            p.flush()
            return nc, outs

        if upto == "p0":
            return early_out()

        with ExitStack() as p1ctx:
            p1sb = lambda n, s, dt=F32: p1ctx.enter_context(nc.sbuf_tensor("p1_" + n, s, dt))
            xlnh = p1sb("xlnh", [HD, NCH * TPC], BF16)
            xlnl = p1sb("xlnl", [HD, NCH * TPC], BF16)
            lntmp = p1sb("lntmp", [HD, TPC])
            xa = [p1sb(f"xa{i}", [HD, TPC]) for i in range(2)]
            _sq0 = p1sb("sq0", [HD, TPC])
            sq = [_sq0, _sq0]
            kstage2 = [p1sb(f"kstage{i}", [HD, TPC]) for i in range(2)]
            vstage = p1sb("vstage", [HD, TPC])
            vtstage2 = [p1sb(f"vtstage{i}", [HD, 8 * HD]) for i in range(2)]
            ktm = p1sb("ktm", [HD, BLK])
            _kp0 = p1sb("kpan0", [HD, NCH * HD])
            _vp0 = p1sb("vpan0", [HD, NCH * HD])
            kpan = [_kp0, _kp0]
            vpan = [_vp0, _vp0]
            kbh = [p1sb(f"kbh{i}", [HD, NCH * HD], BF16) for i in range(2)]
            kbl = [p1sb(f"kbl{i}", [HD, NCH * HD], BF16) for i in range(2)]
            vbh = [p1sb(f"vbh{i}", [HD, NCH * HD], BF16) for i in range(2)]
            vbl = [p1sb(f"vbl{i}", [HD, NCH * HD], BF16) for i in range(2)]

            xa_free = [bar(), bar()]
            with ExitStack() as sctx:
                ps_s1 = [sctx.enter_context(nc.psum_tensor(f"ps_s1{u}", [1, 512], F32))
                         for u in range(2)]
                ps_s2 = [sctx.enter_context(nc.psum_tensor(f"ps_s2{u}", [1, 512], F32))
                         for u in range(2)]
                for j in range(NCH):
                    lv = dma("sync", xa[j % 2][:], XT3[:, j, :], ld, waits=xa_free[j % 2])
                    sqb = sq[j % 2]
                    p.emit("scalar", lambda E, j=j, sqb=sqb: E.square(sqb[:], xa[j % 2][:]),
                           waits=[lv, (ts, ts.n)], inc=ss)
                    for u in range(2):
                        p.emit("tensor", lambda E, j=j, u=u: E.matmul(
                            ps_s1[u][:], onescol[:], xa[j % 2][:, u * 512:(u + 1) * 512],
                            start=(j == 0), stop=(j == NCH - 1)),
                            waits=[lv, ss], inc=None)
                        p.emit("tensor", lambda E, j=j, u=u, sqb=sqb: E.matmul(
                            ps_s2[u][:], onescol[:], sqb[:, u * 512:(u + 1) * 512],
                            start=(j == 0), stop=(j == NCH - 1)),
                            waits=[ss], inc=(ts if u == 1 else None))
                    xa_free[j % 2] = [(ts, ts.n)]
                for u in range(2):
                    p.emit("scalar", lambda E, u=u: E.copy(row(0)[0:1, u * 512:(u + 1) * 512],
                                                           ps_s1[u][:]), waits=[ts], inc=ss)
                    p.emit("scalar", lambda E, u=u: E.copy(row(1)[0:1, u * 512:(u + 1) * 512],
                                                           ps_s2[u][:]), waits=[ts], inc=ss)
                p.flush()
            with nc.psum_tensor("ps_bc0", [HD, 512], F32) as ps_bc:
                p.emit("tensor", lambda E: E.matmul(ps_bc[:, 0:NH], onesrow[:], slopes[:],
                                                    start=True, stop=True),
                       waits=[p0_ld, (ss, ss.n), (ts, ts.n)], inc=ts)
                p.emit("scalar", lambda E: E.copy(sb16[:], ps_bc[:, 0:NH]), waits=[ts], inc=ss)
                p.emit("vector", lambda E: E.tensor_scalar(sbneg[:], sb16[:], -1.0, None, ALU.mult),
                       waits=[ss], inc=vs)
                p.emit("scalar", lambda E: E.activation(bdt[:], sb16[:], AF.Exp, scale=-float(BLK)),
                       waits=[vs], inc=ss)
                p.emit("tensor", lambda E: E.matmul(ps_bc[:, 32:32 + NCORES], onesrow[:], expo[:],
                                                    start=True, stop=True),
                       waits=[(ss, ss.n)], inc=ts)
                p.emit("scalar", lambda E: E.copy(bc1[:, 0:NCORES], ps_bc[:, 32:32 + NCORES]),
                       waits=[ts], inc=ss)
                for c in range(NCORES):
                    p.emit("vector",
                           lambda E, c=c: E.tensor_scalar(bc2[:, 0:NH], sb16[:], -1024.0,
                                                          bc1[:, c:c + 1], ALU.mult, ALU.mult),
                           waits=[ss], inc=vs)
                    p.emit("scalar",
                           lambda E, c=c: E.activation(wcomb[:, c * NH:(c + 1) * NH],
                                                       bc2[:, 0:NH], AF.Exp),
                           waits=[vs], inc=ss)
                with nc.psum_tensor("ps_tr0", [HD, HD], F32) as ps_tr:
                    for h in range(NH):
                        p.emit("scalar",
                               lambda E, h=h: E.activation(row(3, BLK), arrk[:], AF.Exp,
                                                           scale=sbneg[0:1, h:h + 1]),
                               waits=[ts, vs], inc=ss)
                        p.emit("tensor",
                               lambda E: E.matmul(ps_bc[:, 0:BLK], onesrow[:], row(3, BLK),
                                                  start=True, stop=True),
                               waits=[ss], inc=ts)
                        p.emit("scalar",
                               lambda E: E.copy(bc2[:, 0:BLK], ps_bc[:, 0:BLK]),
                               waits=[ts], inc=ss)
                        for j in range(2):
                            p.emit("tensor",
                                   lambda E, j=j: E.transpose(ps_tr[:], bc2[:, j * HD:(j + 1) * HD],
                                                              ident[:]),
                                   waits=[ss, p0_ld], inc=ts)
                            p.emit("scalar",
                                   lambda E, h=h, j=j: E.copy(kdcol[:, h * 2 + j:h * 2 + j + 1],
                                                              ps_tr[:, 0:1]),
                                   waits=[ts], inc=ss)
                    p.flush()

            if upto == "stats0":
                return early_out()
            ln_rows_and_bcast(TPC, EPS_LN)
            p.emit("vector", lambda E: E.tensor_copy(rows2[0:1, 0:TPC], row(0)),
                   waits=[ss], inc=vs)
            p.emit("vector", lambda E: E.tensor_copy(rows2[0:1, TPC:2 * TPC], row(2)),
                   waits=[], inc=vs)
            if upto == "stats":
                return early_out()
            xa_free = [[(ts, ts.n)], [(ts, ts.n)]]
            for j in range(NCH):
                lv = dma("sync", xa[j % 2][:], XT3[:, j, :], ld, waits=xa_free[j % 2])
                p.emit("vector", lambda E, j=j: E.tensor_tensor(
                    lntmp[:], xa[j % 2][:], bc1[:], ALU.mult),
                    waits=[lv, ss], inc=vs)
                p.emit("vector", lambda E, j=j: E.tensor_tensor(
                    lntmp[:], lntmp[:], bc2[:], ALU.subtract), waits=[], inc=vs)
                p.emit("vector", lambda E, j=j: E.tensor_copy(
                    xlnh[:, j * TPC:(j + 1) * TPC], lntmp[:]), waits=[], inc=vs)
                p.emit("vector", lambda E, j=j: E.tensor_tensor(
                    xlnl[:, j * TPC:(j + 1) * TPC], lntmp[:],
                    xlnh[:, j * TPC:(j + 1) * TPC], ALU.subtract), waits=[], inc=vs)
                xa_free[j % 2] = [(vs, vs.n)]
            if upto == "xln":
                return early_out()


            p.emit("vector", lambda E: E.memset(kvst[:], 0.0), waits=[], inc=vs)
            with ExitStack() as dctx:
                ps_k = [dctx.enter_context(nc.psum_tensor(f"ps_k{i}", [HD, 512], F32))
                        for i in range(2)]
                ps_tr = [dctx.enter_context(nc.psum_tensor(f"ps_ktr{i}", [HD, HD], F32))
                         for i in range(2)]
                ps_w = dctx.enter_context(nc.psum_tensor("ps_w", [HD, HD], F32))
                kpan_free = [[(ts, ts.n)], [(ts, ts.n)]]
                vpan_free = [[(ts, ts.n)], [(ts, ts.n)]]
                kst_free = [(gp, gp.n), (ts, ts.n)]
                vst_free = [(ts, ts.n)]
                vts_free = [(gp, gp.n), (ts, ts.n)]
                xlnw = (vs, vs.n)
                tk = {
                    "kb": [bar(), bar()], "vb": [bar(), bar()],
                    "psk": [(ss, ss.n), (ss, ss.n)],
                    "kst": [bar(), bar()], "vts": [bar(), bar()],
                    "vst": [(ts, ts.n)],
                    "pstr": [(ss, ss.n), (ss, ss.n)],
                    "psw": [(vs, vs.n)],
                    "ktm": [(ts, ts.n), (ts, ts.n)],
                    "state": (vs, vs.n),
                }
                head_tok = {}
                xlnw = (vs, vs.n)
                p.emit("vector", lambda E: E.memset(kvst[:], 0.0), waits=[], inc=vs)

                def proj_head(h):
                    sh = h % 2
                    kp, vp = kpan[0], vpan[0]
                    lvk = dma("sync", kp[:], KTW[h * HD:(h + 1) * HD, :], ld,
                              waits=kpan_free[0])
                    lvv = dma("sync", vp[:], VTW[h * HD:(h + 1) * HD, :], ld,
                              waits=vpan_free[0])
                    kspl = split_tile(kp[:], kbh[sh][:], kbl[sh][:], [lvk, *tk["kb"][sh]])
                    vspl = split_tile(vp[:], vbh[sh][:], vbl[sh][:], [lvv, *tk["vb"][sh]])
                    kpan_free[0] = [(vs, kspl[1])]
                    vpan_free[0] = [(vs, vspl[1])]
                    for tt in range(2):
                        pb = ps_k[tt]
                        mm3(pb, kbh[sh], kbl[sh], xlnh, xlnl, tt, TPC,
                            [kspl, xlnw, tk["psk"][tt]])
                        p.emit("scalar", lambda E, tt=tt, pb=pb, sh=sh: E.activation(
                            kstage2[sh][:, tt * 512:(tt + 1) * 512], pb[:], AF.Silu),
                            waits=[ts, *tk["kst"][sh]], inc=ss)
                        tk["psk"][tt] = (ss, ss.n)
                    kevac = (ss, ss.n)
                    tk["kb"][sh] = [(ts, ts.n)]
                    ksp = dma("gpsimd", kfm_scr3[:, h, :], kstage2[sh][:], gp,
                              waits=[kevac])
                    for tt in range(2):
                        pb = ps_k[tt]
                        mm3(pb, vbh[sh], vbl[sh], xlnh, xlnl, tt, TPC,
                            [vspl, tk["psk"][tt]])
                        p.emit("scalar", lambda E, tt=tt, pb=pb: E.activation(
                            vstage[:, tt * 512:(tt + 1) * 512], pb[:], AF.Silu),
                            waits=[ts, *tk["vst"]], inc=ss)
                        tk["psk"][tt] = (ss, ss.n)
                    tk["vb"][sh] = [(ts, ts.n)]
                    for tc in range(8):
                        pt = ps_tr[tc % 2]
                        p.emit("tensor", lambda E, tc=tc, pt=pt: E.transpose(
                            pt[:], vstage[:, tc * HD:(tc + 1) * HD], ident[:]),
                            waits=[(ss, ss.n), tk["pstr"][tc % 2]], inc=ts)
                        trv = (ts, ts.n)
                        p.emit("scalar", lambda E, tc=tc, pt=pt, sh=sh: E.copy(
                            vtstage2[sh][:, tc * HD:(tc + 1) * HD], pt[:]),
                            waits=[trv] + (list(tk["vts"][sh]) if tc == 0 else []), inc=ss)
                        tk["pstr"][tc % 2] = (ss, ss.n)
                    tk["vst"] = [(ts, ts.n)]
                    vtev = (ss, ss.n)
                    vsp = dma("gpsimd",
                              vtm_scr[:, h * HD:(h + 1) * HD].rearrange(
                                  "(tc p) e -> p tc e", p=HD),
                              vtstage2[sh][:].rearrange("p (tc e) -> p tc e", tc=8),
                              gp, waits=[vtev])
                    head_tok[h] = (kevac, vtev, ksp, vsp)

                def state_head(h):
                    sh = h % 2
                    kevac, vtev, ksp, vsp = head_tok[h]
                    for b in range(NBLK):
                        for j in range(2):
                            p.emit("tensor", lambda E, b=b, j=j, sh=sh: E.transpose(
                                ps_tr[j][:],
                                kstage2[sh][:, b * BLK + j * HD:b * BLK + (j + 1) * HD],
                                ident[:]), waits=[kevac, tk["pstr"][j]], inc=ts)
                            trv = (ts, ts.n)
                            p.emit("scalar", lambda E, j=j: E.copy(
                                ktm[:, j * HD:(j + 1) * HD], ps_tr[j][:]),
                                waits=[trv, tk["ktm"][j]], inc=ss)
                            tk["pstr"][j] = (ss, ss.n)
                            p.emit("vector", lambda E, h=h, j=j: E.tensor_scalar(
                                ktm[:, j * HD:(j + 1) * HD], ktm[:, j * HD:(j + 1) * HD],
                                kdcol[:, h * 2 + j:h * 2 + j + 1], None, ALU.mult),
                                waits=[(ss, ss.n)], inc=vs)
                        kkv = (vs, vs.n)
                        for j in range(2):
                            p.emit("tensor", lambda E, h=h, b=b, j=j, sh=sh: E.matmul(
                                ps_w[:], ktm[:, j * HD:(j + 1) * HD],
                                vtstage2[sh][:, (2 * b + j) * HD:(2 * b + j + 1) * HD],
                                start=(j == 0), stop=(j == 1)),
                                waits=[kkv, vtev, *(tk["psw"] if j == 0 else [])],
                                inc=(ts if j == 1 else None))
                        wmm = (ts, ts.n)
                        tk["ktm"][0] = wmm
                        tk["ktm"][1] = wmm
                        p.emit("vector", lambda E, h=h: E.tensor_scalar(
                            kvst[:, h * HD:(h + 1) * HD], kvst[:, h * HD:(h + 1) * HD],
                            bdt[:, h:h + 1], None, ALU.mult), waits=[tk["state"]], inc=vs)
                        p.emit("vector", lambda E, h=h: E.tensor_tensor(
                            kvst[:, h * HD:(h + 1) * HD], kvst[:, h * HD:(h + 1) * HD],
                            ps_w[:], ALU.add), waits=[wmm], inc=vs)
                        tk["state"] = (vs, vs.n)
                        tk["psw"] = [tk["state"]]
                    tk["kst"][sh] = [(gp, ksp[1]), (ts, ts.n)]
                    tk["vts"][sh] = [(gp, vsp[1]), (ts, ts.n)]

                for h in range(nheads):
                    proj_head(h)
                    if h > 0:
                        state_head(h - 1)
                state_head(nheads - 1)
                p.flush()
            if upto == "noag":
                return early_out()
            ag_in3 = ag_in.rearrange("(h d) e -> d h e", d=HD)
            agv = dma("sync", ag_in3, kvst[:].rearrange("d (h e) -> d h e", h=NH), ld,
                      waits=[(vs, vs.n)])
            p.emit("gpsimd", lambda E: E.collective_compute(
                "AllGather", ALU.bypass, replica_groups=[list(range(NCORES))],
                ins=[ag_in.opt()], outs=[ag_out.opt()]),
                waits=[agv], inc=cc)
            p.flush()

        def emit_kv_init(slab_buf=None):
            with ExitStack() as ectx:
                s0 = (slab_buf if slab_buf is not None else
                      ectx.enter_context(nc.sbuf_tensor("sslab0", [HD, NH * HD], F32)))
                sslab = [s0, s0]
                p.emit("vector", lambda E: E.memset(kvst[:], 0.0), waits=[(cc, cc.n)], inc=vs)
                slab_free = bar()
                for c in range(NCORES):
                    srcap = ag_out[c * NH * HD:(c + 1) * NH * HD, :].rearrange(
                        "(h d) e -> d h e", d=HD)
                    lv = dma("sync", sslab[0][:].rearrange("d (h e) -> d h e", h=NH),
                             srcap, ld, waits=[(cc, cc.n), *slab_free])
                    for h in range(NH):
                        p.emit("vector", lambda E, c=c, h=h: E.scalar_tensor_tensor(
                            kvst[:, h * HD:(h + 1) * HD], sslab[0][:, h * HD:(h + 1) * HD],
                            wcomb[:, c * NH + h:c * NH + h + 1], kvst[:, h * HD:(h + 1) * HD],
                            ALU.mult, ALU.add), waits=[lv], inc=vs)
                    slab_free = [(vs, vs.n)]
                if dbg:
                    dma("gpsimd", KV0, kvst[:], gp, waits=[(vs, vs.n)])
                p.flush()

        if stage == "p1":
            emit_kv_init()
            p.emit("sync", lambda E: E.nop(), waits=bar(), inc=None)
            p.flush()
            return nc, outs

        state_done = {}
        for hh in range(2):
            with ExitStack() as hctx:
                hsb = lambda name, shape: hctx.enter_context(
                    nc.sbuf_tensor(f"h{hh}_{name}", shape, F32))
                xha = [hsb(f"xha{i}", [HD, HT]) for i in range(2)]
                hbsb = lambda name, shape: hctx.enter_context(
                    nc.sbuf_tensor(f"h{hh}_{name}", shape, BF16))
                xyh = hbsb("xyh", [HD, NCH * HT])
                xyl = hbsb("xyl", [HD, NCH * HT])
                lnt2 = hsb("lnt2", [HD, HT])
                qbuf = hsb("qbuf", [HD, NH * HT])
                obuf = hsb("obuf", [HD, NH * HT])
                _sqh0 = hsb("sqh0", [HD, HT])
                sqh = [_sqh0, _sqh0]
                pan = [hsb(f"pan{i}", [HD, NCH * HD]) for i in range(2)]
                pbh = [hbsb(f"pbh{i}", [HD, NCH * HD]) for i in range(2)]
                pbl = [hbsb(f"pbl{i}", [HD, NCH * HD]) for i in range(2)]
                kre2 = [hsb(f"kre2_{i}", [HD, 2 * BLK]) for i in range(2)]
                vre2 = [hsb(f"vre2_{i}", [HD, 2 * BLK]) for i in range(2)]
                ktm2 = [hsb(f"ktm2_{i}", [HD, BLK]) for i in range(2)]
                qq2 = [hsb(f"qq2_{i}", [HD, BLK]) for i in range(2)]
                sdt2 = [hsb(f"sdt2_{i}", [HD, 2 * BLK]) for i in range(2)]
                dtt2 = [hsb(f"dtt2_{i}", [HD, 2 * BLK]) for i in range(2)]
                qdb2 = [hsb(f"qdb2_{i}", [HD, BLK]) for i in range(2)]
                gtile = hsb("gtile", [HD, HT])

                xa_free2 = [bar(), bar()]
                p.emit("vector", lambda E: E.tensor_tensor(
                    row(3, HT), rows2[0:1, hh * HT:(hh + 1) * HT],
                    rows2[0:1, TPC + hh * HT:TPC + (hh + 1) * HT], ALU.mult),
                    waits=bar(), inc=vs)
                with nc.psum_tensor(f"ps_bx{hh}", [HD, 512], F32) as ps_b:
                    p.emit("tensor", lambda E: E.matmul(
                        ps_b[:], onesrow[:], rows2[0:1, TPC + hh * HT:TPC + (hh + 1) * HT],
                        start=True, stop=True), waits=[vs, ss], inc=ts)
                    p.emit("scalar", lambda E: E.copy(bc1[:, 0:HT], ps_b[:]), waits=[ts], inc=ss)
                    p.emit("tensor", lambda E: E.matmul(
                        ps_b[:], onesrow[:], row(3, HT), start=True, stop=True),
                        waits=[vs, ss], inc=ts)
                    p.emit("scalar", lambda E: E.copy(bc2[:, 0:HT], ps_b[:]), waits=[ts], inc=ss)
                    p.flush()
                for j in range(NCH):
                    lv = dma("sync", xha[j % 2][:], XT3[:, j, hh * HT:(hh + 1) * HT],
                             ld, waits=xa_free2[j % 2])
                    p.emit("vector", lambda E, j=j: E.tensor_tensor(
                        lnt2[:], xha[j % 2][:],
                        bc1[:, 0:HT], ALU.mult), waits=[lv, ss], inc=vs)
                    p.emit("vector", lambda E, j=j: E.tensor_tensor(
                        lnt2[:], lnt2[:], bc2[:, 0:HT], ALU.subtract), waits=[], inc=vs)
                    split_tile(lnt2[:], xyh[:, j * HT:(j + 1) * HT],
                               xyl[:, j * HT:(j + 1) * HT], [])
                    xa_free2[j % 2] = [(vs, vs.n)]

                with ExitStack() as qctx:
                    ps_q = [qctx.enter_context(nc.psum_tensor(f"ps_q{hh}{i}", [HD, 512], F32))
                            for i in range(2)]
                    pan_free = [[(ts, ts.n)], [(ts, ts.n)]]
                    pb_free = [bar(), bar()]
                    ps_free = [(ss, ss.n), (ss, ss.n)]
                    for h in range(NH):
                        wp = pan[h % 2]
                        lv = dma("sync", wp[:], QTW[h * HD:(h + 1) * HD, :], ld,
                                 waits=pan_free[h % 2])
                        spl = split_tile(wp[:], pbh[h % 2][:], pbl[h % 2][:],
                                         [lv, *pb_free[h % 2]])
                        pan_free[h % 2] = [spl]
                        pb = ps_q[h % 2]
                        mm3(pb, pbh[h % 2], pbl[h % 2], xyh, xyl, 0, HT,
                            [spl, ps_free[h % 2]])
                        p.emit("scalar", lambda E, h=h, pb=pb: E.activation(
                            qbuf[:, h * HT:(h + 1) * HT], pb[:], AF.Silu),
                            waits=[ts, (gp, gp.n)], inc=ss)
                        ps_free[h % 2] = (ss, ss.n)
                        pb_free[h % 2] = [(ts, ts.n)]
                    p.flush()

                if hh == 0:
                    emit_kv_init(obuf[:, 0:NH * HD])
                    for _h in range(NH):
                        state_done[_h] = (vs, vs.n)

                with ExitStack() as kctx:
                    ps_st = [kctx.enter_context(nc.psum_tensor(f"ps_st{hh}{i}", [HD, 2 * BLK], F32))
                             for i in range(2)]
                    ps_o = [kctx.enter_context(nc.psum_tensor(f"ps_o{hh}{i}", [HD, BLK], F32))
                            for i in range(2)]
                    ps_tr = [kctx.enter_context(nc.psum_tensor(f"ps_tr{hh}{i}", [HD, BLK], F32))
                             for i in range(2)]
                    ps_w = [kctx.enter_context(nc.psum_tensor(f"ps_w{hh}{i}", [HD, HD], F32))
                            for i in range(2)]
                    cur = lambda sem: (sem, sem.n)
                    tok = {k: [bar(), bar()] for k in
                           ("kre", "vre", "psst", "pso", "pstr", "psw",
                            "qdb", "dtt", "qq", "sdt", "ktm", "rowq")}
                    head_info = {}

                    def stage1(h, bb, su):
                        sh = h % 2
                        if bb == 0:
                            lvk = dma("sync", kre2[sh][:],
                                      kfm_scr3[:, h, hh * HT:(hh + 1) * HT], ld,
                                      waits=tok["kre"][sh])
                            lvv = dma("sync",
                                      vre2[sh][:].rearrange("p (tc e) -> p tc e", tc=4),
                                      vtm_scr[hh * HT:(hh + 1) * HT,
                                              h * HD:(h + 1) * HD].rearrange(
                                          "(tc p) e -> p tc e", p=HD),
                                      ld, waits=tok["vre"][sh])
                            p.emit("scalar", lambda E, h=h, sh=sh: E.activation(
                                rows[0:1, (2 + sh) * TPC:(2 + sh) * TPC + BLK], arrq[:],
                                AF.Exp, scale=sbneg[0:1, h:h + 1]),
                                waits=tok["rowq"][sh], inc=ss)
                            qdrow = cur(ss)
                            p.emit("tensor", lambda E, su=su, sh=sh: E.matmul(
                                ps_o[su][:], onesrow[:],
                                rows[0:1, (2 + sh) * TPC:(2 + sh) * TPC + BLK],
                                start=True, stop=True),
                                waits=[qdrow, *tok["pso"][su]], inc=ts)
                            bqmm = cur(ts)
                            tok["rowq"][sh] = [bqmm]
                            p.emit("scalar", lambda E, sh=sh, su=su: E.copy(
                                qdb2[sh][:], ps_o[su][:]),
                                waits=[bqmm, *tok["qdb"][sh]], inc=ss)
                            qdbev = cur(ss)
                            for j in range(2):
                                p.emit("scalar", lambda E, h=h, sh=sh, j=j: E.activation(
                                    dtt2[sh][:, j * BLK:(j + 1) * BLK],
                                    idxt[:, j * BLK:(j + 1) * BLK],
                                    AF.Exp, scale=sbneg[:, h:h + 1]),
                                    waits=tok["dtt"][sh] if j == 0 else [], inc=ss)
                            dttev = cur(ss)
                            head_info[h] = (lvk, lvv, qdbev, dttev)
                        lvk, lvv, qdbev, dttev = head_info[h]
                        qcols = qbuf[:, h * HT + bb * BLK:h * HT + (bb + 1) * BLK]
                        p.emit("vector", lambda E, sh=sh, su=su, qcols=qcols: E.tensor_tensor(
                            qq2[su][:], qcols, qdb2[sh][:], ALU.mult),
                            waits=[qdbev, *tok["qq"][su]], inc=vs)
                        qqv = cur(vs)
                        for j in range(2):
                            p.emit("tensor", lambda E, sh=sh, su=su, j=j, bb=bb, qcols=qcols:
                                   E.matmul(
                                ps_st[su][:, j * BLK:(j + 1) * BLK],
                                kre2[sh][:, bb * BLK + j * HD:bb * BLK + (j + 1) * HD],
                                qcols, start=(j == 0), stop=(j == 1)),
                                waits=[lvk, *(tok["psst"][su] if j == 0 else [])],
                                inc=(ts if j == 1 else None))
                        stmm = cur(ts)
                        return qqv, stmm

                    def stage2(h, bb, su, qqv, stmm):
                        sh = h % 2
                        lvk, lvv, qdbev, dttev = head_info[h]
                        kv_h = kvst[:, h * HD:(h + 1) * HD]
                        for j in range(2):
                            p.emit("vector", lambda E, sh=sh, su=su, j=j: E.tensor_tensor(
                                sdt2[su][:, j * BLK:(j + 1) * BLK],
                                ps_st[su][:, j * BLK:(j + 1) * BLK],
                                dtt2[sh][:, j * BLK:(j + 1) * BLK], ALU.mult),
                                waits=[stmm, dttev, *(tok["sdt"][su] if j == 0 else [])],
                                inc=vs)
                        sdtv = cur(vs)
                        tok["psst"][su] = [sdtv]
                        p.emit("tensor", lambda E, su=su, kv_h=kv_h: E.matmul(
                            ps_o[su][:], kv_h, qq2[su][:], start=True, stop=False),
                            waits=[qqv, state_done[h], qdbev, *tok["pso"][su]], inc=None)
                        for j in range(2):
                            p.emit("tensor", lambda E, sh=sh, su=su, j=j, bb=bb: E.matmul(
                                ps_o[su][:],
                                vre2[sh][:, (bb * 2 + j) * HD:(bb * 2 + j + 1) * HD],
                                sdt2[su][:, j * BLK:(j + 1) * BLK],
                                start=False, stop=(j == 1)),
                                waits=[lvv, sdtv], inc=(ts if j == 1 else None))
                        o_done = cur(ts)
                        tok["qq"][su] = [o_done]
                        tok["sdt"][su] = [o_done]
                        p.emit("scalar", lambda E, su=su, h=h, bb=bb: E.copy(
                            obuf[:, h * HT + bb * BLK:h * HT + (bb + 1) * BLK],
                            ps_o[su][:]), waits=[o_done], inc=ss)
                        oev = cur(ss)
                        tok["pso"][su] = [oev]
                        for j in range(2):
                            p.emit("tensor", lambda E, sh=sh, su=su, j=j, bb=bb: E.transpose(
                                ps_tr[su][:, j * HD:(j + 1) * HD],
                                kre2[sh][:, bb * BLK + j * HD:bb * BLK + (j + 1) * HD],
                                ident[:]),
                                waits=[lvk, *(tok["pstr"][su] if j == 0 else [])], inc=ts)
                        trmm = cur(ts)
                        for j in range(2):
                            p.emit("scalar", lambda E, su=su, j=j: E.copy(
                                ktm2[su][:, j * HD:(j + 1) * HD],
                                ps_tr[su][:, j * HD:(j + 1) * HD]),
                                waits=[trmm, *(tok["ktm"][su] if j == 0 else [])], inc=ss)
                        ktmev = cur(ss)
                        tok["pstr"][su] = [ktmev]
                        for j in range(2):
                            p.emit("vector", lambda E, su=su, h=h, j=j: E.tensor_scalar(
                                ktm2[su][:, j * HD:(j + 1) * HD],
                                ktm2[su][:, j * HD:(j + 1) * HD],
                                kdcol[:, h * 2 + j:h * 2 + j + 1], None, ALU.mult),
                                waits=[ktmev] if j == 0 else [], inc=vs)
                        kkv = cur(vs)
                        for j in range(2):
                            p.emit("tensor", lambda E, sh=sh, su=su, j=j, bb=bb: E.matmul(
                                ps_w[su][:], ktm2[su][:, j * HD:(j + 1) * HD],
                                vre2[sh][:, (bb * 2 + j) * HD:(bb * 2 + j + 1) * HD],
                                start=(j == 0), stop=(j == 1)),
                                waits=[kkv, lvv, *(tok["psw"][su] if j == 0 else [])],
                                inc=(ts if j == 1 else None))
                        wmm = cur(ts)
                        tok["ktm"][su] = [wmm]
                        p.emit("vector", lambda E, h=h, kv_h=kv_h: E.tensor_scalar(
                            kv_h, kv_h, bdt[:, h:h + 1], None, ALU.mult),
                            waits=[o_done], inc=vs)
                        p.emit("vector", lambda E, su=su, kv_h=kv_h: E.tensor_tensor(
                            kv_h, kv_h, ps_w[su][:], ALU.add), waits=[wmm], inc=vs)
                        state_done[h] = (vs, vs.n)
                        tok["psw"][su] = [state_done[h]]
                        if bb == 1:
                            tok["kre"][sh] = [(ts, trmm[1] if trmm[1] > wmm[1] else wmm[1])]
                            tok["vre"][sh] = [wmm]
                            tok["qdb"][sh] = [qqv]
                            tok["dtt"][sh] = [sdtv]

                    units = [(h, bb) for h in range(NH) for bb in range(2)]
                    pend = None
                    for ui, (h, bb) in enumerate(units):
                        s1 = stage1(h, bb, ui % 2)
                        if pend is not None:
                            stage2(*pend)
                        pend = (h, bb, ui % 2, *s1)
                    stage2(*pend)
                    p.flush()

                if dbg:
                    for h in range(NH):
                        dma("gpsimd", ODBG[:, (h * 2 + hh) * HT:(h * 2 + hh + 1) * HT],
                            obuf[:, h * HT:(h + 1) * HT], gp, waits=[(ss, ss.n), (vs, vs.n)])
                with ExitStack() as sctx:
                    ps_s2 = sctx.enter_context(nc.psum_tensor(f"ps_rms{hh}", [1, 512], F32))
                    for h in range(NH):
                        sqb = sqh[h % 2]
                        p.emit("scalar", lambda E, h=h, sqb=sqb: E.square(
                            sqb[:], obuf[:, h * HT:(h + 1) * HT]),
                            waits=[(vs, vs.n), (ts, ts.n)], inc=ss)
                        p.emit("tensor", lambda E, h=h, sqb=sqb: E.matmul(
                            ps_s2[:], onescol[:], sqb[:],
                            start=(h == 0), stop=(h == NH - 1)),
                            waits=[ss], inc=ts)
                    p.emit("scalar", lambda E: E.copy(row(1, HT), ps_s2[:]), waits=[ts], inc=ss)
                    p.flush()
                ln_rows_and_bcast(HT, EPS_RMS, rms=True)
                for h in range(NH):
                    p.emit("vector", lambda E, h=h: E.tensor_tensor(
                        obuf[:, h * HT:(h + 1) * HT], obuf[:, h * HT:(h + 1) * HT],
                        bc1[:, 0:HT], ALU.mult), waits=[ss, (gp, gp.n)], inc=vs)
                    p.emit("vector", lambda E, h=h: E.tensor_scalar(
                        obuf[:, h * HT:(h + 1) * HT], obuf[:, h * HT:(h + 1) * HT],
                        nwc[:, h:h + 1], None, ALU.mult), waits=[], inc=vs)

                with ExitStack() as gctx:
                    ps_g = [gctx.enter_context(nc.psum_tensor(f"ps_g{hh}{i}", [HD, 512], F32))
                            for i in range(2)]
                    pan_free = [[(ts, ts.n)], [(ts, ts.n)]]
                    pb_free = [bar(), bar()]
                    ps_free = [(ss, ss.n), (ss, ss.n)]
                    for oc in range(NCH):
                        wp = pan[oc % 2]
                        lv = dma("sync", wp[:], WGT[oc * HD:(oc + 1) * HD, :], ld,
                                 waits=pan_free[oc % 2])
                        spl = split_tile(wp[:], pbh[oc % 2][:], pbl[oc % 2][:],
                                         [lv, *pb_free[oc % 2]])
                        pan_free[oc % 2] = [spl]
                        pb = ps_g[oc % 2]
                        mm3(pb, pbh[oc % 2], pbl[oc % 2], xyh, xyl, 0, HT,
                            [spl, ps_free[oc % 2]])
                        p.emit("scalar", lambda E, pb=pb: E.activation(
                            gtile[:], pb[:], AF.Sigmoid), waits=[ts, (vs, vs.n)], inc=ss)
                        p.emit("vector", lambda E, oc=oc: E.tensor_tensor(
                            obuf[:, oc * HT:(oc + 1) * HT], obuf[:, oc * HT:(oc + 1) * HT],
                            gtile[:], ALU.mult), waits=[ss], inc=vs)
                        ps_free[oc % 2] = (ss, ss.n)
                        pb_free[oc % 2] = [(ts, ts.n)]
                    p.flush()

                gospl = bar()
                for oc in range(NCH):
                    split_tile(obuf[:, oc * HT:(oc + 1) * HT],
                               xyh[:, oc * HT:(oc + 1) * HT],
                               xyl[:, oc * HT:(oc + 1) * HT],
                               gospl if oc == 0 else [])
                gosplv = (vs, vs.n)

                if dbg:
                    for h in range(NH):
                        dma("gpsimd", GODBG[:, (h * 2 + hh) * HT:(h * 2 + hh + 1) * HT],
                            obuf[:, h * HT:(h + 1) * HT], gp, waits=[(ss, ss.n), (vs, vs.n)])
                with ExitStack() as octx:
                    ps_y = [octx.enter_context(nc.psum_tensor(f"ps_y{hh}{i}", [HD, 512], F32))
                            for i in range(2)]
                    ps_s1 = octx.enter_context(nc.psum_tensor(f"ps_ys1{hh}", [1, 512], F32))
                    ps_s2 = octx.enter_context(nc.psum_tensor(f"ps_ys2{hh}", [1, 512], F32))
                    pan_free = [[(ts, ts.n)], [(ts, ts.n)]]
                    pb_free = [bar(), bar()]
                    ps_free = [(ss, ss.n), (ss, ss.n)]
                    for fc in range(NCH):
                        wp = pan[fc % 2]
                        lv = dma("sync", wp[:], WOT[fc * HD:(fc + 1) * HD, :], ld,
                                 waits=pan_free[fc % 2])
                        spl = split_tile(wp[:], pbh[fc % 2][:], pbl[fc % 2][:],
                                         [lv, gosplv, *pb_free[fc % 2]])
                        pan_free[fc % 2] = [spl]
                        pb = ps_y[fc % 2]
                        mm3(pb, pbh[fc % 2], pbl[fc % 2], xyh, xyl, 0, HT,
                            [spl, ps_free[fc % 2]])
                        pb_free[fc % 2] = [(ts, ts.n)]
                        p.emit("scalar", lambda E, fc=fc, pb=pb: E.copy(
                            qbuf[:, fc * HT:(fc + 1) * HT], pb[:]), waits=[ts], inc=ss)
                        sqb = sqh[fc % 2]
                        p.emit("scalar", lambda E, fc=fc, sqb=sqb: E.square(
                            sqb[:], qbuf[:, fc * HT:(fc + 1) * HT]),
                            waits=[(ts, ts.n)], inc=ss)
                        p.emit("tensor", lambda E, fc=fc: E.matmul(
                            ps_s1[:], onescol[:], qbuf[:, fc * HT:(fc + 1) * HT],
                            start=(fc == 0), stop=(fc == NCH - 1)), waits=[ss], inc=None)
                        p.emit("tensor", lambda E, fc=fc, sqb=sqb: E.matmul(
                            ps_s2[:], onescol[:], sqb[:],
                            start=(fc == 0), stop=(fc == NCH - 1)),
                            waits=[ss], inc=ts)
                        ps_free[fc % 2] = (ss, ss.n)
                        pan_free[fc % 2] = [(ts, ts.n)]
                    p.emit("scalar", lambda E: E.copy(row(0, HT), ps_s1[:]), waits=[ts], inc=ss)
                    p.emit("scalar", lambda E: E.copy(row(1, HT), ps_s2[:]), waits=[ts], inc=ss)
                    p.flush()
                if dbg:
                    for j in range(NCH):
                        dma("gpsimd", YDBG[:, (j * 2 + hh) * HT:(j * 2 + hh + 1) * HT],
                            qbuf[:, j * HT:(j + 1) * HT], gp, waits=[(ss, ss.n)])
                ln_rows_and_bcast(HT, EPS_LN)
                for j in range(NCH):
                    lv = dma("sync", xha[j % 2][:], XT3[:, j, hh * HT:(hh + 1) * HT],
                             ld, waits=xa_free2[j % 2])
                    p.emit("vector", lambda E, j=j: E.tensor_tensor(
                        qbuf[:, j * HT:(j + 1) * HT], qbuf[:, j * HT:(j + 1) * HT],
                        bc1[:, 0:HT], ALU.mult), waits=[ss, (gp, gp.n)], inc=vs)
                    p.emit("vector", lambda E, j=j: E.tensor_tensor(
                        qbuf[:, j * HT:(j + 1) * HT], qbuf[:, j * HT:(j + 1) * HT],
                        bc2[:, 0:HT], ALU.subtract), waits=[], inc=vs)
                    p.emit("vector", lambda E, j=j: E.tensor_tensor(
                        qbuf[:, j * HT:(j + 1) * HT], qbuf[:, j * HT:(j + 1) * HT],
                        xha[j % 2][:], ALU.add), waits=[lv], inc=vs)
                    dma("gpsimd", x1_scr3[:, j, hh * HT:(hh + 1) * HT],
                        qbuf[:, j * HT:(j + 1) * HT], gp, waits=[(vs, vs.n)])
                    xa_free2[j % 2] = [(vs, vs.n)]
                p.flush()

        if stage == "p2":
            p.emit("sync", lambda E: E.nop(), waits=bar(), inc=None)
            p.flush()
            return nc, outs

        OUTT3 = OUTT.rearrange("(j p) t -> p j t", p=HD)
        nic = IGRP // HD
        with ExitStack() as fctx:
            fsb = lambda name, shape, dt=F32: fctx.enter_context(
                nc.sbuf_tensor(f"f_{name}", shape, dt))
            ln2h = fsb("ln2h", [HD, NCH * TPC], BF16)
            ln2l = fsb("ln2l", [HD, NCH * TPC], BF16)
            y2 = fsb("y2", [HD, NCH * HT])
            x1t = [fsb(f"x1t{i}", [HD, TPC]) for i in range(2)]
            sqf = [fsb(f"sqf{i}", [HD, TPC]) for i in range(2)]
            w8 = [fsb(f"w8_{i}", [HD, NCH * IGRP], FP8) for i in range(2)]
            gall = fsb("gall", [HD, nic * HT])
            utmp = fsb("utmp", [HD, HT])
            fh = fsb("fh", [HD, nic * HT], BF16)
            fl = fsb("fl", [HD, nic * HT], BF16)
            lntmp = fsb("lntmp", [HD, TPC])
            x1_free = [bar(), bar()]
            with ExitStack() as sctx:
                ps_s1 = [sctx.enter_context(nc.psum_tensor(f"ps3s1{u}", [1, 512], F32))
                         for u in range(2)]
                ps_s2 = [sctx.enter_context(nc.psum_tensor(f"ps3s2{u}", [1, 512], F32))
                         for u in range(2)]
                for j in range(NCH):
                    lv = dma("sync", x1t[j % 2][:], x1_scr3[:, j, :],
                             ld, waits=[*x1_free[j % 2], (gp, gp.n)])
                    sqb = sqf[j % 2]
                    p.emit("scalar", lambda E, j=j, sqb=sqb: E.square(
                        sqb[:], x1t[j % 2][:]), waits=[lv, (ts, ts.n)], inc=ss)
                    for u in range(2):
                        p.emit("tensor", lambda E, j=j, u=u: E.matmul(
                            ps_s1[u][:], onescol[:], x1t[j % 2][:, u * 512:(u + 1) * 512],
                            start=(j == 0), stop=(j == NCH - 1)), waits=[lv, ss], inc=None)
                        p.emit("tensor", lambda E, j=j, u=u, sqb=sqb: E.matmul(
                            ps_s2[u][:], onescol[:], sqb[:, u * 512:(u + 1) * 512],
                            start=(j == 0), stop=(j == NCH - 1)),
                            waits=[ss], inc=(ts if u == 1 else None))
                    x1_free[j % 2] = [(ts, ts.n)]
                for u in range(2):
                    p.emit("scalar", lambda E, u=u: E.copy(row(0)[0:1, u * 512:(u + 1) * 512],
                                                           ps_s1[u][:]), waits=[ts], inc=ss)
                    p.emit("scalar", lambda E, u=u: E.copy(row(1)[0:1, u * 512:(u + 1) * 512],
                                                           ps_s2[u][:]), waits=[ts], inc=ss)
                p.flush()
            ln_rows_and_bcast(TPC, EPS_LN)
            for j in range(NCH):
                lv = dma("sync", x1t[j % 2][:], x1_scr3[:, j, :], ld, waits=x1_free[j % 2])
                p.emit("vector", lambda E, j=j: E.tensor_tensor(
                    lntmp[:], x1t[j % 2][:], bc1[:], ALU.mult),
                    waits=[lv, ss, (ts, ts.n)], inc=vs)
                p.emit("vector", lambda E, j=j: E.tensor_tensor(
                    lntmp[:], lntmp[:], bc2[:], ALU.subtract), waits=[], inc=vs)
                p.emit("vector", lambda E, j=j: E.tensor_copy(
                    ln2h[:, j * TPC:(j + 1) * TPC], lntmp[:]), waits=[], inc=vs)
                p.emit("vector", lambda E, j=j: E.tensor_tensor(
                    ln2l[:, j * TPC:(j + 1) * TPC], lntmp[:],
                    ln2h[:, j * TPC:(j + 1) * TPC], ALU.subtract), waits=[], inc=vs)
                x1_free[j % 2] = [(vs, vs.n)]

            with ExitStack() as wctx:
                ps_gu = [wctx.enter_context(
                    nc.psum_tensor(f"ps_gu{i}", [HD, 512], F32)) for i in range(2)]
                ps_d = [wctx.enter_context(
                    nc.psum_tensor(f"ps_d{i}", [HD, 512], F32)) for i in range(2)]
                slab_free = [[(ts, ts.n)], [(ts, ts.n)]]
                slot = 0
                lnw = (vs, vs.n)
                gu_free = [(ss, ss.n), (ss, ss.n)]
                d_free = [[(ss, ss.n)], [(ss, ss.n)]]
                for hh in range(2):
                    for ig in range(NIG):
                        sg = slot; slot = (slot + 1) % 2
                        lvg = dma("gpsimd", w8[sg][:], GW[ig * HD:(ig + 1) * HD, :], gp,
                                  waits=slab_free[sg])
                        for ic in range(nic):
                            pb = ps_gu[ic % 2]
                            for pi, lnb in enumerate((ln2h, ln2l)):
                                for j in range(NCH):
                                    p.emit("tensor", lambda E, ig=ig, ic=ic, j=j, pb=pb,
                                           lnb=lnb, sg=sg, pi=pi, hh=hh:
                                           E.matmul(pb[:],
                                                    w8[sg][:, j * IGRP + ic * HD:j * IGRP + (ic + 1) * HD],
                                                    lnb[:, j * TPC + hh * HT:j * TPC + (hh + 1) * HT],
                                                    start=(pi == 0 and j == 0),
                                                    stop=(pi == 1 and j == NCH - 1)),
                                           waits=[lvg, lnw, gu_free[ic % 2]],
                                           inc=(ts if (pi == 1 and j == NCH - 1) else None))
                            p.emit("scalar", lambda E, ig=ig, ic=ic, pb=pb: E.activation(
                                gall[:, ic * HT:(ic + 1) * HT], pb[:],
                                AF.Silu, scale=gsc[:, ig * nic + ic:ig * nic + ic + 1]),
                                waits=[ts], inc=ss)
                            gu_free[ic % 2] = (ss, ss.n)
                        slab_free[sg] = [(ts, ts.n)]
                        su = slot; slot = (slot + 1) % 2
                        lvu = dma("gpsimd", w8[su][:], UW[ig * HD:(ig + 1) * HD, :], gp,
                                  waits=slab_free[su])
                        for ic in range(nic):
                            pb = ps_gu[ic % 2]
                            for pi, lnb in enumerate((ln2h, ln2l)):
                                for j in range(NCH):
                                    p.emit("tensor", lambda E, ig=ig, ic=ic, j=j, pb=pb,
                                           lnb=lnb, su=su, pi=pi, hh=hh:
                                           E.matmul(pb[:],
                                                    w8[su][:, j * IGRP + ic * HD:j * IGRP + (ic + 1) * HD],
                                                    lnb[:, j * TPC + hh * HT:j * TPC + (hh + 1) * HT],
                                                    start=(pi == 0 and j == 0),
                                                    stop=(pi == 1 and j == NCH - 1)),
                                           waits=[lvu, gu_free[ic % 2]],
                                           inc=(ts if (pi == 1 and j == NCH - 1) else None))
                            p.emit("scalar", lambda E, ig=ig, ic=ic, pb=pb: E.activation(
                                utmp[:], pb[:], AF.Copy,
                                scale=usc[:, ig * nic + ic:ig * nic + ic + 1]),
                                waits=[ts, (vs, vs.n)], inc=ss)
                            gu_free[ic % 2] = (ss, ss.n)
                            p.emit("vector", lambda E, ic=ic: E.tensor_tensor(
                                gall[:, ic * HT:(ic + 1) * HT], gall[:, ic * HT:(ic + 1) * HT],
                                utmp[:], ALU.mult), waits=[ss], inc=vs)
                            p.emit("vector", lambda E, ic=ic: E.tensor_copy(
                                fh[:, ic * HT:(ic + 1) * HT], gall[:, ic * HT:(ic + 1) * HT]),
                                waits=[], inc=vs)
                            p.emit("vector", lambda E, ic=ic: E.tensor_tensor(
                                fl[:, ic * HT:(ic + 1) * HT], gall[:, ic * HT:(ic + 1) * HT],
                                fh[:, ic * HT:(ic + 1) * HT], ALU.subtract), waits=[], inc=vs)
                        slab_free[su] = [(ts, ts.n)]
                        sd = slot; slot = (slot + 1) % 2
                        lvd = dma("gpsimd", w8[sd][:], DW[ig * HD:(ig + 1) * HD, :], gp,
                                  waits=slab_free[sd])
                        for fc in range(NCH):
                            pb = ps_d[fc % 2]
                            for pi, fb in enumerate((fh, fl)):
                                for jc in range(nic):
                                    p.emit("tensor", lambda E, ig=ig, fc=fc, jc=jc, pb=pb,
                                           fb=fb, sd=sd, pi=pi:
                                           E.matmul(pb[:],
                                                    w8[sd][:, jc * DM + fc * HD:jc * DM + (fc + 1) * HD],
                                                    fb[:, jc * HT:(jc + 1) * HT],
                                                    start=(pi == 0 and jc == 0),
                                                    stop=(pi == 1 and jc == nic - 1)),
                                           waits=[lvd, (vs, vs.n), *d_free[fc % 2]],
                                           inc=(ts if (pi == 1 and jc == nic - 1) else None))
                            if ig == 0:
                                p.emit("scalar", lambda E, fc=fc, pb=pb: E.copy(
                                    y2[:, fc * HT:(fc + 1) * HT], pb[:]),
                                    waits=[ts, (st, st.n)], inc=ss)
                            else:
                                p.emit("vector", lambda E, fc=fc, pb=pb: E.tensor_tensor(
                                    y2[:, fc * HT:(fc + 1) * HT], y2[:, fc * HT:(fc + 1) * HT],
                                    pb[:], ALU.add), waits=[ts], inc=vs)
                            d_free[fc % 2] = [(ss, ss.n), (vs, vs.n)]
                        slab_free[sd] = [(ts, ts.n)]
                    for fc in range(NCH):
                        p.emit("vector", lambda E, fc=fc: E.tensor_scalar(
                            y2[:, fc * HT:(fc + 1) * HT], y2[:, fc * HT:(fc + 1) * HT],
                            dsc[:, fc:fc + 1], None, ALU.mult), waits=[(ss, ss.n)], inc=vs)
                    with ExitStack() as sctx:
                        ps_s1 = sctx.enter_context(nc.psum_tensor(f"ps4s1{hh}", [1, 512], F32))
                        ps_s2 = sctx.enter_context(nc.psum_tensor(f"ps4s2{hh}", [1, 512], F32))
                        for j in range(NCH):
                            sqb = sqf[j % 2]
                            p.emit("scalar", lambda E, j=j, sqb=sqb: E.square(
                                sqb[:, 0:HT], y2[:, j * HT:(j + 1) * HT]),
                                waits=[(vs, vs.n), (ts, ts.n)], inc=ss)
                            p.emit("tensor", lambda E, j=j: E.matmul(
                                ps_s1[:], onescol[:], y2[:, j * HT:(j + 1) * HT],
                                start=(j == 0), stop=(j == NCH - 1)),
                                waits=[(vs, vs.n), ss], inc=None)
                            p.emit("tensor", lambda E, j=j, sqb=sqb: E.matmul(
                                ps_s2[:], onescol[:], sqb[:, 0:HT],
                                start=(j == 0), stop=(j == NCH - 1)), waits=[ss], inc=ts)
                        p.emit("scalar", lambda E: E.copy(row(0, HT), ps_s1[:]),
                               waits=[ts], inc=ss)
                        p.emit("scalar", lambda E: E.copy(row(1, HT), ps_s2[:]),
                               waits=[ts], inc=ss)
                        p.flush()
                    ln_rows_and_bcast(HT, EPS_LN)
                    for j in range(NCH):
                        lv = dma("sync", x1t[j % 2][:, 0:HT],
                                 x1_scr3[:, j, hh * HT:(hh + 1) * HT], ld,
                                 waits=x1_free[j % 2])
                        p.emit("vector", lambda E, j=j: E.tensor_tensor(
                            y2[:, j * HT:(j + 1) * HT], y2[:, j * HT:(j + 1) * HT],
                            bc1[:, 0:HT], ALU.mult), waits=[ss], inc=vs)
                        p.emit("vector", lambda E, j=j: E.tensor_tensor(
                            y2[:, j * HT:(j + 1) * HT], y2[:, j * HT:(j + 1) * HT],
                            bc2[:, 0:HT], ALU.subtract), waits=[], inc=vs)
                        p.emit("vector", lambda E, j=j: E.tensor_tensor(
                            y2[:, j * HT:(j + 1) * HT], y2[:, j * HT:(j + 1) * HT],
                            x1t[j % 2][:, 0:HT], ALU.add), waits=[lv], inc=vs)
                        dma("sync", OUTT3[:, j, hh * HT:(hh + 1) * HT],
                            y2[:, j * HT:(j + 1) * HT], st, waits=[(vs, vs.n)])
                        x1_free[j % 2] = [(vs, vs.n)]
                    p.flush()

        p.emit("sync", lambda E: E.nop(), waits=[(st, st.n)], inc=None)
        p.flush()
        return nc, outs


def stage_inputs(inputs):
    x = np.ascontiguousarray(np.asarray(inputs["L0_attn_hidden_states"], np.float32))

    def pan(M, csz):
        K, N = M.shape
        t = M.reshape(K // HD, HD, N // csz, csz).transpose(2, 1, 0, 3)
        return np.ascontiguousarray(t.reshape(N // csz * HD, K // HD * csz))

    wqkv = np.asarray(inputs["L0_attn_w_qkv"], np.float32).reshape(NH, 3, HD, DM)
    QTW = pan(wqkv[:, 0].reshape(DM, DM).T, HD)
    KTW = pan(wqkv[:, 1].reshape(DM, DM).T, HD)
    VTW = pan(wqkv[:, 2].reshape(DM, DM).T, HD)
    WGT = pan(np.asarray(inputs["L0_attn_w_gate"], np.float32).T, HD)
    WOT = pan(np.asarray(inputs["L0_attn_w_out"], np.float32).T, HD)
    GW = pan(np.asarray(inputs["L0_ffn_gate_w"], np.float32), IGRP)
    UW = pan(np.asarray(inputs["L0_ffn_up_w"], np.float32), IGRP)
    dw_ = np.asarray(inputs["L0_ffn_down_w"], np.float32)
    DW = np.ascontiguousarray(dw_.reshape(NIG, IGRP // HD, HD, DM).transpose(0, 2, 1, 3)
                              .reshape(NIG * HD, (IGRP // HD) * DM))
    SLOPES = np.asarray(inputs["L0_attn_slope_rate"], np.float32).reshape(1, NH)
    NWC = np.ascontiguousarray(np.asarray(inputs["L0_attn_norm_w"], np.float32)
                               .reshape(NCH, HD).T)
    GSC = np.ascontiguousarray(np.asarray(inputs["L0_ffn_gate_scale"], np.float32)
                               .reshape(FI // HD, HD).T)
    USC = np.ascontiguousarray(np.asarray(inputs["L0_ffn_up_scale"], np.float32)
                               .reshape(FI // HD, HD).T)
    DSC = np.ascontiguousarray(np.asarray(inputs["L0_ffn_down_scale"], np.float32)
                               .reshape(NCH, HD).T)
    IDENT = np.eye(HD, dtype=np.float32)
    m = np.arange(BLK, dtype=np.float32)
    IDXT = np.where(m[None, :] >= m[:, None], m[None, :] - m[:, None],
                    np.float32(BIG)).astype(np.float32)
    ARRQ = np.arange(1, BLK + 1, dtype=np.float32).reshape(1, BLK)
    ARRK = (BLK - np.arange(1, BLK + 1, dtype=np.float32)).reshape(1, BLK)
    ONESROW = np.ones((1, HD), np.float32)
    ONESCOL = np.ones((HD, 1), np.float32)

    shared = dict(qtw=QTW, ktw=KTW, vtw=VTW, wgt=WGT, wot=WOT, gw=GW, uw=UW, dw=DW,
                  slopes=SLOPES, nwc=NWC, gsc=GSC, usc=USC, dsc=DSC, ident=IDENT,
                  idxt=IDXT, arrq=ARRQ, arrk=ARRK, onesrow=ONESROW, onescol=ONESCOL)
    in_maps = []
    for c in range(NCORES):
        b, pp = c // 4, c % 4
        xt = np.ascontiguousarray(x[b, pp * TPC:(pp + 1) * TPC, :].T)
        expo = np.full((1, NCORES), BIG, np.float32)
        for cp in range(NCORES):
            if cp // 4 == b and cp % 4 < pp:
                expo[0, cp] = np.float32(pp - 1 - (cp % 4))
        in_maps.append(dict(shared, xt=xt, expo=expo))
    return in_maps


def run(inputs, cfg):
    nc, outs = build(cfg)
    in_maps = stage_inputs(inputs)
    res = run_bass_kernel_spmd(nc, in_maps, core_ids=list(range(NCORES)))
    return res, outs


def kernel(**inputs):
    res, outs = run(inputs, {"stage": "full", "debug": False})
    out = np.empty((2, 4096, DM), np.float32)
    for c in range(NCORES):
        b, pp = c // 4, c % 4
        out[b, pp * TPC:(pp + 1) * TPC, :] = np.asarray(res.results[c]["outt"]).T
    return out
